# version 20 (final)
# baseline (speedup 1.0000x reference)
import numpy as np
import ml_dtypes
import concourse.bass as bass
import concourse.mybir as mybir
from concourse.bass_utils import run_bass_kernel_spmd

F32 = mybir.dt.float32
BF16 = mybir.dt.bfloat16
AF = mybir.ActivationFunctionType
ALU = mybir.AluOpType
AX = mybir.AxisListType

B, S, D = 2, 8192, 2048
NH, HD = 16, 128
DE = 8192
NCORES = 8
TOK = 2048
HALO = 128
TT = TOK + HALO
NST = 4
STOWN = 512
STT = STOWN + HALO
NQT = 4
NTT = 5
HT = 1024
EB = 4
KCH = 16
ECH = DE // 128
NWCH = 128
SM_SCALE = float(1.0 / np.sqrt(HD))
EPS_LN = 1e-5
EPS_Q = 1e-8
THETA_F = 10000.0

SB_BASE = 16512
SB_TOP = SB_BASE + 212_863


def _bytes_pp(shape, dtype):
    n = 1
    for v in shape[1:]:
        n *= v
    return n * mybir.dt.size(dtype)


class SB:
    def __init__(self, nc):
        self.nc = nc
        self.off = SB_BASE
        self.peak = SB_BASE

    def alloc(self, name, shape, dtype, align=64):
        self.off = (self.off + align - 1) // align * align
        t = self.nc.alloc_sbuf_tensor_at(name, list(shape), dtype, offset=self.off)
        self.off += _bytes_pp(shape, dtype)
        self.peak = max(self.peak, self.off)
        assert self.off <= SB_TOP, f"SBUF overflow at {name}: {self.off}"
        return t.ap()

    def mark(self):
        return self.off

    def reset(self, mark):
        self.off = mark


class Sched:
    ENGS = ("sync", "tensor", "vector", "scalar", "gpsimd")
    DMAQ = ("sync", "scalar", "gpsimd")

    def __init__(self, nc):
        self.nc = nc
        self.q = {e: [] for e in self.ENGS}
        self.sem = {e: nc.alloc_semaphore(f"es_{e}") for e in self.ENGS}
        self.cnt = {e: 0 for e in self.ENGS}
        self.dstream = {}
        self.dq_of_sem = {}
        self.hw = {e: {} for e in self.ENGS}
        self.wr = {}
        self.rd = {}
        self.nwaits = 0
        self.log = []

    def op(self, eng, fn, reads=(), writes=(), dma=False, acc=False,
           stream=None):
        waits = {}

        def add(tok):
            sem, val = tok
            if eng == "tensor" and sem is self.sem["tensor"]:
                return
            k = id(sem)
            skey = self.dq_of_sem.get(k)
            if skey is not None:
                val = max(val, self.dstream[skey][1])
            if self.hw[eng].get(k, 0) >= val:
                return
            cur = waits.get(k)
            if cur is None or cur[1] < val:
                waits[k] = (sem, val)

        for r in reads:
            for t in self.wr.get(r, ()):
                add(t)
        for w in writes:
            for t in self.wr.get(w, ()):
                add(t)
            for t in self.rd.get(w, ()):
                add(t)
        wlist = list(waits.values())
        self.nwaits += len(wlist)
        for k, (sem, val) in waits.items():
            self.hw[eng][k] = val
        if dma:
            skey = (eng, stream or eng)
            ent = self.dstream.get(skey)
            if ent is None:
                ent = [self.nc.alloc_semaphore(f"ds_{skey[0]}_{skey[1]}"), 0]
                self.dstream[skey] = ent
                self.dq_of_sem[id(ent[0])] = skey
            ent[1] += 16
            tok = (ent[0], ent[1])
            inc = 16
        else:
            self.cnt[eng] += 1
            tok = (self.sem[eng], self.cnt[eng])
            inc = 1

        def emit(e, wlist=wlist, fn=fn, tok=tok, inc=inc):
            for sem, val in wlist:
                e.wait_ge(sem, val)
            fn(e).then_inc(tok[0], inc)

        self.q[eng].append(emit)
        self.log.append((eng, list(wlist), tok, list(reads), list(writes), dma))
        for w in writes:
            if acc:
                self.wr.setdefault(w, []).append(tok)
            else:
                self.wr[w] = [tok]
                self.rd[w] = []
        for r in reads:
            self.rd.setdefault(r, []).append(tok)
        return tok

    def alias(self, new_regions, old_regions):
        toks = []
        for r in old_regions:
            toks += list(self.wr.get(r, ()))
            toks += list(self.rd.get(r, ()))
        best = {}
        for sem, val in toks:
            k = id(sem)
            if k not in best or best[k][1] < val:
                best[k] = (sem, val)
        toks = list(best.values())
        for w in new_regions:
            self.wr[w] = list(toks)
            self.rd[w] = []

    def finalize(self, block):
        for e in self.ENGS:
            ops = self.q[e]

            def run(eng, ops=ops):
                for f in ops:
                    f(eng)

            getattr(block, e)(run)


def build_program(debug=False):
    nc = bass.Bass("TRN2", target_bir_lowering=False)

    d_x = nc.dram_tensor("x", [TT, D], F32, kind="ExternalInput").ap()
    d_wqkv = nc.dram_tensor("wqkv", [D, 3 * D], F32, kind="ExternalInput").ap()
    d_wout = nc.dram_tensor("wout", [D, D], F32, kind="ExternalInput").ap()
    d_bout = nc.dram_tensor("bout", [128, 16], F32, kind="ExternalInput").ap()
    d_wu = nc.dram_tensor("wu", [DE, D], F32, kind="ExternalInput").ap()
    d_wg = nc.dram_tensor("wg", [DE, D], F32, kind="ExternalInput").ap()
    d_wd = nc.dram_tensor("wd", [D, DE], F32, kind="ExternalInput").ap()
    d_cos = nc.dram_tensor("cost", [128, TT], BF16, kind="ExternalInput").ap()
    d_sin = nc.dram_tensor("sint", [128, TT], BF16, kind="ExternalInput").ap()
    d_mtf = nc.dram_tensor("maskf", [128, 256], BF16, kind="ExternalInput").ap()
    d_mtr = nc.dram_tensor("maskr", [128, 256], BF16, kind="ExternalInput").ap()
    d_rt = nc.dram_tensor("rt", [128, 128], BF16, kind="ExternalInput").ap()
    d_idn = nc.dram_tensor("idn", [128, 128], BF16, kind="ExternalInput").ap()
    d_onesr = nc.dram_tensor("onesr", [1, 128], F32, kind="ExternalInput").ap()
    d_onesc = nc.dram_tensor("onesc", [128, 1], F32, kind="ExternalInput").ap()
    d_onesb = nc.dram_tensor("onesb", [128, 1], BF16, kind="ExternalInput").ap()
    d_consts = nc.dram_tensor("consts", [128, 8], F32, kind="ExternalInput").ap()

    d_out = nc.dram_tensor("out", [TOK, D], F32, kind="ExternalOutput").ap()

    d_xbf = nc.dram_tensor("xbf", [TT, D], BF16, kind="Internal").ap()
    d_wqkvbf = nc.dram_tensor("wqkvbf", [D, 3 * D], BF16, kind="Internal").ap()
    d_woutbf = nc.dram_tensor("woutbf", [D, D], BF16, kind="Internal").ap()
    d_wut = nc.dram_tensor("wut", [DE, D], BF16, kind="Internal").ap()
    d_wgt = nc.dram_tensor("wgt", [DE, D], BF16, kind="Internal").ap()
    d_wdt = nc.dram_tensor("wdt", [D, DE], BF16, kind="Internal").ap()
    d_z1 = nc.dram_tensor("z1s", [D, TOK], BF16, kind="Internal").ap()

    taps = {}
    if debug:
        def tap(name, shape, dtype=BF16):
            taps[name] = nc.dram_tensor(name, list(shape), dtype,
                                        kind="ExternalOutput").ap()
        tap("t_xln1", [128, KCH * STT])
        tap("t_xbf", [640, D])
        tap("t_xln1b", [128, KCH * STT])
        tap("t_kb", [128, STT])
        tap("t_qb", [128, STOWN])
        tap("t_vb", [128, NTT * 256])
        tap("t_q", [128, STOWN])
        tap("t_k", [128, STT])
        tap("t_at", [128, 256])
        tap("t_attn", [128, KCH * TOK])
        tap("t_z1", [D, TOK])
        tap("t_scl", [1, 16], F32)
        tap("t_wut", [256, D])
        tap("t_wdt", [256, DE])
        tap("t_xln2", [128, KCH * HT])
        tap("t_acc", [128, 2 * HT], F32)

    sb = SB(nc)
    k_consts = sb.alloc("k_consts", [128, 8], F32)
    k_cos = sb.alloc("k_cos", [128, TT], BF16)
    k_sin = sb.alloc("k_sin", [128, TT], BF16)
    k_mtf = sb.alloc("k_mtf", [128, 256], BF16)
    k_mtr = sb.alloc("k_mtr", [128, 256], BF16)
    k_rt = sb.alloc("k_rt", [128, 128], BF16)
    k_idn = sb.alloc("k_idn", [128, 128], BF16)
    k_onesr = sb.alloc("k_onesr", [1, 128], F32)
    k_onesc = sb.alloc("k_onesc", [128, 1], F32)
    k_onesb = sb.alloc("k_onesb", [128, 1], BF16)
    k_bout = sb.alloc("k_bout", [128, 16], F32)
    k_scl = sb.alloc("k_scl", [128, 8], F32)
    k_accu = sb.alloc("k_accu", [128, 384], F32)
    k_part = sb.alloc("k_part", [128, 8], F32)
    r_a = sb.alloc("r_a", [1, TT], F32)
    r_b = sb.alloc("r_b", [1, TT], F32)
    r_rr = sb.alloc("r_rr", [1, 512], F32)
    r_s = sb.alloc("r_s", [1, 16], F32)
    pers_end = sb.mark()

    t_wch = [sb.alloc(f"t_wch{i}", [128, 1024], F32) for i in range(3)]
    t_wcht = [sb.alloc(f"t_wcht{i}", [128, 1024], F32) for i in range(3)]
    t_s1 = sb.alloc("t_s1", [128, 1024], BF16)

    a_attn = sb.alloc("a_attn", [128, KCH * TOK], BF16)
    thp_mark = sb.mark()
    t_tn = [sb.alloc(f"t_tn{i}", [128, 1024], BF16) for i in range(2)]
    t_thp = sb.alloc("t_thp", [128, 1024], F32)
    t_thn = sb.alloc("t_thn", [128, 1024], F32)

    att_mark = sb.mark()
    a_tmp1 = sb.alloc("a_tmp1", [128, STT], F32)
    a_tmp2 = sb.alloc("a_tmp2", [128, STT], F32)
    a_xln1 = sb.alloc("a_xln1", [128, KCH * STT], BF16)
    a_q = sb.alloc("a_q", [128, STOWN], BF16)
    a_qraw = sb.alloc("a_qraw", [128, STOWN], BF16)
    a_k = sb.alloc("a_k", [128, STT], BF16)
    a_kraw = sb.alloc("a_kraw", [128, STT], BF16)
    a_v = [sb.alloc(f"a_v{i}", [128, NTT * 256], BF16) for i in range(2)]
    a_wqk = [sb.alloc(f"a_wqk{i}", [128, KCH * 256], BF16) for i in range(2)]
    a_wv = [sb.alloc(f"a_wv{i}", [128, KCH * 256], BF16) for i in range(2)]
    a_atr = [sb.alloc(f"a_atr{i}", [128, 256], BF16) for i in range(4)]
    a_at = [sb.alloc(f"a_at{i}", [128, 256], BF16) for i in range(4)]
    a_rrb = [sb.alloc(f"a_rrb{i}", [128, 128], F32) for i in range(4)]
    att_end = sb.mark()

    sb.reset(thp_mark)
    c_ybf = sb.alloc("c_ybf", [128, KCH * 512], BF16)
    c_wout = sb.alloc("c_wout", [128, KCH * 2048], BF16)
    c_sq = sb.alloc("c_sq", [128, 512], BF16)
    c_t = sb.alloc("c_t", [128, 512], F32)
    c_z1 = [sb.alloc(f"c_z1{i}", [128, 512], BF16) for i in range(2)]
    sb.reset(att_mark)

    sb.reset(pers_end)
    f_xln2 = sb.alloc("f_xln2", [128, KCH * HT], BF16)
    ffn_w_mark = sb.mark()
    f_acc = sb.alloc("f_acc", [128, KCH * HT], F32)
    ffn_hid_mark = sb.mark()
    f_hid = sb.alloc("f_hid", [128, EB * HT], BF16)
    f_wu = sb.alloc("f_wu", [128, KCH * EB * 128], BF16)
    f_wg = sb.alloc("f_wg", [128, KCH * EB * 128], BF16)
    f_wd = sb.alloc("f_wd", [128, EB * D], BF16)
    f_c1 = [sb.alloc(f"f_c1{i}", [128, HT], BF16) for i in range(2)]
    f_c2 = [sb.alloc(f"f_c2{i}", [128, HT], BF16) for i in range(2)]
    f_t = sb.alloc("f_t", [128, HT], F32)
    ffn_end = sb.mark()

    sb.reset(pers_end)
    e_zsum = sb.alloc("e_zsum", [128, KCH * HT], BF16)
    sb.reset(ffn_hid_mark)
    e_zsT = [sb.alloc(f"e_zsT{i}", [128, D], F32) for i in range(2)]
    e_x = [sb.alloc(f"e_x{i}", [128, D], F32) for i in range(2)]
    e_out = [sb.alloc(f"e_out{i}", [128, D], F32) for i in range(2)]
    sb.reset(ffn_end)

    pp = [nc.alloc_psum_tensor(f"pp{i}", [128, 1024], F32).ap() for i in range(4)]

    def PK(i, half):
        return f"P{i}{half}"

    s = Sched(nc)

    C_EPS1 = k_consts[0:1, 0:1]
    C_SMS = k_consts[:, 2:3]

    def ld(dst, src, key):
        s.op("sync", lambda e, d=dst, sr=src: e.dma_start(out=d, in_=sr),
             writes=[key], dma=True, stream="const")

    ld(k_consts, d_consts, "k_consts")
    ld(k_cos, d_cos, "k_cos")
    ld(k_sin, d_sin, "k_sin")
    ld(k_mtf, d_mtf, "k_mtf")
    ld(k_mtr, d_mtr, "k_mtr")
    ld(k_rt, d_rt, "k_rt")
    ld(k_idn, d_idn, "k_idn")
    ld(k_onesr, d_onesr, "k_onesr")
    ld(k_onesc, d_onesc, "k_onesc")
    ld(k_onesb, d_onesb, "k_onesb")
    ld(k_bout, d_bout, "k_bout")

    xr = [(0, 640), (640, 1152), (1152, 1664), (1664, 2176)]
    for i, (q0, q1) in enumerate(xr):
        s.op("gpsimd", lambda e, q0=q0, q1=q1: e.dma_start(
            out=d_xbf[q0:q1, :], in_=d_x[q0:q1, :]), writes=[f"xbf{i}"],
            dma=True, stream="cast")
    s.op("gpsimd", lambda e: e.dma_start(out=d_wqkvbf, in_=d_wqkv),
         writes=["wqkvbf"], dma=True, stream="cast")
    s.op("gpsimd", lambda e: e.dma_start(out=d_woutbf, in_=d_wout),
         writes=["woutbf"], dma=True, stream="cast")

    def wchunk(dsrc, i):
        if dsrc is d_wd:
            rr, cc = i // 8, i % 8
        else:
            rr, cc = i // 2, i % 2
        return dsrc[rr * 128:(rr + 1) * 128, cc * 1024:(cc + 1) * 1024]

    def wchunk_dst(ddst, i, is_wd):
        if is_wd:
            rr, cc = i // 8, i % 8
        else:
            rr, cc = i // 2, i % 2
        return ddst[rr * 128:(rr + 1) * 128, cc * 1024:(cc + 1) * 1024]

    def bcast_scalar_to_col(val_ap, val_key, dst_col, col_key):
        s.op("tensor", lambda e: e.matmul(out=pp[2][:, 1016:1017], lhsT=k_onesr,
                                          rhs=val_ap, start=True, stop=True),
             reads=[val_key, "k_onesr"], writes=[PK(2, "b")])
        s.op("scalar", lambda e: e.copy(out=dst_col, in_=pp[2][:, 1016:1017]),
             reads=[PK(2, "b")], writes=[col_key])

    def scale_stream(mi, mn, dsrc):
        def dma_in(i, dsrc=dsrc):
            buf, bk = t_wch[i % 3], f"twch{i % 3}"
            s.op("gpsimd", lambda e, i=i, dsrc=dsrc, buf=buf: e.dma_start(
                out=buf, in_=wchunk(dsrc, i)), writes=[bk], dma=True,
                stream=f"scalein{i % 3}")

        for i in range(3):
            dma_in(i)
        for i in range(NWCH):
            buf, bk = t_wch[i % 3], f"twch{i % 3}"
            s.op("scalar", lambda e, i=i, mi=mi, buf=buf: e.activation(
                out=t_s1, in_=buf, func=AF.Abs,
                accum_out=k_accu[:, mi * 128 + i:mi * 128 + i + 1]),
                reads=[bk], writes=["t_s1", "k_accu"], acc=(i > 0))
            if i + 3 < NWCH:
                dma_in(i + 3)
            yield
        s.op("scalar", lambda e, mi=mi: e.copy(
            out=r_s[0:1, 12:13], in_=k_accu[0:1, mi * 128:mi * 128 + 1]),
            reads=["k_accu"], writes=["accfence"])
        s.op("vector", lambda e, mi=mi: e.tensor_reduce(
            out=k_part[:, mi:mi + 1], in_=k_accu[:, mi * 128:(mi + 1) * 128],
            axis=AX.X, op=ALU.add),
            reads=["k_accu", "accfence"], writes=[f"part_{mn}"])
        pcol, prk = k_part[:, mi:mi + 1], f"part_{mn}"
        s.op("tensor", lambda e, pcol=pcol: e.matmul(
            out=pp[2][0:1, 1018:1019], lhsT=pcol, rhs=k_onesc,
            start=True, stop=True),
            reads=[prk, "k_onesc"], writes=[PK(2, "b")])
        s.op("vector", lambda e, mi=mi: e.tensor_scalar(
            out=r_s[0:1, mi:mi + 1], in0=pp[2][0:1, 1018:1019],
            scalar1=1.0 / (DE * D), scalar2=EPS_Q,
            op0=ALU.mult, op1=ALU.add),
            reads=[PK(2, "b")], writes=["r_s"], acc=True)
        s.op("vector", lambda e, mi=mi: e.tensor_scalar(
            out=r_s[0:1, mi + 4:mi + 5], in0=r_s[0:1, mi:mi + 1],
            scalar1=-0.5, scalar2=None, op0=ALU.mult),
            reads=["r_s"], writes=["r_s"], acc=True)
        s.op("vector", lambda e, mi=mi: e.tensor_scalar(
            out=r_s[0:1, mi + 8:mi + 9], in0=r_s[0:1, mi:mi + 1],
            scalar1=0.5, scalar2=None, op0=ALU.mult),
            reads=["r_s"], writes=["r_s"], acc=True)
        if mn == "u":
            bcast_scalar_to_col(r_s[0:1, 0:1], "r_s",
                                k_scl[:, 3:4], "sclp_u")
        yield

    def tern_stream(mi, mn, dsrc, ddst):
        for sgn, slot, tile, tk in ((1, mi + 8, t_thp, f"thp_{mn}"),
                                    (-1, mi + 4, t_thn, f"thn_{mn}")):
            s.op("vector", lambda e: e.memset(r_a[0:1, 0:1024], 1.0),
                 writes=["r_a"])
            for j0 in (0, 512):
                s.op("tensor", lambda e, slot=slot, j0=j0: e.matmul(
                    out=pp[2][0:1, j0:j0 + 512], lhsT=r_s[0:1, slot:slot + 1],
                    rhs=r_a[0:1, j0:j0 + 512], start=True, stop=True),
                    reads=["r_s", "r_a"],
                    writes=[PK(2, "a" if j0 == 0 else "b")])
            s.op("vector", lambda e: e.tensor_copy(
                out=r_a[0:1, 0:1024], in_=pp[2][0:1, 0:1024]),
                reads=[PK(2, "a"), PK(2, "b")], writes=["r_a"])
            s.op("tensor", lambda e: e.matmul(
                out=pp[3][:, 0:512], lhsT=k_onesr, rhs=r_a[0:1, 0:512],
                start=True, stop=True),
                reads=["r_a", "k_onesr"], writes=[PK(3, "a")])
            s.op("tensor", lambda e: e.matmul(
                out=pp[3][:, 512:1024], lhsT=k_onesr, rhs=r_a[0:1, 512:1024],
                start=True, stop=True),
                reads=["r_a", "k_onesr"], writes=[PK(3, "b")])
            s.op("vector", lambda e, tile=tile: e.tensor_copy(
                out=tile, in_=pp[3][:, 0:1024]),
                reads=[PK(3, "a"), PK(3, "b")], writes=[tk])

        def dma_in(i, dsrc=dsrc):
            buf, bk = t_wcht[i % 3], f"twcht{i % 3}"
            s.op("scalar", lambda e, i=i, dsrc=dsrc, buf=buf: e.dma_start(
                out=buf, in_=wchunk(dsrc, i)), writes=[bk], dma=True,
                stream=f"ternin{i % 3}")

        for i in range(3):
            dma_in(i)
        for i in range(NWCH):
            buf, bk = t_wcht[i % 3], f"twcht{i % 3}"
            tn, tnk = t_tn[i % 2], f"ttn{i % 2}"
            s.op("vector", lambda e, buf=buf: e.tensor_tensor(
                out=t_s1, in0=buf, in1=t_thp, op=ALU.is_gt),
                reads=[bk, f"thp_{mn}"], writes=["t_s1"])
            s.op("vector", lambda e, buf=buf, tn=tn: e.tensor_tensor(
                out=tn, in0=buf, in1=t_thn, op=ALU.is_lt),
                reads=[bk, f"thn_{mn}"], writes=[tnk])
            s.op("gpsimd", lambda e, tn=tn: e.tensor_tensor(
                out=tn, in0=t_s1, in1=tn, op=ALU.subtract),
                reads=["t_s1", tnk], writes=[tnk])
            s.op("gpsimd", lambda e, i=i, ddst=ddst, tn=tn, mn=mn: e.dma_start(
                out=wchunk_dst(ddst, i, mn == "d"), in_=tn),
                reads=[tnk], writes=[f"tern_{mn}"], dma=True, acc=True,
                stream="ternout")
            if i + 3 < NWCH:
                dma_in(i + 3)
            yield

    def weight_prep():
        import itertools
        for _ in scale_stream(0, "u", d_wu):
            yield
        for _ in itertools.zip_longest(scale_stream(1, "g", d_wg),
                                       tern_stream(0, "u", d_wu, d_wut)):
            yield
        for _ in itertools.zip_longest(scale_stream(2, "d", d_wd),
                                       tern_stream(1, "g", d_wg, d_wgt)):
            yield
        for _ in tern_stream(2, "d", d_wd, d_wdt):
            yield

    prep = weight_prep()
    prep_done = [False]

    def pump(n):
        if prep_done[0]:
            return
        for _ in range(n):
            try:
                next(prep)
            except StopIteration:
                prep_done[0] = True
                return

    def ln_finish(T, s1_ap, s1_keys, s2_ap, s2_keys, mb_tiles, rb_tiles):
        m_row = r_a[0:1, 0:T]
        v_row = r_b[0:1, 0:T]
        s.op("vector", lambda e: e.tensor_scalar(
            out=m_row, in0=s1_ap, scalar1=1.0 / D, scalar2=None, op0=ALU.mult),
            reads=list(s1_keys), writes=["r_a"])
        s.op("vector", lambda e: e.tensor_tensor(
            out=v_row, in0=m_row, in1=m_row, op=ALU.mult),
            reads=["r_a"], writes=["r_b"])
        s.op("vector", lambda e: e.scalar_tensor_tensor(
            out=v_row, in0=s2_ap, scalar=1.0 / D, in1=v_row,
            op0=ALU.mult, op1=ALU.subtract),
            reads=list(s2_keys) + ["r_b"], writes=["r_b"])
        s.op("scalar", lambda e: e.activation(
            out=v_row, in_=v_row, func=AF.Sqrt, bias=C_EPS1),
            reads=["r_b", "k_consts"], writes=["r_b"])
        s.op("vector", lambda e: e.reciprocal(out=v_row, in_=v_row),
             reads=["r_b"], writes=["r_b"])
        for ap_, bk, j0, j1 in mb_tiles:
            s.op("tensor", lambda e, ap_=ap_, j0=j0, j1=j1: e.matmul(
                out=ap_, lhsT=k_onesr, rhs=m_row[0:1, j0:j1],
                start=True, stop=True),
                reads=["r_a", "k_onesr"], writes=[bk])
        for ap_, bk, j0, j1 in rb_tiles:
            s.op("tensor", lambda e, ap_=ap_, j0=j0, j1=j1: e.matmul(
                out=ap_, lhsT=k_onesr, rhs=v_row[0:1, j0:j1],
                start=True, stop=True),
                reads=["r_b", "k_onesr"], writes=[bk])

    NHG = NH // 2

    for st in range(NST):
        r0 = st * STOWN
        xkeys = [f"xbf{i}" for i, (q0, q1) in enumerate(xr)
                 if q0 < r0 + STT and q1 > r0]
        if debug and st == 0:
            s.op("gpsimd", lambda e: e.dma_start(out=taps["t_xbf"],
                                                 in_=d_xbf[0:640, :]),
                 reads=["xbf0"], writes=["OUT"], dma=True, acc=True)
        for c in range(KCH):
            xk = f"xln1:{c}"
            s.op("sync", lambda e, c=c, r0=r0: e.dma_start(
                out=a_xln1[:, c * STT:(c + 1) * STT],
                in_=d_xbf[r0:r0 + STT, c * 128:(c + 1) * 128],
                transpose=True), reads=xkeys, writes=[xk], dma=True,
                stream=f"xfm{c % 4}")
            s.op("scalar", lambda e, c=c: e.activation(
                out=a_tmp1.bitcast(BF16)[:, 0:STT],
                in_=a_xln1[:, c * STT:(c + 1) * STT], func=AF.Square),
                reads=[xk], writes=["a_tmp1"])
            for j0, j1 in ((0, 512), (512, 640)):
                half = "a" if j0 == 0 else "b"
                s.op("tensor", lambda e, c=c, j0=j0, j1=j1: e.matmul(
                    out=pp[2][0:1, j0:j1], lhsT=k_onesb,
                    rhs=a_xln1[:, c * STT + j0:c * STT + j1],
                    start=(c == 0), stop=(c == KCH - 1)),
                    reads=[f"xln1:{c}", "k_onesb"],
                    writes=[PK(2, half)] if c in (0, KCH - 1) else [],
                    acc=(c == KCH - 1))
                s.op("tensor", lambda e, c=c, j0=j0, j1=j1: e.matmul(
                    out=pp[3][0:1, j0:j1], lhsT=k_onesb,
                    rhs=a_tmp1.bitcast(BF16)[:, j0:j1],
                    start=(c == 0), stop=(c == KCH - 1)),
                    reads=["a_tmp1", "k_onesb"],
                    writes=[PK(3, half)] if c in (0, KCH - 1) else [],
                    acc=(c == KCH - 1))
        pump(8)
        ln_finish(STT, pp[2][0:1, 0:STT], [PK(2, "a"), PK(2, "b")],
                  pp[3][0:1, 0:STT], [PK(3, "a"), PK(3, "b")],
                  [(pp[2][:, 0:512], PK(2, "a"), 0, 512),
                   (pp[2][:, 512:640], PK(2, "b"), 512, 640)],
                  [(pp[3][:, 0:512], PK(3, "a"), 0, 512),
                   (pp[3][:, 512:640], PK(3, "b"), 512, 640)])
        for c in range(KCH):
            s.op("vector", lambda e, c=c: e.tensor_tensor(
                out=a_tmp2, in0=a_xln1[:, c * STT:(c + 1) * STT],
                in1=pp[2][:, 0:STT], op=ALU.subtract),
                reads=[f"xln1:{c}", PK(2, "a"), PK(2, "b")], writes=["a_tmp2"])
            s.op("vector", lambda e, c=c: e.tensor_tensor(
                out=a_xln1[:, c * STT:(c + 1) * STT], in0=a_tmp2,
                in1=pp[3][:, 0:STT], op=ALU.mult),
                reads=["a_tmp2", PK(3, "a"), PK(3, "b")], writes=[f"xln1:{c}"])
        pump(8)
        xlnkeys = [f"xln1:{c}" for c in range(KCH)]

        if debug and st == 0:
            s.op("gpsimd", lambda e: e.dma_start(out=taps["t_xln1"], in_=a_xln1),
                 reads=xlnkeys, writes=["OUT"], dma=True, acc=True)
        if debug and st == 1:
            s.op("gpsimd", lambda e: e.dma_start(out=taps["t_xln1b"], in_=a_xln1),
                 reads=xlnkeys, writes=["OUT"], dma=True, acc=True)

        CHB = [(pp[3], 0, PK(3, "a")), (pp[3], 512, PK(3, "b")),
               (pp[2], 0, PK(2, "a")), (pp[2], 512, PK(2, "b"))]
        qv = d_wqkvbf.rearrange("(c p) f -> p c f", p=128)

        def emit_loads(h):
            if h >= NH:
                return
            wqk, wqkk = a_wqk[h % 2], f"awqk{h % 2}"
            wv2 = wqk.rearrange("p (c f) -> p c f", c=KCH)
            s.op("sync", lambda e, h=h, wv2=wv2: e.dma_start(
                out=wv2, in_=qv[:, :, h * 384:h * 384 + 256]),
                reads=["wqkvbf"], writes=[wqkk], dma=True,
                stream=f"wqk{h % 2}")
            if h % 2 == 0:
                hg = h // 2
                h0 = hg * 2
                vb = a_wv[hg % 2]
                vbk = f"awv{hg % 2}"
                wvv = vb.rearrange("p (c f) -> p c f", c=KCH)
                for hh in range(2):
                    s.op("sync", lambda e, hh=hh, wvv=wvv, h0=h0: e.dma_start(
                        out=wvv[:, :, hh * 128:(hh + 1) * 128],
                        in_=qv[:, :, (h0 + hh) * 384 + 256:(h0 + hh) * 384 + 384]),
                        reads=["wqkvbf"], writes=[vbk], dma=True, acc=(hh == 1),
                        stream=f"wv{hg % 2}")

        def emit_V(hg):
            vbuf = a_v[hg % 2]
            vk = f"av{hg % 2}"
            wv = a_wv[hg % 2]
            wvk = f"awv{hg % 2}"
            for tt in range(NTT):
                par = tt % 2
                pv = pp[0][:, 0:256] if par == 0 else pp[0][:, 512:768]
                pvk = PK(0, "a" if par == 0 else "b")
                for kc in range(KCH):
                    s.op("tensor", lambda e, tt=tt, kc=kc, pv=pv, wv=wv: e.matmul(
                        out=pv,
                        lhsT=a_xln1[:, kc * STT + tt * 128:kc * STT + (tt + 1) * 128],
                        rhs=wv[:, kc * 256:(kc + 1) * 256],
                        start=(kc == 0), stop=(kc == KCH - 1)),
                        reads=xlnkeys + [wvk],
                        writes=[pvk] if kc in (0, KCH - 1) else [],
                        acc=(kc == KCH - 1))
                s.op("scalar", lambda e, tt=tt, pv=pv, vbuf=vbuf: e.copy(
                    out=vbuf[:, tt * 256:(tt + 1) * 256], in_=pv),
                    reads=[pvk], writes=[f"{vk}:{tt}"])
            if debug and st == 1 and hg == 0:
                s.op("gpsimd", lambda e, vbuf=vbuf: e.dma_start(
                    out=taps["t_vb"], in_=vbuf),
                    reads=[f"{vk}:{t}" for t in range(NTT)], writes=["OUT"],
                    dma=True, acc=True)

        def emit_QKrope(h):
            wqk, wqkk = a_wqk[h % 2], f"awqk{h % 2}"
            for kc in range(KCH):
                s.op("tensor", lambda e, kc=kc, wqk=wqk: e.matmul(
                    out=pp[0][:, 0:512],
                    lhsT=wqk[:, kc * 256:kc * 256 + 128],
                    rhs=a_xln1[:, kc * STT + 128:kc * STT + STT],
                    start=(kc == 0), stop=(kc == KCH - 1)),
                    reads=xlnkeys + [wqkk],
                    writes=[PK(0, "a")] if kc in (0, KCH - 1) else [],
                    acc=(kc == KCH - 1))
            s.op("scalar", lambda e: e.copy(out=a_qraw, in_=pp[0][:, 0:512]),
                 reads=[PK(0, "a")], writes=["a_qraw"])
            for kc in range(KCH):
                for j0, j1 in ((0, 512), (512, 640)):
                    half = "a" if j0 == 0 else "b"
                    s.op("tensor", lambda e, kc=kc, j0=j0, j1=j1, wqk=wqk: e.matmul(
                        out=pp[1][:, j0:j1],
                        lhsT=wqk[:, kc * 256 + 128:kc * 256 + 256],
                        rhs=a_xln1[:, kc * STT + j0:kc * STT + j1],
                        start=(kc == 0), stop=(kc == KCH - 1)),
                        reads=xlnkeys + [wqkk],
                        writes=[PK(1, half)] if kc in (0, KCH - 1) else [],
                        acc=(kc == KCH - 1))
            s.op("scalar", lambda e: e.copy(out=a_kraw[:, 0:512],
                                            in_=pp[1][:, 0:512]),
                 reads=[PK(1, "a")], writes=["a_kraw"])
            s.op("scalar", lambda e: e.copy(out=a_kraw[:, 512:640],
                                            in_=pp[1][:, 512:640]),
                 reads=[PK(1, "b")], writes=["a_kraw"], acc=True)
            s.op("tensor", lambda e: e.matmul(
                out=pp[0][:, 512:1024], lhsT=k_rt, rhs=a_qraw,
                start=True, stop=True),
                reads=["a_qraw", "k_rt"], writes=[PK(0, "b")])
            for j0, j1 in ((0, 512), (512, 640)):
                half = "a" if j0 == 0 else "b"
                s.op("tensor", lambda e, j0=j0, j1=j1: e.matmul(
                    out=pp[1][:, j0:j1], lhsT=k_rt, rhs=a_kraw[:, j0:j1],
                    start=True, stop=True),
                    reads=["a_kraw", "k_rt"], writes=[PK(1, half)])
            cq = k_cos[:, 128 + r0:128 + r0 + 512]
            sq_ = k_sin[:, 128 + r0:128 + r0 + 512]
            ck = k_cos[:, r0:r0 + STT]
            sk = k_sin[:, r0:r0 + STT]
            s.op("gpsimd", lambda e, cq=cq: e.tensor_tensor(
                out=a_tmp1[:, 0:512], in0=a_qraw, in1=cq, op=ALU.mult),
                reads=["a_qraw", "k_cos"], writes=["a_tmp1"])
            s.op("vector", lambda e, sq_=sq_: e.tensor_tensor(
                out=a_tmp2[:, 0:512], in0=pp[0][:, 512:1024], in1=sq_,
                op=ALU.mult),
                reads=[PK(0, "b"), "k_sin"], writes=["a_tmp2"])
            s.op("vector", lambda e: e.tensor_tensor(
                out=a_q, in0=a_tmp1[:, 0:512], in1=a_tmp2[:, 0:512],
                op=ALU.add),
                reads=["a_tmp1", "a_tmp2"], writes=["a_q"])
            s.op("gpsimd", lambda e, ck=ck: e.tensor_tensor(
                out=a_tmp1[:, 0:STT], in0=a_kraw, in1=ck, op=ALU.mult),
                reads=["a_kraw", "k_cos"], writes=["a_tmp1"])
            for j0, j1 in ((0, 512), (512, 640)):
                half = "a" if j0 == 0 else "b"
                s.op("vector", lambda e, j0=j0, j1=j1, sk=sk: e.tensor_tensor(
                    out=a_tmp2[:, j0:j1], in0=pp[1][:, j0:j1],
                    in1=sk[:, j0:j1], op=ALU.mult),
                    reads=[PK(1, half), "k_sin"], writes=["a_tmp2"],
                    acc=(j0 != 0))
            s.op("vector", lambda e: e.tensor_tensor(
                out=a_k, in0=a_tmp1[:, 0:STT], in1=a_tmp2[:, 0:STT],
                op=ALU.add),
                reads=["a_tmp1", "a_tmp2"], writes=["a_k"])
            if debug and st == 0 and h == 0:
                s.op("gpsimd", lambda e: e.dma_start(out=taps["t_q"], in_=a_q),
                     reads=["a_q"], writes=["OUT"], dma=True, acc=True)
                s.op("gpsimd", lambda e: e.dma_start(out=taps["t_k"], in_=a_k),
                     reads=["a_k"], writes=["OUT"], dma=True, acc=True)
            if debug and st == 1 and h == 0:
                s.op("gpsimd", lambda e: e.dma_start(out=taps["t_qb"], in_=a_q),
                     reads=["a_q"], writes=["OUT"], dma=True, acc=True)
                s.op("gpsimd", lambda e: e.dma_start(out=taps["t_kb"], in_=a_k),
                     reads=["a_k"], writes=["OUT"], dma=True, acc=True)

        def emit_A(h):
            for qt in range(NQT):
                ppx, off, chk = CHB[qt]
                pst = ppx[:, off:off + 256]
                atr, at = a_atr[qt], a_at[qt]
                for ksub in range(2):
                    ktt = qt + ksub
                    s.op("tensor", lambda e, ksub=ksub, ktt=ktt, pst=pst, qt=qt: e.matmul(
                        out=pst[:, ksub * 128:(ksub + 1) * 128],
                        lhsT=a_k[:, ktt * 128:(ktt + 1) * 128],
                        rhs=a_q[:, qt * 128:(qt + 1) * 128],
                        start=(ksub == 0), stop=(ksub == 1)),
                        reads=["a_k", "a_q"],
                        writes=[chk], acc=(ksub == 1))
                s.op("scalar", lambda e, pst=pst, atr=atr: e.activation(
                    out=atr, in_=pst, func=AF.Exp, scale=C_SMS),
                    reads=[chk, "k_consts"], writes=[f"atr{qt}"])
                mask = k_mtf if (st == 0 and qt == 0) else k_mtr
                mkey = "k_mtf" if (st == 0 and qt == 0) else "k_mtr"
                s.op("gpsimd", lambda e, atr=atr, at=at, mask=mask: e.tensor_tensor(
                    out=at, in0=atr, in1=mask, op=ALU.mult),
                    reads=[f"atr{qt}", mkey], writes=[f"at{qt}"])
                if debug and st == 0 and h == 0 and qt == 0:
                    s.op("gpsimd", lambda e, at=at: e.dma_start(
                        out=taps["t_at"], in_=at),
                        reads=[f"at{qt}"], writes=["OUT"], dma=True, acc=True)

        def emit_B(h):
            hh = h % 2
            vbuf = a_v[(h // 2) % 2]
            vk = f"av{(h // 2) % 2}"
            for qt in range(NQT):
                ppx, off, chk = CHB[qt]
                prs = ppx[0:1, off + 256:off + 384]
                at = a_at[qt]
                for ksub in range(2):
                    s.op("tensor", lambda e, ksub=ksub, at=at, prs=prs: e.matmul(
                        out=prs, lhsT=k_onesb,
                        rhs=at[:, ksub * 128:(ksub + 1) * 128],
                        start=(ksub == 0), stop=(ksub == 1)),
                        reads=[f"at{qt}", "k_onesb"],
                        writes=[chk], acc=True)
            for qt in range(NQT):
                ppx, off, chk = CHB[qt]
                prs = ppx[0:1, off + 256:off + 384]
                rrrow = r_rr[0:1, qt * 128:qt * 128 + 128]
                s.op("vector", lambda e, prs=prs, rrrow=rrrow: e.reciprocal(
                    out=rrrow, in_=prs),
                    reads=[chk], writes=[f"rr{qt}"])
            for qt in range(NQT):
                ppx, off, chk = CHB[qt]
                prrb = ppx[:, off + 384:off + 512]
                rrrow = r_rr[0:1, qt * 128:qt * 128 + 128]
                s.op("tensor", lambda e, rrrow=rrrow, prrb=prrb: e.matmul(
                    out=prrb, lhsT=k_onesr, rhs=rrrow, start=True, stop=True),
                    reads=[f"rr{qt}", "k_onesr"],
                    writes=[chk], acc=True)
            for qt in range(NQT):
                ppx, off, chk = CHB[qt]
                prrb = ppx[:, off + 384:off + 512]
                s.op("scalar", lambda e, prrb=prrb, rrb=a_rrb[qt]: e.copy(
                    out=rrb, in_=prrb),
                    reads=[chk], writes=[f"rrb{qt}"])
            for qt in range(NQT):
                ppx, off, chk = CHB[qt]
                pav = ppx[:, off:off + 128]
                at = a_at[qt]
                for ksub in range(2):
                    ktt = qt + ksub
                    s.op("tensor", lambda e, ksub=ksub, ktt=ktt, at=at, pav=pav, hh=hh, vbuf=vbuf: e.matmul(
                        out=pav,
                        lhsT=vbuf[:, ktt * 256 + hh * 128:ktt * 256 + hh * 128 + 128],
                        rhs=at[:, ksub * 128:(ksub + 1) * 128],
                        start=(ksub == 0), stop=(ksub == 1)),
                        reads=[f"at{qt}", f"{vk}:{ktt}"],
                        writes=[chk], acc=True)
            for qt in range(NQT):
                ppx, off, chk = CHB[qt]
                pav = ppx[:, off:off + 128]
                col = st * 512 + qt * 128
                s.op("vector", lambda e, pav=pav, rrb=a_rrb[qt], h=h, col=col: e.tensor_tensor(
                    out=a_attn[:, h * TOK + col:h * TOK + col + 128],
                    in0=pav, in1=rrb, op=ALU.mult),
                    reads=[chk, f"rrb{qt}"], writes=[f"attn:{h}"], acc=True)

        emit_loads(0)
        emit_loads(1)
        prev = None
        for h in range(NH):
            if prev is not None:
                emit_A(prev)
            if h % 2 == 0:
                emit_V(h // 2)
            emit_QKrope(h)
            emit_loads(h + 2)
            if prev is not None:
                emit_B(prev)
                pump(6)
            prev = h
        emit_A(prev)
        emit_B(prev)
        pump(6)

    attnkeys = [f"attn:{h}" for h in range(NH)]
    if debug:
        s.op("gpsimd", lambda e: e.dma_start(out=taps["t_attn"], in_=a_attn),
             reads=attnkeys, writes=["OUT"], dma=True, acc=True)

    pump(10 ** 9)

    if debug:
        s.op("gpsimd", lambda e: e.dma_start(out=taps["t_scl"],
                                              in_=r_s.broadcast_to([128, 16])
                                              if False else r_s),
             reads=["r_s"], writes=["OUT"], dma=True, acc=True)
        s.op("gpsimd", lambda e: e.dma_start(out=taps["t_wut"], in_=d_wut[0:256, :]),
             reads=["tern_u"], writes=["OUT"], dma=True, acc=True)
        s.op("gpsimd", lambda e: e.dma_start(out=taps["t_wdt"], in_=d_wdt[0:256, :]),
             reads=["tern_d"], writes=["OUT"], dma=True, acc=True)

    att_work_keys = (["a_tmp1", "a_tmp2", "a_q", "a_qraw",
                      "a_k", "a_kraw", "awv0", "awv1", "awqk0", "awqk1"]
                     + [f"atr{q}" for q in range(4)] + [f"at{q}" for q in range(4)]
                     + [f"rrb{q}" for q in range(4)]
                     + [f"av{p}:{t}" for p in range(2) for t in range(NTT)]
                     + xlnkeys
                     + [f"th{sg}_{mn}" for sg in ("p", "n") for mn in ("u", "g", "d")]
                     + ["ttn0", "ttn1"])
    s.alias(["c_ybf", "c_wout", "c_sq", "c_t", "cz10", "cz11"],
            att_work_keys)
    s.alias([f"ybf:{fc}" for fc in range(KCH)], att_work_keys)

    for ec in range(KCH):
        s.op("sync", lambda e, ec=ec: e.dma_start(
            out=c_wout[:, ec * 2048:(ec + 1) * 2048],
            in_=d_woutbf[ec * 128:(ec + 1) * 128, :]),
            reads=["woutbf"], writes=["c_wout"], dma=True, acc=(ec > 0),
            stream="wout")
    for tq in range(4):
        tq0 = tq * 512
        for fc in range(KCH):
            par = fc % 2
            py = pp[par][:, 0:512]
            pyk = PK(par, "a")
            for ec in range(KCH):
                s.op("tensor", lambda e, ec=ec, fc=fc, py=py, tq0=tq0: e.matmul(
                    out=py, lhsT=c_wout[:, ec * 2048 + fc * 128:ec * 2048 + (fc + 1) * 128],
                    rhs=a_attn[:, ec * TOK + tq0:ec * TOK + tq0 + 512],
                    start=(ec == 0), stop=(ec == KCH - 1)),
                    reads=attnkeys + ["c_wout"],
                    writes=[pyk] if ec in (0, KCH - 1) else [],
                    acc=(ec == KCH - 1))
            s.op("vector", lambda e, py=py, fc=fc: e.tensor_scalar(
                out=c_ybf[:, fc * 512:(fc + 1) * 512], in0=py,
                scalar1=k_bout[:, fc:fc + 1], scalar2=None, op0=ALU.add),
                reads=[pyk, "k_bout"], writes=[f"ybf:{fc}"])
            pump(3)
        for fc in range(KCH):
            s.op("scalar", lambda e, fc=fc: e.activation(
                out=c_sq, in_=c_ybf[:, fc * 512:(fc + 1) * 512], func=AF.Square),
                reads=[f"ybf:{fc}"], writes=["c_sq"])
            s.op("tensor", lambda e, fc=fc: e.matmul(
                out=pp[2][0:1, 0:512], lhsT=k_onesb,
                rhs=c_ybf[:, fc * 512:(fc + 1) * 512],
                start=(fc == 0), stop=(fc == KCH - 1)),
                reads=[f"ybf:{fc}", "k_onesb"],
                writes=[PK(2, "a")] if fc in (0, KCH - 1) else [],
                acc=(fc == KCH - 1))
            s.op("tensor", lambda e, fc=fc: e.matmul(
                out=pp[2][0:1, 512:1024], lhsT=k_onesb, rhs=c_sq,
                start=(fc == 0), stop=(fc == KCH - 1)),
                reads=["c_sq", "k_onesb"],
                writes=[PK(2, "b")] if fc in (0, KCH - 1) else [],
                acc=(fc == KCH - 1))
        ln_finish(512, pp[2][0:1, 0:512], [PK(2, "a")],
                  pp[2][0:1, 512:1024], [PK(2, "b")],
                  [(pp[3][:, 0:512], PK(3, "a"), 0, 512)],
                  [(pp[3][:, 512:1024], PK(3, "b"), 0, 512)])
        for fc in range(KCH):
            zb, zbk = c_z1[fc % 2], f"cz1{fc % 2}"
            s.op("vector", lambda e, fc=fc: e.tensor_tensor(
                out=c_t, in0=c_ybf[:, fc * 512:(fc + 1) * 512],
                in1=pp[3][:, 0:512], op=ALU.subtract),
                reads=[f"ybf:{fc}", PK(3, "a")], writes=["c_t"])
            s.op("vector", lambda e, zb=zb: e.tensor_tensor(
                out=zb, in0=c_t, in1=pp[3][:, 512:1024], op=ALU.mult),
                reads=["c_t", PK(3, "b")], writes=[zbk])
            s.op("gpsimd", lambda e, fc=fc, tq0=tq0, zb=zb: e.dma_start(
                out=d_z1[fc * 128:(fc + 1) * 128, tq0:tq0 + 512], in_=zb),
                reads=[zbk], writes=["z1s"], dma=True, acc=True, stream="z1w")

    if debug:
        s.op("gpsimd", lambda e: e.dma_start(out=taps["t_z1"], in_=d_z1),
             reads=["z1s"], writes=["OUT"], dma=True, acc=True)

    c_keys = (["c_ybf", "c_wout", "c_sq", "c_t", "cz10", "cz11"]
              + [f"ybf:{fc}" for fc in range(KCH)]
              + ["t_s1", "ttn0", "ttn1", "twch0", "twch1"] + attnkeys)
    ffn_first = (["fc10", "fc11", "fc20", "fc21", "f_t", "f_wu", "f_wg", "f_wd"]
                 + [f"xn:{c}" for c in range(KCH)]
                 + [f"hid:{ec}" for ec in range(EB)]
                 + [f"facc:{dc}:{tt}" for dc in range(KCH) for tt in range(2)])
    s.alias(ffn_first, c_keys)


    for hf in range(2):
        hr0 = HALO + hf * HT
        hx = [f"xbf{i}" for i, (q0, q1) in enumerate(xr)
              if q0 < hr0 + HT and q1 > hr0]
        if hf == 1:
            s.alias([f"xn:{c}" for c in range(KCH)],
                    [f"zsum:{dc}" for dc in range(KCH)])
            s.alias(["f_wu", "f_wg", "f_wd", "fc10", "fc11", "fc20", "fc21",
                     "f_t"],
                    ["zsT0", "zsT1", "ex0", "ex1", "eout0", "eout1"]
                    + [f"zsum:{dc}" for dc in range(KCH)])
        for c in range(KCH):
            xc, xck = f_c1[c % 2], f"fc1{c % 2}"
            zc, zck = f_c2[c % 2], f"fc2{c % 2}"
            s.op("sync", lambda e, c=c, xc=xc, hr0=hr0: e.dma_start(
                out=xc, in_=d_xbf[hr0:hr0 + HT, c * 128:(c + 1) * 128],
                transpose=True), reads=hx, writes=[xck], dma=True,
                stream=f"xfm{c % 4}")
            s.op("gpsimd", lambda e, c=c, zc=zc, hf=hf: e.dma_start(
                out=zc, in_=d_z1[c * 128:(c + 1) * 128, hf * HT:hf * HT + HT]),
                reads=["z1s"], writes=[zck], dma=True, stream=f"z1r{c % 2}")
            s.op("vector", lambda e, c=c, xc=xc, zc=zc: e.tensor_tensor(
                out=f_xln2[:, c * HT:(c + 1) * HT], in0=xc, in1=zc, op=ALU.add),
                reads=[xck, zck], writes=[f"xn:{c}"])
            s.op("scalar", lambda e, c=c: e.activation(
                out=f_t.bitcast(BF16)[:, 0:HT],
                in_=f_xln2[:, c * HT:(c + 1) * HT], func=AF.Square),
                reads=[f"xn:{c}"], writes=["f_t"])
            for j0, j1 in ((0, 512), (512, 1024)):
                half = "a" if j0 == 0 else "b"
                s.op("tensor", lambda e, c=c, j0=j0, j1=j1: e.matmul(
                    out=pp[2][0:1, j0:j1], lhsT=k_onesb,
                    rhs=f_xln2[:, c * HT + j0:c * HT + j1],
                    start=(c == 0), stop=(c == KCH - 1)),
                    reads=[f"xn:{c}", "k_onesb"],
                    writes=[PK(2, half)] if c in (0, KCH - 1) else [],
                    acc=(c == KCH - 1))
                s.op("tensor", lambda e, c=c, j0=j0, j1=j1: e.matmul(
                    out=pp[3][0:1, j0:j1], lhsT=k_onesb,
                    rhs=f_t.bitcast(BF16)[:, j0:j1],
                    start=(c == 0), stop=(c == KCH - 1)),
                    reads=["f_t", "k_onesb"],
                    writes=[PK(3, half)] if c in (0, KCH - 1) else [],
                    acc=(c == KCH - 1))
        ln_finish(HT, pp[2][0:1, 0:HT], [PK(2, "a"), PK(2, "b")],
                  pp[3][0:1, 0:HT], [PK(3, "a"), PK(3, "b")],
                  [(pp[0][:, 0:512], PK(0, "a"), 0, 512),
                   (pp[0][:, 512:1024], PK(0, "b"), 512, 1024)],
                  [(pp[1][:, 0:512], PK(1, "a"), 0, 512),
                   (pp[1][:, 512:1024], PK(1, "b"), 512, 1024)])
        for c in range(KCH):
            s.op("vector", lambda e, c=c: e.tensor_tensor(
                out=f_t, in0=f_xln2[:, c * HT:(c + 1) * HT],
                in1=pp[0][:, 0:HT], op=ALU.subtract),
                reads=[f"xn:{c}", PK(0, "a"), PK(0, "b")], writes=["f_t"])
            s.op("vector", lambda e, c=c: e.tensor_tensor(
                out=f_xln2[:, c * HT:(c + 1) * HT], in0=f_t,
                in1=pp[1][:, 0:HT], op=ALU.mult),
                reads=["f_t", PK(1, "a"), PK(1, "b")], writes=[f"xln2:{c}"])
        xln2keys = [f"xln2:{c}" for c in range(KCH)]
        if debug and hf == 0:
            s.op("gpsimd", lambda e: e.dma_start(out=taps["t_xln2"], in_=f_xln2),
                 reads=xln2keys, writes=["OUT"], dma=True, acc=True)

        NB = ECH // EB

        def ld_wu(bb):
            if bb >= NB:
                return
            e0 = bb * EB
            for kd in range(KCH):
                s.op("sync", lambda e, kd=kd, e0=e0: e.dma_start(
                    out=f_wu[:, kd * EB * 128:(kd + 1) * EB * 128],
                    in_=d_wut[e0 * 128:e0 * 128 + EB * 128,
                              kd * 128:(kd + 1) * 128], transpose=True),
                    reads=["tern_u"], writes=["f_wu"], dma=True, acc=(kd > 0),
                    stream="fwu")

        def ld_wg(bb):
            if bb >= NB:
                return
            e0 = bb * EB
            for kd in range(KCH):
                s.op("sync", lambda e, kd=kd, e0=e0: e.dma_start(
                    out=f_wg[:, kd * EB * 128:(kd + 1) * EB * 128],
                    in_=d_wgt[e0 * 128:e0 * 128 + EB * 128,
                              kd * 128:(kd + 1) * 128], transpose=True),
                    reads=["tern_g"], writes=["f_wg"], dma=True, acc=(kd > 0),
                    stream="fwg")

        def ld_wd(bb):
            if bb >= NB:
                return
            e0 = bb * EB
            for ec in range(EB):
                s.op("sync", lambda e, ec=ec, e0=e0: e.dma_start(
                    out=f_wd[:, ec * D:(ec + 1) * D],
                    in_=d_wdt[:, (e0 + ec) * 128:(e0 + ec + 1) * 128],
                    transpose=True),
                    reads=["tern_d"], writes=["f_wd"], dma=True, acc=(ec > 0),
                    stream="fwd")

        ld_wu(0)
        ld_wg(0)
        ld_wd(0)
        for bb in range(NB):
            for ec in range(EB):
                for jt in range(2):
                    par = jt
                    pu = pp[0][:, par * 512:(par + 1) * 512]
                    puk = PK(0, "a" if par == 0 else "b")
                    for kd in range(KCH):
                        s.op("tensor", lambda e, ec=ec, jt=jt, kd=kd, pu=pu: e.matmul(
                            out=pu,
                            lhsT=f_wu[:, kd * EB * 128 + ec * 128:kd * EB * 128 + (ec + 1) * 128],
                            rhs=f_xln2[:, kd * HT + jt * 512:kd * HT + (jt + 1) * 512],
                            start=(kd == 0), stop=(kd == KCH - 1)),
                            reads=xln2keys + ["f_wu"],
                            writes=[puk] if kd in (0, KCH - 1) else [],
                            acc=(kd == KCH - 1))
                    s.op("scalar", lambda e, ec=ec, jt=jt, pu=pu: e.activation(
                        out=f_hid[:, ec * HT + jt * 512:ec * HT + (jt + 1) * 512],
                        in_=pu, func=AF.Silu, scale=k_scl[:, 3:4]),
                        reads=[puk, "sclp_u"], writes=[f"hid:{ec}"],
                        acc=(jt == 1))
            ld_wu(bb + 1)
            for ec in range(EB):
                for jt in range(2):
                    par = jt
                    pg = pp[1][:, par * 512:(par + 1) * 512]
                    pgk = PK(1, "a" if par == 0 else "b")
                    for kd in range(KCH):
                        s.op("tensor", lambda e, ec=ec, jt=jt, kd=kd, pg=pg: e.matmul(
                            out=pg,
                            lhsT=f_wg[:, kd * EB * 128 + ec * 128:kd * EB * 128 + (ec + 1) * 128],
                            rhs=f_xln2[:, kd * HT + jt * 512:kd * HT + (jt + 1) * 512],
                            start=(kd == 0), stop=(kd == KCH - 1)),
                            reads=xln2keys + ["f_wg"],
                            writes=[pgk] if kd in (0, KCH - 1) else [],
                            acc=(kd == KCH - 1))
                    s.op("vector", lambda e, ec=ec, jt=jt, pg=pg: e.tensor_tensor(
                        out=f_hid[:, ec * HT + jt * 512:ec * HT + (jt + 1) * 512],
                        in0=pg,
                        in1=f_hid[:, ec * HT + jt * 512:ec * HT + (jt + 1) * 512],
                        op=ALU.mult),
                        reads=[pgk, f"hid:{ec}"], writes=[f"hid:{ec}"],
                        acc=(jt == 1))
            ld_wg(bb + 1)
            hidkeys = [f"hid:{ec}" for ec in range(EB)]
            for dc in range(KCH):
                for tti in range(2):
                    par = (dc * 2 + tti) % 4
                    pd = (pp[2] if par < 2 else pp[3])[
                        :, (par % 2) * 512:(par % 2) * 512 + 512]
                    pdk = PK(2 + par // 2, "a" if par % 2 == 0 else "b")
                    for ec in range(EB):
                        s.op("tensor", lambda e, dc=dc, tti=tti, ec=ec, pd=pd: e.matmul(
                            out=pd,
                            lhsT=f_wd[:, ec * D + dc * 128:ec * D + (dc + 1) * 128],
                            rhs=f_hid[:, ec * HT + tti * 512:ec * HT + (tti + 1) * 512],
                            start=(ec == 0), stop=(ec == EB - 1)),
                            reads=hidkeys + ["f_wd"],
                            writes=[pdk] if ec in (0, EB - 1) else [],
                            acc=(ec == EB - 1))
                    dst = f_acc[:, dc * HT + tti * 512:dc * HT + (tti + 1) * 512]
                    if bb == 0:
                        s.op("vector", lambda e, dst=dst, pd=pd: e.tensor_copy(
                            out=dst, in_=pd),
                            reads=[pdk], writes=[f"facc:{dc}:{tti}"])
                    else:
                        s.op("vector", lambda e, dst=dst, pd=pd: e.tensor_tensor(
                            out=dst, in0=pd, in1=dst, op=ALU.add),
                            reads=[pdk, f"facc:{dc}:{tti}"],
                            writes=[f"facc:{dc}:{tti}"])
            ld_wd(bb + 1)

        facckeys = [f"facc:{dc}:{tti}" for dc in range(KCH) for tti in range(2)]
        if debug and hf == 0:
            s.op("gpsimd", lambda e: e.dma_start(
                out=taps["t_acc"], in_=f_acc[:, 0:2 * HT]),
                reads=facckeys, writes=["OUT"], dma=True, acc=True)

        for dc in range(KCH):
            cb, cbk = f_c1[dc % 2], f"fc1{dc % 2}"
            s.op("gpsimd", lambda e, dc=dc, cb=cb: e.tensor_copy(
                out=cb, in_=f_acc[:, dc * HT:(dc + 1) * HT]),
                reads=[f"facc:{dc}:0", f"facc:{dc}:1"], writes=[cbk])
            s.op("scalar", lambda e, cb=cb: e.activation(
                out=f_t.bitcast(BF16)[:, 0:HT], in_=cb, func=AF.Square),
                reads=[cbk], writes=["f_t"])
            for j0, j1 in ((0, 512), (512, 1024)):
                half = "a" if j0 == 0 else "b"
                s.op("tensor", lambda e, dc=dc, j0=j0, j1=j1, cb=cb: e.matmul(
                    out=pp[2][0:1, j0:j1], lhsT=k_onesb, rhs=cb[:, j0:j1],
                    start=(dc == 0), stop=(dc == KCH - 1)),
                    reads=[cbk, "k_onesb"],
                    writes=[PK(2, half)] if dc in (0, KCH - 1) else [],
                    acc=(dc == KCH - 1))
                s.op("tensor", lambda e, dc=dc, j0=j0, j1=j1: e.matmul(
                    out=pp[3][0:1, j0:j1], lhsT=k_onesb,
                    rhs=f_t.bitcast(BF16)[:, j0:j1],
                    start=(dc == 0), stop=(dc == KCH - 1)),
                    reads=["f_t", "k_onesb"],
                    writes=[PK(3, half)] if dc in (0, KCH - 1) else [],
                    acc=(dc == KCH - 1))
        ln_finish(HT, pp[2][0:1, 0:HT], [PK(2, "a"), PK(2, "b")],
                  pp[3][0:1, 0:HT], [PK(3, "a"), PK(3, "b")],
                  [(pp[0][:, 0:512], PK(0, "a"), 0, 512),
                   (pp[0][:, 512:1024], PK(0, "b"), 512, 1024)],
                  [(pp[1][:, 0:512], PK(1, "a"), 0, 512),
                   (pp[1][:, 512:1024], PK(1, "b"), 512, 1024)])
        s.alias([f"zsum:{dc}" for dc in range(KCH)], xln2keys)
        s.alias(["zsT0", "zsT1", "ex0", "ex1", "eout0", "eout1"],
                ["f_wu", "f_wg", "f_wd"] + [f"hid:{ec}" for ec in range(EB)])
        for dc in range(KCH):
            zc, zck = f_c2[dc % 2], f"fc2{dc % 2}"
            s.op("gpsimd", lambda e, dc=dc, zc=zc, hf=hf: e.dma_start(
                out=zc, in_=d_z1[dc * 128:(dc + 1) * 128, hf * HT:hf * HT + HT]),
                reads=["z1s"], writes=[zck], dma=True, stream=f"z1r{dc % 2}")
            s.op("vector", lambda e, dc=dc: e.tensor_tensor(
                out=f_t, in0=f_acc[:, dc * HT:(dc + 1) * HT],
                in1=pp[0][:, 0:HT], op=ALU.subtract),
                reads=[f"facc:{dc}:0", f"facc:{dc}:1", PK(0, "a"), PK(0, "b")],
                writes=["f_t"])
            s.op("vector", lambda e, dc=dc: e.tensor_tensor(
                out=f_t.bitcast(BF16)[:, 0:HT], in0=f_t, in1=pp[1][:, 0:HT],
                op=ALU.mult),
                reads=["f_t", PK(1, "a"), PK(1, "b")], writes=["f_t"])
            s.op("vector", lambda e, dc=dc, zc=zc: e.tensor_tensor(
                out=e_zsum[:, dc * HT:(dc + 1) * HT],
                in0=f_t.bitcast(BF16)[:, 0:HT], in1=zc, op=ALU.add),
                reads=["f_t", zck], writes=[f"zsum:{dc}"])
        for t8 in range(8):
            par = t8 % 2
            ptr = pp[par].bitcast(BF16)
            for dc in range(KCH):
                bank = dc // 8
                slot = dc % 8
                bkk = PK(par, "a" if bank == 0 else "b")
                s.op("tensor", lambda e, dc=dc, t8=t8, ptr=ptr, bank=bank, slot=slot: e.transpose(
                    out=ptr[:, bank * 1024 + slot * 128:bank * 1024 + (slot + 1) * 128],
                    in_=e_zsum[:, dc * HT + t8 * 128:dc * HT + (t8 + 1) * 128],
                    identity=k_idn),
                    reads=[f"zsum:{dc}", "k_idn"],
                    writes=[bkk], acc=(slot > 0))
            zt = e_zsT[par]
            for bank in range(2):
                bkk = PK(par, "a" if bank == 0 else "b")
                s.op("scalar", lambda e, zt=zt, ptr=ptr, bank=bank: e.copy(
                    out=zt[:, bank * 1024:(bank + 1) * 1024],
                    in_=ptr[:, bank * 1024:bank * 1024 + 1024]),
                    reads=[bkk], writes=[f"zsT{par}"], acc=(bank > 0))
            ex = e_x[par]
            s.op("gpsimd", lambda e, t8=t8, ex=ex, hf=hf: e.dma_start(
                out=ex,
                in_=d_x[HALO + hf * HT + t8 * 128:HALO + hf * HT + (t8 + 1) * 128, :]),
                writes=[f"ex{par}"], dma=True, stream=f"xld{par}")
            eo = e_out[par]
            s.op("vector", lambda e, ex=ex, zt=zt, eo=eo: e.tensor_tensor(
                out=eo, in0=ex, in1=zt, op=ALU.add),
                reads=[f"ex{par}", f"zsT{par}"], writes=[f"eout{par}"])
            s.op("gpsimd", lambda e, t8=t8, eo=eo, hf=hf: e.dma_start(
                out=d_out[hf * HT + t8 * 128:hf * HT + (t8 + 1) * 128, :], in_=eo),
                reads=[f"eout{par}"], writes=["OUT"], dma=True, acc=True,
                stream="outw")

    s.op("sync", lambda e: e.nop(), reads=["OUT"])

    with nc.Block() as block:
        s.finalize(block)

    return nc, taps, s


def _host_inputs(inputs):
    x = np.ascontiguousarray(np.asarray(inputs["L0_attn_x"], dtype=np.float32))
    wqkv = np.asarray(inputs["L0_attn_W_qkv"], dtype=np.float32)
    wout = np.ascontiguousarray(np.asarray(inputs["L0_attn_W_out"], dtype=np.float32))
    bout = np.asarray(inputs["L0_attn_b_out"], dtype=np.float32)
    wu = np.ascontiguousarray(np.asarray(inputs["L0_ffn_W_up"], dtype=np.float32))
    wg = np.ascontiguousarray(np.asarray(inputs["L0_ffn_W_gate"], dtype=np.float32))
    wd = np.ascontiguousarray(np.asarray(inputs["L0_ffn_W_down"], dtype=np.float32))

    wqkv_r = np.ascontiguousarray(
        wqkv.reshape(D, NH, HD, 3).transpose(0, 1, 3, 2).reshape(D, 3 * D))
    bout_fm = np.ascontiguousarray(bout.reshape(16, 128).T)

    inv_freq = 1.0 / (THETA_F ** (np.arange(0, HD, 2, dtype=np.float64) / HD))

    r = np.arange(128)
    m0_rest = (r[:, None] >= r[None, :]).astype(np.float32)
    m1 = (r[:, None] <= r[None, :]).astype(np.float32)
    mask_rest = np.ascontiguousarray(
        np.concatenate([m0_rest, m1], axis=1)).astype(ml_dtypes.bfloat16)
    mask_first = np.ascontiguousarray(
        np.concatenate([np.zeros((128, 128), np.float32), m1],
                       axis=1)).astype(ml_dtypes.bfloat16)

    R = np.zeros((128, 128), np.float32)
    ii = np.arange(0, 128, 2)
    R[ii, ii + 1] = -1.0
    R[ii + 1, ii] = 1.0
    RT = np.ascontiguousarray(R.T).astype(ml_dtypes.bfloat16)
    idn = np.eye(128, dtype=ml_dtypes.bfloat16)
    onesr = np.ones((1, 128), np.float32)
    onesc = np.ones((128, 1), np.float32)
    onesb = np.ones((128, 1), ml_dtypes.bfloat16)
    consts = np.tile(np.array([[EPS_LN, 1.0 / D, SM_SCALE, 0.5,
                                1.0 / (DE * D), EPS_Q, 1.0, 0.0]],
                              np.float32), (128, 1))

    in_maps = []
    for c in range(NCORES):
        bi, sc = c // 4, c % 4
        s0 = sc * TOK
        if sc == 0:
            xs = np.concatenate([np.zeros((HALO, D), np.float32),
                                 x[bi, 0:TOK]], axis=0)
        else:
            xs = x[bi, s0 - HALO:s0 + TOK]
        xs = np.ascontiguousarray(xs)
        pos = np.maximum(np.arange(s0 - HALO, s0 + TOK, dtype=np.float64), 0.0)
        ang = np.repeat(pos[:, None] * inv_freq[None, :], 2, axis=1)
        cosT = np.ascontiguousarray(np.cos(ang).T.astype(ml_dtypes.bfloat16))
        sinT = np.ascontiguousarray(np.sin(ang).T.astype(ml_dtypes.bfloat16))
        in_maps.append({
            "x": xs, "wqkv": wqkv_r, "wout": wout, "bout": bout_fm,
            "wu": wu, "wg": wg, "wd": wd,
            "cost": cosT, "sint": sinT,
            "maskf": mask_first if sc == 0 else mask_rest,
            "maskr": mask_rest,
            "rt": RT, "idn": idn, "onesr": onesr, "onesc": onesc,
            "onesb": onesb, "consts": consts,
        })
    return in_maps


_CACHED = {}


def kernel(**inputs):
    if "nc" not in _CACHED:
        _CACHED["nc"] = build_program(debug=False)[0]
    nc = _CACHED["nc"]
    in_maps = _host_inputs(inputs)
    res = run_bass_kernel_spmd(nc, in_maps, core_ids=list(range(NCORES)))
    out = np.empty((B, S, D), dtype=np.float32)
    for c in range(NCORES):
        bi, sc = c // 4, c % 4
        out[bi, sc * TOK:(sc + 1) * TOK, :] = res.results[c]["out"]
    return out


if __name__ == "__main__":
    nc, _, _s = build_program(debug=False)
    print("built ok; instructions:", nc.n_instructions())


# version 21
# speedup vs baseline: 1.0034x; 1.0034x over previous
import numpy as np
import ml_dtypes
import concourse.bass as bass
import concourse.mybir as mybir
from concourse.bass_utils import run_bass_kernel_spmd

F32 = mybir.dt.float32
BF16 = mybir.dt.bfloat16
AF = mybir.ActivationFunctionType
ALU = mybir.AluOpType
AX = mybir.AxisListType

B, S, D = 2, 8192, 2048
NH, HD = 16, 128
DE = 8192
NCORES = 8
TOK = 2048
HALO = 128
TT = TOK + HALO
NST = 4
STOWN = 512
STT = STOWN + HALO
NQT = 4
NTT = 5
HT = 1024
EB = 4
KCH = 16
ECH = DE // 128
NWCH = 128
SM_SCALE = float(1.0 / np.sqrt(HD))
EPS_LN = 1e-5
EPS_Q = 1e-8
THETA_F = 10000.0

SB_BASE = 16512
SB_TOP = SB_BASE + 212_863


def _bytes_pp(shape, dtype):
    n = 1
    for v in shape[1:]:
        n *= v
    return n * mybir.dt.size(dtype)


class SB:
    def __init__(self, nc):
        self.nc = nc
        self.off = SB_BASE
        self.peak = SB_BASE

    def alloc(self, name, shape, dtype, align=64):
        self.off = (self.off + align - 1) // align * align
        t = self.nc.alloc_sbuf_tensor_at(name, list(shape), dtype, offset=self.off)
        self.off += _bytes_pp(shape, dtype)
        self.peak = max(self.peak, self.off)
        assert self.off <= SB_TOP, f"SBUF overflow at {name}: {self.off}"
        return t.ap()

    def mark(self):
        return self.off

    def reset(self, mark):
        self.off = mark


class Sched:
    ENGS = ("sync", "tensor", "vector", "scalar", "gpsimd")
    DMAQ = ("sync", "scalar", "gpsimd")

    def __init__(self, nc):
        self.nc = nc
        self.q = {e: [] for e in self.ENGS}
        self.sem = {e: nc.alloc_semaphore(f"es_{e}") for e in self.ENGS}
        self.cnt = {e: 0 for e in self.ENGS}
        self.dstream = {}
        self.dq_of_sem = {}
        self.hw = {e: {} for e in self.ENGS}
        self.wr = {}
        self.rd = {}
        self.nwaits = 0
        self.log = []

    def op(self, eng, fn, reads=(), writes=(), dma=False, acc=False,
           stream=None):
        waits = {}

        def add(tok):
            sem, val = tok
            if eng == "tensor" and sem is self.sem["tensor"]:
                return
            k = id(sem)
            skey = self.dq_of_sem.get(k)
            if skey is not None:
                val = max(val, self.dstream[skey][1])
            if self.hw[eng].get(k, 0) >= val:
                return
            cur = waits.get(k)
            if cur is None or cur[1] < val:
                waits[k] = (sem, val)

        for r in reads:
            for t in self.wr.get(r, ()):
                add(t)
        for w in writes:
            for t in self.wr.get(w, ()):
                add(t)
            for t in self.rd.get(w, ()):
                add(t)
        wlist = list(waits.values())
        self.nwaits += len(wlist)
        for k, (sem, val) in waits.items():
            self.hw[eng][k] = val
        if dma:
            skey = (eng, stream or eng)
            ent = self.dstream.get(skey)
            if ent is None:
                ent = [self.nc.alloc_semaphore(f"ds_{skey[0]}_{skey[1]}"), 0]
                self.dstream[skey] = ent
                self.dq_of_sem[id(ent[0])] = skey
            ent[1] += 16
            tok = (ent[0], ent[1])
            inc = 16
        else:
            self.cnt[eng] += 1
            tok = (self.sem[eng], self.cnt[eng])
            inc = 1

        def emit(e, wlist=wlist, fn=fn, tok=tok, inc=inc):
            for sem, val in wlist:
                e.wait_ge(sem, val)
            fn(e).then_inc(tok[0], inc)

        self.q[eng].append(emit)
        self.log.append((eng, list(wlist), tok, list(reads), list(writes), dma))
        for w in writes:
            if acc:
                self.wr.setdefault(w, []).append(tok)
            else:
                self.wr[w] = [tok]
                self.rd[w] = []
        for r in reads:
            self.rd.setdefault(r, []).append(tok)
        return tok

    def alias(self, new_regions, old_regions):
        toks = []
        for r in old_regions:
            toks += list(self.wr.get(r, ()))
            toks += list(self.rd.get(r, ()))
        best = {}
        for sem, val in toks:
            k = id(sem)
            if k not in best or best[k][1] < val:
                best[k] = (sem, val)
        toks = list(best.values())
        for w in new_regions:
            self.wr[w] = list(toks)
            self.rd[w] = []

    def finalize(self, block):
        for e in self.ENGS:
            ops = self.q[e]

            def run(eng, ops=ops):
                for f in ops:
                    f(eng)

            getattr(block, e)(run)


def build_program(debug=False):
    nc = bass.Bass("TRN2", target_bir_lowering=False)

    d_x = nc.dram_tensor("x", [TT, D], F32, kind="ExternalInput").ap()
    d_wqkv = nc.dram_tensor("wqkv", [D, 3 * D], F32, kind="ExternalInput").ap()
    d_wout = nc.dram_tensor("wout", [D, D], F32, kind="ExternalInput").ap()
    d_bout = nc.dram_tensor("bout", [128, 16], F32, kind="ExternalInput").ap()
    d_wu = nc.dram_tensor("wu", [DE, D], F32, kind="ExternalInput").ap()
    d_wg = nc.dram_tensor("wg", [DE, D], F32, kind="ExternalInput").ap()
    d_wd = nc.dram_tensor("wd", [D, DE], F32, kind="ExternalInput").ap()
    d_cos = nc.dram_tensor("cost", [128, TT], BF16, kind="ExternalInput").ap()
    d_sin = nc.dram_tensor("sint", [128, TT], BF16, kind="ExternalInput").ap()
    d_mtf = nc.dram_tensor("maskf", [128, 256], BF16, kind="ExternalInput").ap()
    d_mtr = nc.dram_tensor("maskr", [128, 256], BF16, kind="ExternalInput").ap()
    d_rt = nc.dram_tensor("rt", [128, 128], BF16, kind="ExternalInput").ap()
    d_idn = nc.dram_tensor("idn", [128, 128], BF16, kind="ExternalInput").ap()
    d_onesr = nc.dram_tensor("onesr", [1, 128], F32, kind="ExternalInput").ap()
    d_onesc = nc.dram_tensor("onesc", [128, 1], F32, kind="ExternalInput").ap()
    d_onesb = nc.dram_tensor("onesb", [128, 1], BF16, kind="ExternalInput").ap()
    d_consts = nc.dram_tensor("consts", [128, 8], F32, kind="ExternalInput").ap()

    d_out = nc.dram_tensor("out", [TOK, D], F32, kind="ExternalOutput").ap()

    d_xbf = nc.dram_tensor("xbf", [TT, D], BF16, kind="Internal").ap()
    d_wqkvbf = nc.dram_tensor("wqkvbf", [D, 3 * D], BF16, kind="Internal").ap()
    d_woutbf = nc.dram_tensor("woutbf", [D, D], BF16, kind="Internal").ap()
    d_wut = nc.dram_tensor("wut", [DE, D], BF16, kind="Internal").ap()
    d_wgt = nc.dram_tensor("wgt", [DE, D], BF16, kind="Internal").ap()
    d_wdt = nc.dram_tensor("wdt", [D, DE], BF16, kind="Internal").ap()
    d_z1 = nc.dram_tensor("z1s", [D, TOK], BF16, kind="Internal").ap()

    taps = {}
    if debug:
        def tap(name, shape, dtype=BF16):
            taps[name] = nc.dram_tensor(name, list(shape), dtype,
                                        kind="ExternalOutput").ap()
        tap("t_xln1", [128, KCH * STT])
        tap("t_xbf", [640, D])
        tap("t_xln1b", [128, KCH * STT])
        tap("t_kb", [128, STT])
        tap("t_qb", [128, STOWN])
        tap("t_vb", [128, NTT * 256])
        tap("t_q", [128, STOWN])
        tap("t_k", [128, STT])
        tap("t_at", [128, 256])
        tap("t_attn", [128, KCH * TOK])
        tap("t_z1", [D, TOK])
        tap("t_scl", [1, 16], F32)
        tap("t_wut", [256, D])
        tap("t_wdt", [256, DE])
        tap("t_xln2", [128, KCH * HT])
        tap("t_acc", [128, 2 * HT], F32)

    sb = SB(nc)
    k_consts = sb.alloc("k_consts", [128, 8], F32)
    k_cos = sb.alloc("k_cos", [128, TT], BF16)
    k_sin = sb.alloc("k_sin", [128, TT], BF16)
    k_mtf = sb.alloc("k_mtf", [128, 256], BF16)
    k_mtr = sb.alloc("k_mtr", [128, 256], BF16)
    k_rt = sb.alloc("k_rt", [128, 128], BF16)
    k_idn = sb.alloc("k_idn", [128, 128], BF16)
    k_onesr = sb.alloc("k_onesr", [1, 128], F32)
    k_onesc = sb.alloc("k_onesc", [128, 1], F32)
    k_onesb = sb.alloc("k_onesb", [128, 1], BF16)
    k_bout = sb.alloc("k_bout", [128, 16], F32)
    k_scl = sb.alloc("k_scl", [128, 8], F32)
    k_accu = sb.alloc("k_accu", [128, 384], F32)
    k_part = sb.alloc("k_part", [128, 8], F32)
    r_a = sb.alloc("r_a", [1, TT], F32)
    r_b = sb.alloc("r_b", [1, TT], F32)
    r_rr = sb.alloc("r_rr", [1, 512], F32)
    r_s = sb.alloc("r_s", [1, 16], F32)
    pers_end = sb.mark()

    t_wch = [sb.alloc(f"t_wch{i}", [128, 1024], F32) for i in range(3)]
    t_wcht = [sb.alloc(f"t_wcht{i}", [128, 1024], F32) for i in range(3)]
    t_s1 = sb.alloc("t_s1", [128, 1024], BF16)

    a_attn = sb.alloc("a_attn", [128, KCH * TOK], BF16)
    thp_mark = sb.mark()
    t_tn = [sb.alloc(f"t_tn{i}", [128, 1024], BF16) for i in range(2)]
    t_thp = sb.alloc("t_thp", [128, 1024], F32)
    t_thn = sb.alloc("t_thn", [128, 1024], F32)

    att_mark = sb.mark()
    a_tmp1 = sb.alloc("a_tmp1", [128, STT], F32)
    a_tmp2 = sb.alloc("a_tmp2", [128, STT], F32)
    a_xln1 = sb.alloc("a_xln1", [128, KCH * STT], BF16)
    a_q = sb.alloc("a_q", [128, STOWN], BF16)
    a_qraw = sb.alloc("a_qraw", [128, STOWN], BF16)
    a_k = sb.alloc("a_k", [128, STT], BF16)
    a_kraw = sb.alloc("a_kraw", [128, STT], BF16)
    a_v = [sb.alloc(f"a_v{i}", [128, NTT * 256], BF16) for i in range(2)]
    a_wqk = [sb.alloc(f"a_wqk{i}", [128, KCH * 256], BF16) for i in range(2)]
    a_wv = [sb.alloc(f"a_wv{i}", [128, KCH * 256], BF16) for i in range(2)]
    a_atr = [sb.alloc(f"a_atr{i}", [128, 256], BF16) for i in range(4)]
    a_at = [sb.alloc(f"a_at{i}", [128, 256], BF16) for i in range(4)]
    a_rrb = [sb.alloc(f"a_rrb{i}", [128, 128], F32) for i in range(4)]
    att_end = sb.mark()

    sb.reset(thp_mark)
    c_ybf = sb.alloc("c_ybf", [128, KCH * 512], BF16)
    c_wout = sb.alloc("c_wout", [128, KCH * 2048], BF16)
    c_sq = sb.alloc("c_sq", [128, 512], BF16)
    c_t = sb.alloc("c_t", [128, 512], F32)
    c_z1 = [sb.alloc(f"c_z1{i}", [128, 512], BF16) for i in range(2)]
    sb.reset(att_mark)

    sb.reset(pers_end)
    f_xln2 = sb.alloc("f_xln2", [128, KCH * HT], BF16)
    ffn_w_mark = sb.mark()
    f_acc = sb.alloc("f_acc", [128, KCH * HT], F32)
    ffn_hid_mark = sb.mark()
    f_hid = sb.alloc("f_hid", [128, EB * HT], BF16)
    f_wu = sb.alloc("f_wu", [128, KCH * EB * 128], BF16)
    f_wg = sb.alloc("f_wg", [128, KCH * EB * 128], BF16)
    f_wd = sb.alloc("f_wd", [128, EB * D], BF16)
    f_c1 = [sb.alloc(f"f_c1{i}", [128, HT], BF16) for i in range(2)]
    f_c2 = [sb.alloc(f"f_c2{i}", [128, HT], BF16) for i in range(2)]
    f_t = sb.alloc("f_t", [128, HT], F32)
    ffn_end = sb.mark()

    sb.reset(pers_end)
    e_zsum = sb.alloc("e_zsum", [128, KCH * HT], BF16)
    sb.reset(ffn_hid_mark)
    e_zsT = [sb.alloc(f"e_zsT{i}", [128, D], F32) for i in range(2)]
    e_x = [sb.alloc(f"e_x{i}", [128, D], F32) for i in range(2)]
    e_out = [sb.alloc(f"e_out{i}", [128, D], F32) for i in range(2)]
    sb.reset(ffn_end)

    pp = [nc.alloc_psum_tensor(f"pp{i}", [128, 1024], F32).ap() for i in range(4)]

    def PK(i, half):
        return f"P{i}{half}"

    s = Sched(nc)

    C_EPS1 = k_consts[0:1, 0:1]
    C_SMS = k_consts[:, 2:3]

    def ld(dst, src, key):
        s.op("sync", lambda e, d=dst, sr=src: e.dma_start(out=d, in_=sr),
             writes=[key], dma=True, stream="const")

    ld(k_consts, d_consts, "k_consts")
    ld(k_cos, d_cos, "k_cos")
    ld(k_sin, d_sin, "k_sin")
    ld(k_mtf, d_mtf, "k_mtf")
    ld(k_mtr, d_mtr, "k_mtr")
    ld(k_rt, d_rt, "k_rt")
    ld(k_idn, d_idn, "k_idn")
    ld(k_onesr, d_onesr, "k_onesr")
    ld(k_onesc, d_onesc, "k_onesc")
    ld(k_onesb, d_onesb, "k_onesb")
    ld(k_bout, d_bout, "k_bout")

    xr = [(0, 640), (640, 1152), (1152, 1664), (1664, 2176)]
    for i, (q0, q1) in enumerate(xr):
        s.op("gpsimd", lambda e, q0=q0, q1=q1: e.dma_start(
            out=d_xbf[q0:q1, :], in_=d_x[q0:q1, :]), writes=[f"xbf{i}"],
            dma=True, stream="cast")
    s.op("gpsimd", lambda e: e.dma_start(out=d_wqkvbf, in_=d_wqkv),
         writes=["wqkvbf"], dma=True, stream="cast")
    s.op("gpsimd", lambda e: e.dma_start(out=d_woutbf, in_=d_wout),
         writes=["woutbf"], dma=True, stream="cast")

    def wchunk(dsrc, i):
        if dsrc is d_wd:
            rr, cc = i // 8, i % 8
        else:
            rr, cc = i // 2, i % 2
        return dsrc[rr * 128:(rr + 1) * 128, cc * 1024:(cc + 1) * 1024]

    def wchunk_dst(ddst, i, is_wd):
        if is_wd:
            rr, cc = i // 8, i % 8
        else:
            rr, cc = i // 2, i % 2
        return ddst[rr * 128:(rr + 1) * 128, cc * 1024:(cc + 1) * 1024]

    def bcast_scalar_to_col(val_ap, val_key, dst_col, col_key):
        s.op("tensor", lambda e: e.matmul(out=pp[2][:, 1016:1017], lhsT=k_onesr,
                                          rhs=val_ap, start=True, stop=True),
             reads=[val_key, "k_onesr"], writes=[PK(2, "b")])
        s.op("scalar", lambda e: e.copy(out=dst_col, in_=pp[2][:, 1016:1017]),
             reads=[PK(2, "b")], writes=[col_key])

    def scale_stream(mi, mn, dsrc):
        def dma_in(i, dsrc=dsrc):
            buf, bk = t_wch[i % 3], f"twch{i % 3}"
            s.op("gpsimd", lambda e, i=i, dsrc=dsrc, buf=buf: e.dma_start(
                out=buf, in_=wchunk(dsrc, i)), writes=[bk], dma=True,
                stream=f"scalein{i % 3}")

        for i in range(3):
            dma_in(i)
        for i in range(NWCH):
            buf, bk = t_wch[i % 3], f"twch{i % 3}"
            s.op("scalar", lambda e, i=i, mi=mi, buf=buf: e.activation(
                out=t_s1, in_=buf, func=AF.Abs,
                accum_out=k_accu[:, mi * 128 + i:mi * 128 + i + 1]),
                reads=[bk], writes=["t_s1", "k_accu"], acc=(i > 0))
            if i + 3 < NWCH:
                dma_in(i + 3)
            yield
        s.op("scalar", lambda e, mi=mi: e.copy(
            out=r_s[0:1, 12:13], in_=k_accu[0:1, mi * 128:mi * 128 + 1]),
            reads=["k_accu"], writes=["accfence"])
        s.op("vector", lambda e, mi=mi: e.tensor_reduce(
            out=k_part[:, mi:mi + 1], in_=k_accu[:, mi * 128:(mi + 1) * 128],
            axis=AX.X, op=ALU.add),
            reads=["k_accu", "accfence"], writes=[f"part_{mn}"])
        pcol, prk = k_part[:, mi:mi + 1], f"part_{mn}"
        s.op("tensor", lambda e, pcol=pcol: e.matmul(
            out=pp[2][0:1, 1018:1019], lhsT=pcol, rhs=k_onesc,
            start=True, stop=True),
            reads=[prk, "k_onesc"], writes=[PK(2, "b")])
        s.op("vector", lambda e, mi=mi: e.tensor_scalar(
            out=r_s[0:1, mi:mi + 1], in0=pp[2][0:1, 1018:1019],
            scalar1=1.0 / (DE * D), scalar2=EPS_Q,
            op0=ALU.mult, op1=ALU.add),
            reads=[PK(2, "b")], writes=["r_s"], acc=True)
        s.op("vector", lambda e, mi=mi: e.tensor_scalar(
            out=r_s[0:1, mi + 4:mi + 5], in0=r_s[0:1, mi:mi + 1],
            scalar1=-0.5, scalar2=None, op0=ALU.mult),
            reads=["r_s"], writes=["r_s"], acc=True)
        s.op("vector", lambda e, mi=mi: e.tensor_scalar(
            out=r_s[0:1, mi + 8:mi + 9], in0=r_s[0:1, mi:mi + 1],
            scalar1=0.5, scalar2=None, op0=ALU.mult),
            reads=["r_s"], writes=["r_s"], acc=True)
        if mn == "u":
            bcast_scalar_to_col(r_s[0:1, 0:1], "r_s",
                                k_scl[:, 3:4], "sclp_u")
        yield

    def tern_stream(mi, mn, dsrc, ddst):
        for sgn, slot, tile, tk in ((1, mi + 8, t_thp, f"thp_{mn}"),
                                    (-1, mi + 4, t_thn, f"thn_{mn}")):
            s.op("vector", lambda e: e.memset(r_a[0:1, 0:1024], 1.0),
                 writes=["r_a"])
            for j0 in (0, 512):
                s.op("tensor", lambda e, slot=slot, j0=j0: e.matmul(
                    out=pp[2][0:1, j0:j0 + 512], lhsT=r_s[0:1, slot:slot + 1],
                    rhs=r_a[0:1, j0:j0 + 512], start=True, stop=True),
                    reads=["r_s", "r_a"],
                    writes=[PK(2, "a" if j0 == 0 else "b")])
            s.op("vector", lambda e: e.tensor_copy(
                out=r_a[0:1, 0:1024], in_=pp[2][0:1, 0:1024]),
                reads=[PK(2, "a"), PK(2, "b")], writes=["r_a"])
            s.op("tensor", lambda e: e.matmul(
                out=pp[3][:, 0:512], lhsT=k_onesr, rhs=r_a[0:1, 0:512],
                start=True, stop=True),
                reads=["r_a", "k_onesr"], writes=[PK(3, "a")])
            s.op("tensor", lambda e: e.matmul(
                out=pp[3][:, 512:1024], lhsT=k_onesr, rhs=r_a[0:1, 512:1024],
                start=True, stop=True),
                reads=["r_a", "k_onesr"], writes=[PK(3, "b")])
            s.op("vector", lambda e, tile=tile: e.tensor_copy(
                out=tile, in_=pp[3][:, 0:1024]),
                reads=[PK(3, "a"), PK(3, "b")], writes=[tk])

        def dma_in(i, dsrc=dsrc):
            buf, bk = t_wcht[i % 3], f"twcht{i % 3}"
            s.op("scalar", lambda e, i=i, dsrc=dsrc, buf=buf: e.dma_start(
                out=buf, in_=wchunk(dsrc, i)), writes=[bk], dma=True,
                stream=f"ternin{i % 3}")

        for i in range(3):
            dma_in(i)
        for i in range(NWCH):
            buf, bk = t_wcht[i % 3], f"twcht{i % 3}"
            tn, tnk = t_tn[i % 2], f"ttn{i % 2}"
            s.op("vector", lambda e, buf=buf: e.tensor_tensor(
                out=t_s1, in0=buf, in1=t_thp, op=ALU.is_gt),
                reads=[bk, f"thp_{mn}"], writes=["t_s1"])
            s.op("vector", lambda e, buf=buf, tn=tn: e.tensor_tensor(
                out=tn, in0=buf, in1=t_thn, op=ALU.is_lt),
                reads=[bk, f"thn_{mn}"], writes=[tnk])
            s.op("gpsimd", lambda e, tn=tn: e.tensor_tensor(
                out=tn, in0=t_s1, in1=tn, op=ALU.subtract),
                reads=["t_s1", tnk], writes=[tnk])
            s.op("gpsimd", lambda e, i=i, ddst=ddst, tn=tn, mn=mn: e.dma_start(
                out=wchunk_dst(ddst, i, mn == "d"), in_=tn),
                reads=[tnk], writes=[f"tern_{mn}"], dma=True, acc=True,
                stream="ternout")
            if i + 3 < NWCH:
                dma_in(i + 3)
            yield

    def weight_prep():
        import itertools
        for _ in scale_stream(0, "u", d_wu):
            yield
        for _ in itertools.zip_longest(scale_stream(1, "g", d_wg),
                                       tern_stream(0, "u", d_wu, d_wut)):
            yield
        for _ in itertools.zip_longest(scale_stream(2, "d", d_wd),
                                       tern_stream(1, "g", d_wg, d_wgt)):
            yield
        for _ in tern_stream(2, "d", d_wd, d_wdt):
            yield

    prep = weight_prep()
    prep_done = [False]

    def pump(n):
        if prep_done[0]:
            return
        for _ in range(n):
            try:
                next(prep)
            except StopIteration:
                prep_done[0] = True
                return

    def ln_finish(T, s1_ap, s1_keys, s2_ap, s2_keys, mb_tiles, rb_tiles):
        m_row = r_a[0:1, 0:T]
        v_row = r_b[0:1, 0:T]
        s.op("vector", lambda e: e.tensor_scalar(
            out=m_row, in0=s1_ap, scalar1=1.0 / D, scalar2=None, op0=ALU.mult),
            reads=list(s1_keys), writes=["r_a"])
        s.op("vector", lambda e: e.tensor_tensor(
            out=v_row, in0=m_row, in1=m_row, op=ALU.mult),
            reads=["r_a"], writes=["r_b"])
        s.op("vector", lambda e: e.scalar_tensor_tensor(
            out=v_row, in0=s2_ap, scalar=1.0 / D, in1=v_row,
            op0=ALU.mult, op1=ALU.subtract),
            reads=list(s2_keys) + ["r_b"], writes=["r_b"])
        s.op("scalar", lambda e: e.activation(
            out=v_row, in_=v_row, func=AF.Sqrt, bias=C_EPS1),
            reads=["r_b", "k_consts"], writes=["r_b"])
        s.op("vector", lambda e: e.reciprocal(out=v_row, in_=v_row),
             reads=["r_b"], writes=["r_b"])
        for ap_, bk, j0, j1 in mb_tiles:
            s.op("tensor", lambda e, ap_=ap_, j0=j0, j1=j1: e.matmul(
                out=ap_, lhsT=k_onesr, rhs=m_row[0:1, j0:j1],
                start=True, stop=True),
                reads=["r_a", "k_onesr"], writes=[bk])
        for ap_, bk, j0, j1 in rb_tiles:
            s.op("tensor", lambda e, ap_=ap_, j0=j0, j1=j1: e.matmul(
                out=ap_, lhsT=k_onesr, rhs=v_row[0:1, j0:j1],
                start=True, stop=True),
                reads=["r_b", "k_onesr"], writes=[bk])

    NHG = NH // 2

    for st in range(NST):
        r0 = st * STOWN
        xkeys = [f"xbf{i}" for i, (q0, q1) in enumerate(xr)
                 if q0 < r0 + STT and q1 > r0]
        if debug and st == 0:
            s.op("gpsimd", lambda e: e.dma_start(out=taps["t_xbf"],
                                                 in_=d_xbf[0:640, :]),
                 reads=["xbf0"], writes=["OUT"], dma=True, acc=True)
        for c in range(KCH):
            xk = f"xln1:{c}"
            s.op("sync", lambda e, c=c, r0=r0: e.dma_start(
                out=a_xln1[:, c * STT:(c + 1) * STT],
                in_=d_xbf[r0:r0 + STT, c * 128:(c + 1) * 128],
                transpose=True), reads=xkeys, writes=[xk], dma=True,
                stream=f"xfm{c % 4}")
            s.op("scalar", lambda e, c=c: e.activation(
                out=a_tmp1.bitcast(BF16)[:, 0:STT],
                in_=a_xln1[:, c * STT:(c + 1) * STT], func=AF.Square),
                reads=[xk], writes=["a_tmp1"])
            for j0, j1 in ((0, 512), (512, 640)):
                half = "a" if j0 == 0 else "b"
                s.op("tensor", lambda e, c=c, j0=j0, j1=j1: e.matmul(
                    out=pp[2][0:1, j0:j1], lhsT=k_onesb,
                    rhs=a_xln1[:, c * STT + j0:c * STT + j1],
                    start=(c == 0), stop=(c == KCH - 1)),
                    reads=[f"xln1:{c}", "k_onesb"],
                    writes=[PK(2, half)] if c in (0, KCH - 1) else [],
                    acc=(c == KCH - 1))
                s.op("tensor", lambda e, c=c, j0=j0, j1=j1: e.matmul(
                    out=pp[3][0:1, j0:j1], lhsT=k_onesb,
                    rhs=a_tmp1.bitcast(BF16)[:, j0:j1],
                    start=(c == 0), stop=(c == KCH - 1)),
                    reads=["a_tmp1", "k_onesb"],
                    writes=[PK(3, half)] if c in (0, KCH - 1) else [],
                    acc=(c == KCH - 1))
        pump(8)
        ln_finish(STT, pp[2][0:1, 0:STT], [PK(2, "a"), PK(2, "b")],
                  pp[3][0:1, 0:STT], [PK(3, "a"), PK(3, "b")],
                  [(pp[2][:, 0:512], PK(2, "a"), 0, 512),
                   (pp[2][:, 512:640], PK(2, "b"), 512, 640)],
                  [(pp[3][:, 0:512], PK(3, "a"), 0, 512),
                   (pp[3][:, 512:640], PK(3, "b"), 512, 640)])
        for c in range(KCH):
            s.op("vector", lambda e, c=c: e.tensor_tensor(
                out=a_tmp2, in0=a_xln1[:, c * STT:(c + 1) * STT],
                in1=pp[2][:, 0:STT], op=ALU.subtract),
                reads=[f"xln1:{c}", PK(2, "a"), PK(2, "b")], writes=["a_tmp2"])
            s.op("vector", lambda e, c=c: e.tensor_tensor(
                out=a_xln1[:, c * STT:(c + 1) * STT], in0=a_tmp2,
                in1=pp[3][:, 0:STT], op=ALU.mult),
                reads=["a_tmp2", PK(3, "a"), PK(3, "b")], writes=[f"xln1:{c}"])
        pump(8)
        xlnkeys = [f"xln1:{c}" for c in range(KCH)]

        if debug and st == 0:
            s.op("gpsimd", lambda e: e.dma_start(out=taps["t_xln1"], in_=a_xln1),
                 reads=xlnkeys, writes=["OUT"], dma=True, acc=True)
        if debug and st == 1:
            s.op("gpsimd", lambda e: e.dma_start(out=taps["t_xln1b"], in_=a_xln1),
                 reads=xlnkeys, writes=["OUT"], dma=True, acc=True)

        CHB = [(pp[3], 0, PK(3, "a")), (pp[3], 512, PK(3, "b")),
               (pp[2], 0, PK(2, "a")), (pp[2], 512, PK(2, "b"))]
        qv = d_wqkvbf.rearrange("(c p) f -> p c f", p=128)

        def emit_loads(h):
            if h >= NH:
                return
            wqk, wqkk = a_wqk[h % 2], f"awqk{h % 2}"
            wv2 = wqk.rearrange("p (c f) -> p c f", c=KCH)
            s.op("sync", lambda e, h=h, wv2=wv2: e.dma_start(
                out=wv2, in_=qv[:, :, h * 384:h * 384 + 256]),
                reads=["wqkvbf"], writes=[wqkk], dma=True,
                stream=f"wqk{h % 2}")
            if h % 2 == 0:
                hg = h // 2
                h0 = hg * 2
                vb = a_wv[hg % 2]
                vbk = f"awv{hg % 2}"
                wvv = vb.rearrange("p (c f) -> p c f", c=KCH)
                for hh in range(2):
                    s.op("sync", lambda e, hh=hh, wvv=wvv, h0=h0: e.dma_start(
                        out=wvv[:, :, hh * 128:(hh + 1) * 128],
                        in_=qv[:, :, (h0 + hh) * 384 + 256:(h0 + hh) * 384 + 384]),
                        reads=["wqkvbf"], writes=[vbk], dma=True, acc=(hh == 1),
                        stream=f"wv{hg % 2}")

        def emit_V(hg):
            vbuf = a_v[hg % 2]
            vk = f"av{hg % 2}"
            wv = a_wv[hg % 2]
            wvk = f"awv{hg % 2}"
            for tt in range(NTT):
                par = tt % 2
                pv = pp[0][:, 0:256] if par == 0 else pp[0][:, 512:768]
                pvk = PK(0, "a" if par == 0 else "b")
                for kc in range(KCH):
                    s.op("tensor", lambda e, tt=tt, kc=kc, pv=pv, wv=wv: e.matmul(
                        out=pv,
                        lhsT=a_xln1[:, kc * STT + tt * 128:kc * STT + (tt + 1) * 128],
                        rhs=wv[:, kc * 256:(kc + 1) * 256],
                        start=(kc == 0), stop=(kc == KCH - 1)),
                        reads=xlnkeys + [wvk],
                        writes=[pvk] if kc in (0, KCH - 1) else [],
                        acc=(kc == KCH - 1))
                s.op("scalar", lambda e, tt=tt, pv=pv, vbuf=vbuf: e.copy(
                    out=vbuf[:, tt * 256:(tt + 1) * 256], in_=pv),
                    reads=[pvk], writes=[f"{vk}:{tt}"])
            if debug and st == 1 and hg == 0:
                s.op("gpsimd", lambda e, vbuf=vbuf: e.dma_start(
                    out=taps["t_vb"], in_=vbuf),
                    reads=[f"{vk}:{t}" for t in range(NTT)], writes=["OUT"],
                    dma=True, acc=True)

        def emit_QKrope(h):
            wqk, wqkk = a_wqk[h % 2], f"awqk{h % 2}"
            for kc in range(KCH):
                s.op("tensor", lambda e, kc=kc, wqk=wqk: e.matmul(
                    out=pp[0][:, 0:512],
                    lhsT=wqk[:, kc * 256:kc * 256 + 128],
                    rhs=a_xln1[:, kc * STT + 128:kc * STT + STT],
                    start=(kc == 0), stop=(kc == KCH - 1)),
                    reads=xlnkeys + [wqkk],
                    writes=[PK(0, "a")] if kc in (0, KCH - 1) else [],
                    acc=(kc == KCH - 1))
            s.op("scalar", lambda e: e.copy(out=a_qraw, in_=pp[0][:, 0:512]),
                 reads=[PK(0, "a")], writes=["a_qraw"])
            for kc in range(KCH):
                for j0, j1 in ((0, 512), (512, 640)):
                    half = "a" if j0 == 0 else "b"
                    s.op("tensor", lambda e, kc=kc, j0=j0, j1=j1, wqk=wqk: e.matmul(
                        out=pp[1][:, j0:j1],
                        lhsT=wqk[:, kc * 256 + 128:kc * 256 + 256],
                        rhs=a_xln1[:, kc * STT + j0:kc * STT + j1],
                        start=(kc == 0), stop=(kc == KCH - 1)),
                        reads=xlnkeys + [wqkk],
                        writes=[PK(1, half)] if kc in (0, KCH - 1) else [],
                        acc=(kc == KCH - 1))
            s.op("scalar", lambda e: e.copy(out=a_kraw[:, 0:512],
                                            in_=pp[1][:, 0:512]),
                 reads=[PK(1, "a")], writes=["a_kraw"])
            s.op("scalar", lambda e: e.copy(out=a_kraw[:, 512:640],
                                            in_=pp[1][:, 512:640]),
                 reads=[PK(1, "b")], writes=["a_kraw"], acc=True)
            s.op("tensor", lambda e: e.matmul(
                out=pp[0][:, 512:1024], lhsT=k_rt, rhs=a_qraw,
                start=True, stop=True),
                reads=["a_qraw", "k_rt"], writes=[PK(0, "b")])
            for j0, j1 in ((0, 512), (512, 640)):
                half = "a" if j0 == 0 else "b"
                s.op("tensor", lambda e, j0=j0, j1=j1: e.matmul(
                    out=pp[1][:, j0:j1], lhsT=k_rt, rhs=a_kraw[:, j0:j1],
                    start=True, stop=True),
                    reads=["a_kraw", "k_rt"], writes=[PK(1, half)])
            cq = k_cos[:, 128 + r0:128 + r0 + 512]
            sq_ = k_sin[:, 128 + r0:128 + r0 + 512]
            ck = k_cos[:, r0:r0 + STT]
            sk = k_sin[:, r0:r0 + STT]
            s.op("gpsimd", lambda e, cq=cq: e.tensor_tensor(
                out=a_tmp1[:, 0:512], in0=a_qraw, in1=cq, op=ALU.mult),
                reads=["a_qraw", "k_cos"], writes=["a_tmp1"])
            s.op("vector", lambda e, sq_=sq_: e.tensor_tensor(
                out=a_tmp2[:, 0:512], in0=pp[0][:, 512:1024], in1=sq_,
                op=ALU.mult),
                reads=[PK(0, "b"), "k_sin"], writes=["a_tmp2"])
            s.op("vector", lambda e: e.tensor_tensor(
                out=a_q, in0=a_tmp1[:, 0:512], in1=a_tmp2[:, 0:512],
                op=ALU.add),
                reads=["a_tmp1", "a_tmp2"], writes=["a_q"])
            s.op("gpsimd", lambda e, ck=ck: e.tensor_tensor(
                out=a_tmp1[:, 0:STT], in0=a_kraw, in1=ck, op=ALU.mult),
                reads=["a_kraw", "k_cos"], writes=["a_tmp1"])
            for j0, j1 in ((0, 512), (512, 640)):
                half = "a" if j0 == 0 else "b"
                s.op("vector", lambda e, j0=j0, j1=j1, sk=sk: e.tensor_tensor(
                    out=a_tmp2[:, j0:j1], in0=pp[1][:, j0:j1],
                    in1=sk[:, j0:j1], op=ALU.mult),
                    reads=[PK(1, half), "k_sin"], writes=["a_tmp2"],
                    acc=(j0 != 0))
            s.op("vector", lambda e: e.tensor_tensor(
                out=a_k, in0=a_tmp1[:, 0:STT], in1=a_tmp2[:, 0:STT],
                op=ALU.add),
                reads=["a_tmp1", "a_tmp2"], writes=["a_k"])
            if debug and st == 0 and h == 0:
                s.op("gpsimd", lambda e: e.dma_start(out=taps["t_q"], in_=a_q),
                     reads=["a_q"], writes=["OUT"], dma=True, acc=True)
                s.op("gpsimd", lambda e: e.dma_start(out=taps["t_k"], in_=a_k),
                     reads=["a_k"], writes=["OUT"], dma=True, acc=True)
            if debug and st == 1 and h == 0:
                s.op("gpsimd", lambda e: e.dma_start(out=taps["t_qb"], in_=a_q),
                     reads=["a_q"], writes=["OUT"], dma=True, acc=True)
                s.op("gpsimd", lambda e: e.dma_start(out=taps["t_kb"], in_=a_k),
                     reads=["a_k"], writes=["OUT"], dma=True, acc=True)

        def emit_A(h):
            for qt in range(NQT):
                ppx, off, chk = CHB[qt]
                pst = ppx[:, off:off + 256]
                atr, at = a_atr[qt], a_at[qt]
                for ksub in range(2):
                    ktt = qt + ksub
                    s.op("tensor", lambda e, ksub=ksub, ktt=ktt, pst=pst, qt=qt: e.matmul(
                        out=pst[:, ksub * 128:(ksub + 1) * 128],
                        lhsT=a_k[:, ktt * 128:(ktt + 1) * 128],
                        rhs=a_q[:, qt * 128:(qt + 1) * 128],
                        start=(ksub == 0), stop=(ksub == 1)),
                        reads=["a_k", "a_q"],
                        writes=[chk], acc=(ksub == 1))
                s.op("scalar", lambda e, pst=pst, atr=atr: e.activation(
                    out=atr, in_=pst, func=AF.Exp, scale=C_SMS),
                    reads=[chk, "k_consts"], writes=[f"atr{qt}"])
                mask = k_mtf if (st == 0 and qt == 0) else k_mtr
                mkey = "k_mtf" if (st == 0 and qt == 0) else "k_mtr"
                s.op("gpsimd", lambda e, atr=atr, at=at, mask=mask: e.tensor_tensor(
                    out=at, in0=atr, in1=mask, op=ALU.mult),
                    reads=[f"atr{qt}", mkey], writes=[f"at{qt}"])
                if debug and st == 0 and h == 0 and qt == 0:
                    s.op("gpsimd", lambda e, at=at: e.dma_start(
                        out=taps["t_at"], in_=at),
                        reads=[f"at{qt}"], writes=["OUT"], dma=True, acc=True)

        def emit_B(h):
            hh = h % 2
            vbuf = a_v[(h // 2) % 2]
            vk = f"av{(h // 2) % 2}"
            for qt in range(NQT):
                ppx, off, chk = CHB[qt]
                prs = ppx[0:1, off + 256:off + 384]
                at = a_at[qt]
                for ksub in range(2):
                    s.op("tensor", lambda e, ksub=ksub, at=at, prs=prs: e.matmul(
                        out=prs, lhsT=k_onesb,
                        rhs=at[:, ksub * 128:(ksub + 1) * 128],
                        start=(ksub == 0), stop=(ksub == 1)),
                        reads=[f"at{qt}", "k_onesb"],
                        writes=[chk], acc=True)
            for qt in range(NQT):
                ppx, off, chk = CHB[qt]
                prs = ppx[0:1, off + 256:off + 384]
                rrrow = r_rr[0:1, qt * 128:qt * 128 + 128]
                s.op("vector", lambda e, prs=prs, rrrow=rrrow: e.reciprocal(
                    out=rrrow, in_=prs),
                    reads=[chk], writes=[f"rr{qt}"])
            for qt in range(NQT):
                ppx, off, chk = CHB[qt]
                prrb = ppx[:, off + 384:off + 512]
                rrrow = r_rr[0:1, qt * 128:qt * 128 + 128]
                s.op("tensor", lambda e, rrrow=rrrow, prrb=prrb: e.matmul(
                    out=prrb, lhsT=k_onesr, rhs=rrrow, start=True, stop=True),
                    reads=[f"rr{qt}", "k_onesr"],
                    writes=[chk], acc=True)
            for qt in range(NQT):
                ppx, off, chk = CHB[qt]
                prrb = ppx[:, off + 384:off + 512]
                s.op("scalar", lambda e, prrb=prrb, rrb=a_rrb[qt]: e.copy(
                    out=rrb, in_=prrb),
                    reads=[chk], writes=[f"rrb{qt}"])
            for qt in range(NQT):
                ppx, off, chk = CHB[qt]
                pav = ppx[:, off:off + 128]
                at = a_at[qt]
                for ksub in range(2):
                    ktt = qt + ksub
                    s.op("tensor", lambda e, ksub=ksub, ktt=ktt, at=at, pav=pav, hh=hh, vbuf=vbuf: e.matmul(
                        out=pav,
                        lhsT=vbuf[:, ktt * 256 + hh * 128:ktt * 256 + hh * 128 + 128],
                        rhs=at[:, ksub * 128:(ksub + 1) * 128],
                        start=(ksub == 0), stop=(ksub == 1)),
                        reads=[f"at{qt}", f"{vk}:{ktt}"],
                        writes=[chk], acc=True)
            for qt in range(NQT):
                ppx, off, chk = CHB[qt]
                pav = ppx[:, off:off + 128]
                col = st * 512 + qt * 128
                s.op("vector", lambda e, pav=pav, rrb=a_rrb[qt], h=h, col=col: e.tensor_tensor(
                    out=a_attn[:, h * TOK + col:h * TOK + col + 128],
                    in0=pav, in1=rrb, op=ALU.mult),
                    reads=[chk, f"rrb{qt}"], writes=[f"attn:{h}"], acc=True)

        emit_loads(0)
        emit_loads(1)
        prev = None
        for h in range(NH):
            if prev is not None:
                emit_A(prev)
            if h % 2 == 0:
                emit_V(h // 2)
            emit_QKrope(h)
            emit_loads(h + 2)
            if prev is not None:
                emit_B(prev)
                pump(6)
            prev = h
        emit_A(prev)
        emit_B(prev)
        pump(6)

    attnkeys = [f"attn:{h}" for h in range(NH)]
    if debug:
        s.op("gpsimd", lambda e: e.dma_start(out=taps["t_attn"], in_=a_attn),
             reads=attnkeys, writes=["OUT"], dma=True, acc=True)

    pump(10 ** 9)

    if debug:
        s.op("gpsimd", lambda e: e.dma_start(out=taps["t_scl"],
                                              in_=r_s.broadcast_to([128, 16])
                                              if False else r_s),
             reads=["r_s"], writes=["OUT"], dma=True, acc=True)
        s.op("gpsimd", lambda e: e.dma_start(out=taps["t_wut"], in_=d_wut[0:256, :]),
             reads=["tern_u"], writes=["OUT"], dma=True, acc=True)
        s.op("gpsimd", lambda e: e.dma_start(out=taps["t_wdt"], in_=d_wdt[0:256, :]),
             reads=["tern_d"], writes=["OUT"], dma=True, acc=True)

    att_work_keys = (["a_tmp1", "a_tmp2", "a_q", "a_qraw",
                      "a_k", "a_kraw", "awv0", "awv1", "awqk0", "awqk1"]
                     + [f"atr{q}" for q in range(4)] + [f"at{q}" for q in range(4)]
                     + [f"rrb{q}" for q in range(4)]
                     + [f"av{p}:{t}" for p in range(2) for t in range(NTT)]
                     + xlnkeys
                     + [f"th{sg}_{mn}" for sg in ("p", "n") for mn in ("u", "g", "d")]
                     + ["ttn0", "ttn1"])
    s.alias(["c_ybf", "c_wout", "c_sq", "c_t", "cz10", "cz11"],
            att_work_keys)
    s.alias([f"ybf:{fc}" for fc in range(KCH)], att_work_keys)

    for ec in range(KCH):
        s.op("sync", lambda e, ec=ec: e.dma_start(
            out=c_wout[:, ec * 2048:(ec + 1) * 2048],
            in_=d_woutbf[ec * 128:(ec + 1) * 128, :]),
            reads=["woutbf"], writes=["c_wout"], dma=True, acc=(ec > 0),
            stream="wout")
    for tq in range(4):
        tq0 = tq * 512
        for fc in range(KCH):
            par = fc % 2
            py = pp[par][:, 0:512]
            pyk = PK(par, "a")
            for ec in range(KCH):
                s.op("tensor", lambda e, ec=ec, fc=fc, py=py, tq0=tq0: e.matmul(
                    out=py, lhsT=c_wout[:, ec * 2048 + fc * 128:ec * 2048 + (fc + 1) * 128],
                    rhs=a_attn[:, ec * TOK + tq0:ec * TOK + tq0 + 512],
                    start=(ec == 0), stop=(ec == KCH - 1)),
                    reads=attnkeys + ["c_wout"],
                    writes=[pyk] if ec in (0, KCH - 1) else [],
                    acc=(ec == KCH - 1))
            s.op("vector", lambda e, py=py, fc=fc: e.tensor_scalar(
                out=c_ybf[:, fc * 512:(fc + 1) * 512], in0=py,
                scalar1=k_bout[:, fc:fc + 1], scalar2=None, op0=ALU.add),
                reads=[pyk, "k_bout"], writes=[f"ybf:{fc}"])
            pump(3)
        for fc in range(KCH):
            s.op("scalar", lambda e, fc=fc: e.activation(
                out=c_sq, in_=c_ybf[:, fc * 512:(fc + 1) * 512], func=AF.Square),
                reads=[f"ybf:{fc}"], writes=["c_sq"])
            s.op("tensor", lambda e, fc=fc: e.matmul(
                out=pp[2][0:1, 0:512], lhsT=k_onesb,
                rhs=c_ybf[:, fc * 512:(fc + 1) * 512],
                start=(fc == 0), stop=(fc == KCH - 1)),
                reads=[f"ybf:{fc}", "k_onesb"],
                writes=[PK(2, "a")] if fc in (0, KCH - 1) else [],
                acc=(fc == KCH - 1))
            s.op("tensor", lambda e, fc=fc: e.matmul(
                out=pp[2][0:1, 512:1024], lhsT=k_onesb, rhs=c_sq,
                start=(fc == 0), stop=(fc == KCH - 1)),
                reads=["c_sq", "k_onesb"],
                writes=[PK(2, "b")] if fc in (0, KCH - 1) else [],
                acc=(fc == KCH - 1))
        ln_finish(512, pp[2][0:1, 0:512], [PK(2, "a")],
                  pp[2][0:1, 512:1024], [PK(2, "b")],
                  [(pp[3][:, 0:512], PK(3, "a"), 0, 512)],
                  [(pp[3][:, 512:1024], PK(3, "b"), 0, 512)])
        for fc in range(KCH):
            zb, zbk = c_z1[fc % 2], f"cz1{fc % 2}"
            s.op("vector", lambda e, fc=fc: e.tensor_tensor(
                out=c_t, in0=c_ybf[:, fc * 512:(fc + 1) * 512],
                in1=pp[3][:, 0:512], op=ALU.subtract),
                reads=[f"ybf:{fc}", PK(3, "a")], writes=["c_t"])
            s.op("vector", lambda e, zb=zb: e.tensor_tensor(
                out=zb, in0=c_t, in1=pp[3][:, 512:1024], op=ALU.mult),
                reads=["c_t", PK(3, "b")], writes=[zbk])
            s.op("gpsimd", lambda e, fc=fc, tq0=tq0, zb=zb: e.dma_start(
                out=d_z1[fc * 128:(fc + 1) * 128, tq0:tq0 + 512], in_=zb),
                reads=[zbk], writes=["z1s"], dma=True, acc=True, stream="z1w")

    if debug:
        s.op("gpsimd", lambda e: e.dma_start(out=taps["t_z1"], in_=d_z1),
             reads=["z1s"], writes=["OUT"], dma=True, acc=True)

    c_keys = (["c_ybf", "c_wout", "c_sq", "c_t", "cz10", "cz11"]
              + [f"ybf:{fc}" for fc in range(KCH)]
              + ["t_s1", "ttn0", "ttn1", "twch0", "twch1"] + attnkeys)
    ffn_first = (["fc10", "fc11", "fc20", "fc21", "f_t", "f_wu", "f_wg", "f_wd"]
                 + [f"xn:{c}" for c in range(KCH)]
                 + [f"hid:{ec}" for ec in range(EB)]
                 + [f"facc:{dc}" for dc in range(KCH)])
    s.alias(ffn_first, c_keys)


    for hf in range(2):
        hr0 = HALO + hf * HT
        hx = [f"xbf{i}" for i, (q0, q1) in enumerate(xr)
              if q0 < hr0 + HT and q1 > hr0]
        if hf == 1:
            s.alias([f"xn:{c}" for c in range(KCH)],
                    [f"zsum:{dc}" for dc in range(KCH)])
            s.alias(["f_wu", "f_wg", "f_wd", "fc10", "fc11", "fc20", "fc21",
                     "f_t"],
                    ["zsT0", "zsT1", "ex0", "ex1", "eout0", "eout1"]
                    + [f"zsum:{dc}" for dc in range(KCH)])
        for c in range(KCH):
            xc, xck = f_c1[c % 2], f"fc1{c % 2}"
            zc, zck = f_c2[c % 2], f"fc2{c % 2}"
            s.op("sync", lambda e, c=c, xc=xc, hr0=hr0: e.dma_start(
                out=xc, in_=d_xbf[hr0:hr0 + HT, c * 128:(c + 1) * 128],
                transpose=True), reads=hx, writes=[xck], dma=True,
                stream=f"xfm{c % 4}")
            s.op("gpsimd", lambda e, c=c, zc=zc, hf=hf: e.dma_start(
                out=zc, in_=d_z1[c * 128:(c + 1) * 128, hf * HT:hf * HT + HT]),
                reads=["z1s"], writes=[zck], dma=True, stream=f"z1r{c % 2}")
            s.op("vector", lambda e, c=c, xc=xc, zc=zc: e.tensor_tensor(
                out=f_xln2[:, c * HT:(c + 1) * HT], in0=xc, in1=zc, op=ALU.add),
                reads=[xck, zck], writes=[f"xn:{c}"])
            s.op("scalar", lambda e, c=c: e.activation(
                out=f_t.bitcast(BF16)[:, 0:HT],
                in_=f_xln2[:, c * HT:(c + 1) * HT], func=AF.Square),
                reads=[f"xn:{c}"], writes=["f_t"])
            for j0, j1 in ((0, 512), (512, 1024)):
                half = "a" if j0 == 0 else "b"
                s.op("tensor", lambda e, c=c, j0=j0, j1=j1: e.matmul(
                    out=pp[2][0:1, j0:j1], lhsT=k_onesb,
                    rhs=f_xln2[:, c * HT + j0:c * HT + j1],
                    start=(c == 0), stop=(c == KCH - 1)),
                    reads=[f"xn:{c}", "k_onesb"],
                    writes=[PK(2, half)] if c in (0, KCH - 1) else [],
                    acc=(c == KCH - 1))
                s.op("tensor", lambda e, c=c, j0=j0, j1=j1: e.matmul(
                    out=pp[3][0:1, j0:j1], lhsT=k_onesb,
                    rhs=f_t.bitcast(BF16)[:, j0:j1],
                    start=(c == 0), stop=(c == KCH - 1)),
                    reads=["f_t", "k_onesb"],
                    writes=[PK(3, half)] if c in (0, KCH - 1) else [],
                    acc=(c == KCH - 1))
        ln_finish(HT, pp[2][0:1, 0:HT], [PK(2, "a"), PK(2, "b")],
                  pp[3][0:1, 0:HT], [PK(3, "a"), PK(3, "b")],
                  [(pp[0][:, 0:512], PK(0, "a"), 0, 512),
                   (pp[0][:, 512:1024], PK(0, "b"), 512, 1024)],
                  [(pp[1][:, 0:512], PK(1, "a"), 0, 512),
                   (pp[1][:, 512:1024], PK(1, "b"), 512, 1024)])
        for c in range(KCH):
            s.op("vector", lambda e, c=c: e.tensor_tensor(
                out=f_t, in0=f_xln2[:, c * HT:(c + 1) * HT],
                in1=pp[0][:, 0:HT], op=ALU.subtract),
                reads=[f"xn:{c}", PK(0, "a"), PK(0, "b")], writes=["f_t"])
            s.op("vector", lambda e, c=c: e.tensor_tensor(
                out=f_xln2[:, c * HT:(c + 1) * HT], in0=f_t,
                in1=pp[1][:, 0:HT], op=ALU.mult),
                reads=["f_t", PK(1, "a"), PK(1, "b")], writes=[f"xln2:{c}"])
        xln2keys = [f"xln2:{c}" for c in range(KCH)]
        if debug and hf == 0:
            s.op("gpsimd", lambda e: e.dma_start(out=taps["t_xln2"], in_=f_xln2),
                 reads=xln2keys, writes=["OUT"], dma=True, acc=True)

        NB = ECH // EB

        def ld_wu(bb):
            if bb >= NB:
                return
            e0 = bb * EB
            for kd in range(KCH):
                s.op("sync", lambda e, kd=kd, e0=e0: e.dma_start(
                    out=f_wu[:, kd * EB * 128:(kd + 1) * EB * 128],
                    in_=d_wut[e0 * 128:e0 * 128 + EB * 128,
                              kd * 128:(kd + 1) * 128], transpose=True),
                    reads=["tern_u"], writes=["f_wu"], dma=True, acc=(kd > 0),
                    stream="fwu")

        def ld_wg(bb):
            if bb >= NB:
                return
            e0 = bb * EB
            for kd in range(KCH):
                s.op("sync", lambda e, kd=kd, e0=e0: e.dma_start(
                    out=f_wg[:, kd * EB * 128:(kd + 1) * EB * 128],
                    in_=d_wgt[e0 * 128:e0 * 128 + EB * 128,
                              kd * 128:(kd + 1) * 128], transpose=True),
                    reads=["tern_g"], writes=["f_wg"], dma=True, acc=(kd > 0),
                    stream="fwg")

        def ld_wd(bb):
            if bb >= NB:
                return
            e0 = bb * EB
            for ec in range(EB):
                s.op("sync", lambda e, ec=ec, e0=e0: e.dma_start(
                    out=f_wd[:, ec * D:(ec + 1) * D],
                    in_=d_wdt[:, (e0 + ec) * 128:(e0 + ec + 1) * 128],
                    transpose=True),
                    reads=["tern_d"], writes=["f_wd"], dma=True, acc=(ec > 0),
                    stream="fwd")

        ld_wu(0)
        ld_wg(0)
        ld_wd(0)
        for bb in range(NB):
            for ec in range(EB):
                for jt in range(2):
                    par = jt
                    pu = pp[0][:, par * 512:(par + 1) * 512]
                    puk = PK(0, "a" if par == 0 else "b")
                    for kd in range(KCH):
                        s.op("tensor", lambda e, ec=ec, jt=jt, kd=kd, pu=pu: e.matmul(
                            out=pu,
                            lhsT=f_wu[:, kd * EB * 128 + ec * 128:kd * EB * 128 + (ec + 1) * 128],
                            rhs=f_xln2[:, kd * HT + jt * 512:kd * HT + (jt + 1) * 512],
                            start=(kd == 0), stop=(kd == KCH - 1)),
                            reads=xln2keys + ["f_wu"],
                            writes=[puk] if kd in (0, KCH - 1) else [],
                            acc=(kd == KCH - 1))
                    s.op("scalar", lambda e, ec=ec, jt=jt, pu=pu: e.activation(
                        out=f_hid[:, ec * HT + jt * 512:ec * HT + (jt + 1) * 512],
                        in_=pu, func=AF.Silu, scale=k_scl[:, 3:4]),
                        reads=[puk, "sclp_u"], writes=[f"hid:{ec}"],
                        acc=(jt == 1))
            ld_wu(bb + 1)
            for ec in range(EB):
                for jt in range(2):
                    par = jt
                    pg = pp[1][:, par * 512:(par + 1) * 512]
                    pgk = PK(1, "a" if par == 0 else "b")
                    for kd in range(KCH):
                        s.op("tensor", lambda e, ec=ec, jt=jt, kd=kd, pg=pg: e.matmul(
                            out=pg,
                            lhsT=f_wg[:, kd * EB * 128 + ec * 128:kd * EB * 128 + (ec + 1) * 128],
                            rhs=f_xln2[:, kd * HT + jt * 512:kd * HT + (jt + 1) * 512],
                            start=(kd == 0), stop=(kd == KCH - 1)),
                            reads=xln2keys + ["f_wg"],
                            writes=[pgk] if kd in (0, KCH - 1) else [],
                            acc=(kd == KCH - 1))
                    s.op("vector", lambda e, ec=ec, jt=jt, pg=pg: e.tensor_tensor(
                        out=f_hid[:, ec * HT + jt * 512:ec * HT + (jt + 1) * 512],
                        in0=pg,
                        in1=f_hid[:, ec * HT + jt * 512:ec * HT + (jt + 1) * 512],
                        op=ALU.mult),
                        reads=[pgk, f"hid:{ec}"], writes=[f"hid:{ec}"],
                        acc=(jt == 1))
            ld_wg(bb + 1)
            hidkeys = [f"hid:{ec}" for ec in range(EB)]
            for dc in range(KCH):
                ppx = pp[2] if dc % 2 == 0 else pp[3]
                pk0 = PK(2 + dc % 2, "a")
                pk1 = PK(2 + dc % 2, "b")
                for tti in range(2):
                    pd = ppx[:, tti * 512:(tti + 1) * 512]
                    pdk = pk0 if tti == 0 else pk1
                    for ec in range(EB):
                        s.op("tensor", lambda e, dc=dc, tti=tti, ec=ec, pd=pd: e.matmul(
                            out=pd,
                            lhsT=f_wd[:, ec * D + dc * 128:ec * D + (dc + 1) * 128],
                            rhs=f_hid[:, ec * HT + tti * 512:ec * HT + (tti + 1) * 512],
                            start=(ec == 0), stop=(ec == EB - 1)),
                            reads=hidkeys + ["f_wd"],
                            writes=[pdk] if ec in (0, EB - 1) else [],
                            acc=(ec == EB - 1))
                dst = f_acc[:, dc * HT:(dc + 1) * HT]
                if bb == 0:
                    s.op("vector", lambda e, dst=dst, ppx=ppx: e.tensor_copy(
                        out=dst, in_=ppx[:, 0:1024]),
                        reads=[pk0, pk1], writes=[f"facc:{dc}"])
                else:
                    s.op("vector", lambda e, dst=dst, ppx=ppx: e.tensor_tensor(
                        out=dst, in0=ppx[:, 0:1024], in1=dst, op=ALU.add),
                        reads=[pk0, pk1, f"facc:{dc}"],
                        writes=[f"facc:{dc}"])
            ld_wd(bb + 1)

        facckeys = [f"facc:{dc}" for dc in range(KCH)]
        if debug and hf == 0:
            s.op("gpsimd", lambda e: e.dma_start(
                out=taps["t_acc"], in_=f_acc[:, 0:2 * HT]),
                reads=facckeys, writes=["OUT"], dma=True, acc=True)

        for dc in range(KCH):
            cb, cbk = f_c1[dc % 2], f"fc1{dc % 2}"
            s.op("gpsimd", lambda e, dc=dc, cb=cb: e.tensor_copy(
                out=cb, in_=f_acc[:, dc * HT:(dc + 1) * HT]),
                reads=[f"facc:{dc}"], writes=[cbk])
            s.op("scalar", lambda e, cb=cb: e.activation(
                out=f_t.bitcast(BF16)[:, 0:HT], in_=cb, func=AF.Square),
                reads=[cbk], writes=["f_t"])
            for j0, j1 in ((0, 512), (512, 1024)):
                half = "a" if j0 == 0 else "b"
                s.op("tensor", lambda e, dc=dc, j0=j0, j1=j1, cb=cb: e.matmul(
                    out=pp[2][0:1, j0:j1], lhsT=k_onesb, rhs=cb[:, j0:j1],
                    start=(dc == 0), stop=(dc == KCH - 1)),
                    reads=[cbk, "k_onesb"],
                    writes=[PK(2, half)] if dc in (0, KCH - 1) else [],
                    acc=(dc == KCH - 1))
                s.op("tensor", lambda e, dc=dc, j0=j0, j1=j1: e.matmul(
                    out=pp[3][0:1, j0:j1], lhsT=k_onesb,
                    rhs=f_t.bitcast(BF16)[:, j0:j1],
                    start=(dc == 0), stop=(dc == KCH - 1)),
                    reads=["f_t", "k_onesb"],
                    writes=[PK(3, half)] if dc in (0, KCH - 1) else [],
                    acc=(dc == KCH - 1))
        ln_finish(HT, pp[2][0:1, 0:HT], [PK(2, "a"), PK(2, "b")],
                  pp[3][0:1, 0:HT], [PK(3, "a"), PK(3, "b")],
                  [(pp[0][:, 0:512], PK(0, "a"), 0, 512),
                   (pp[0][:, 512:1024], PK(0, "b"), 512, 1024)],
                  [(pp[1][:, 0:512], PK(1, "a"), 0, 512),
                   (pp[1][:, 512:1024], PK(1, "b"), 512, 1024)])
        s.alias([f"zsum:{dc}" for dc in range(KCH)], xln2keys)
        s.alias(["zsT0", "zsT1", "ex0", "ex1", "eout0", "eout1"],
                ["f_wu", "f_wg", "f_wd"] + [f"hid:{ec}" for ec in range(EB)])
        for dc in range(KCH):
            zc, zck = f_c2[dc % 2], f"fc2{dc % 2}"
            s.op("gpsimd", lambda e, dc=dc, zc=zc, hf=hf: e.dma_start(
                out=zc, in_=d_z1[dc * 128:(dc + 1) * 128, hf * HT:hf * HT + HT]),
                reads=["z1s"], writes=[zck], dma=True, stream=f"z1r{dc % 2}")
            s.op("vector", lambda e, dc=dc: e.tensor_tensor(
                out=f_t, in0=f_acc[:, dc * HT:(dc + 1) * HT],
                in1=pp[0][:, 0:HT], op=ALU.subtract),
                reads=[f"facc:{dc}", PK(0, "a"), PK(0, "b")],
                writes=["f_t"])
            s.op("vector", lambda e, dc=dc: e.tensor_tensor(
                out=f_t.bitcast(BF16)[:, 0:HT], in0=f_t, in1=pp[1][:, 0:HT],
                op=ALU.mult),
                reads=["f_t", PK(1, "a"), PK(1, "b")], writes=["f_t"])
            s.op("vector", lambda e, dc=dc, zc=zc: e.tensor_tensor(
                out=e_zsum[:, dc * HT:(dc + 1) * HT],
                in0=f_t.bitcast(BF16)[:, 0:HT], in1=zc, op=ALU.add),
                reads=["f_t", zck], writes=[f"zsum:{dc}"])
        for t8 in range(8):
            par = t8 % 2
            ptr = pp[par].bitcast(BF16)
            for dc in range(KCH):
                bank = dc // 8
                slot = dc % 8
                bkk = PK(par, "a" if bank == 0 else "b")
                s.op("tensor", lambda e, dc=dc, t8=t8, ptr=ptr, bank=bank, slot=slot: e.transpose(
                    out=ptr[:, bank * 1024 + slot * 128:bank * 1024 + (slot + 1) * 128],
                    in_=e_zsum[:, dc * HT + t8 * 128:dc * HT + (t8 + 1) * 128],
                    identity=k_idn),
                    reads=[f"zsum:{dc}", "k_idn"],
                    writes=[bkk], acc=(slot > 0))
            zt = e_zsT[par]
            for bank in range(2):
                bkk = PK(par, "a" if bank == 0 else "b")
                s.op("scalar", lambda e, zt=zt, ptr=ptr, bank=bank: e.copy(
                    out=zt[:, bank * 1024:(bank + 1) * 1024],
                    in_=ptr[:, bank * 1024:bank * 1024 + 1024]),
                    reads=[bkk], writes=[f"zsT{par}"], acc=(bank > 0))
            ex = e_x[par]
            s.op("gpsimd", lambda e, t8=t8, ex=ex, hf=hf: e.dma_start(
                out=ex,
                in_=d_x[HALO + hf * HT + t8 * 128:HALO + hf * HT + (t8 + 1) * 128, :]),
                writes=[f"ex{par}"], dma=True, stream=f"xld{par}")
            eo = e_out[par]
            s.op("vector", lambda e, ex=ex, zt=zt, eo=eo: e.tensor_tensor(
                out=eo, in0=ex, in1=zt, op=ALU.add),
                reads=[f"ex{par}", f"zsT{par}"], writes=[f"eout{par}"])
            s.op("gpsimd", lambda e, t8=t8, eo=eo, hf=hf: e.dma_start(
                out=d_out[hf * HT + t8 * 128:hf * HT + (t8 + 1) * 128, :], in_=eo),
                reads=[f"eout{par}"], writes=["OUT"], dma=True, acc=True,
                stream="outw")

    s.op("sync", lambda e: e.nop(), reads=["OUT"])

    with nc.Block() as block:
        s.finalize(block)

    return nc, taps, s


def _host_inputs(inputs):
    x = np.ascontiguousarray(np.asarray(inputs["L0_attn_x"], dtype=np.float32))
    wqkv = np.asarray(inputs["L0_attn_W_qkv"], dtype=np.float32)
    wout = np.ascontiguousarray(np.asarray(inputs["L0_attn_W_out"], dtype=np.float32))
    bout = np.asarray(inputs["L0_attn_b_out"], dtype=np.float32)
    wu = np.ascontiguousarray(np.asarray(inputs["L0_ffn_W_up"], dtype=np.float32))
    wg = np.ascontiguousarray(np.asarray(inputs["L0_ffn_W_gate"], dtype=np.float32))
    wd = np.ascontiguousarray(np.asarray(inputs["L0_ffn_W_down"], dtype=np.float32))

    wqkv_r = np.ascontiguousarray(
        wqkv.reshape(D, NH, HD, 3).transpose(0, 1, 3, 2).reshape(D, 3 * D))
    bout_fm = np.ascontiguousarray(bout.reshape(16, 128).T)

    inv_freq = 1.0 / (THETA_F ** (np.arange(0, HD, 2, dtype=np.float64) / HD))

    r = np.arange(128)
    m0_rest = (r[:, None] >= r[None, :]).astype(np.float32)
    m1 = (r[:, None] <= r[None, :]).astype(np.float32)
    mask_rest = np.ascontiguousarray(
        np.concatenate([m0_rest, m1], axis=1)).astype(ml_dtypes.bfloat16)
    mask_first = np.ascontiguousarray(
        np.concatenate([np.zeros((128, 128), np.float32), m1],
                       axis=1)).astype(ml_dtypes.bfloat16)

    R = np.zeros((128, 128), np.float32)
    ii = np.arange(0, 128, 2)
    R[ii, ii + 1] = -1.0
    R[ii + 1, ii] = 1.0
    RT = np.ascontiguousarray(R.T).astype(ml_dtypes.bfloat16)
    idn = np.eye(128, dtype=ml_dtypes.bfloat16)
    onesr = np.ones((1, 128), np.float32)
    onesc = np.ones((128, 1), np.float32)
    onesb = np.ones((128, 1), ml_dtypes.bfloat16)
    consts = np.tile(np.array([[EPS_LN, 1.0 / D, SM_SCALE, 0.5,
                                1.0 / (DE * D), EPS_Q, 1.0, 0.0]],
                              np.float32), (128, 1))

    in_maps = []
    for c in range(NCORES):
        bi, sc = c // 4, c % 4
        s0 = sc * TOK
        if sc == 0:
            xs = np.concatenate([np.zeros((HALO, D), np.float32),
                                 x[bi, 0:TOK]], axis=0)
        else:
            xs = x[bi, s0 - HALO:s0 + TOK]
        xs = np.ascontiguousarray(xs)
        pos = np.maximum(np.arange(s0 - HALO, s0 + TOK, dtype=np.float64), 0.0)
        ang = np.repeat(pos[:, None] * inv_freq[None, :], 2, axis=1)
        cosT = np.ascontiguousarray(np.cos(ang).T.astype(ml_dtypes.bfloat16))
        sinT = np.ascontiguousarray(np.sin(ang).T.astype(ml_dtypes.bfloat16))
        in_maps.append({
            "x": xs, "wqkv": wqkv_r, "wout": wout, "bout": bout_fm,
            "wu": wu, "wg": wg, "wd": wd,
            "cost": cosT, "sint": sinT,
            "maskf": mask_first if sc == 0 else mask_rest,
            "maskr": mask_rest,
            "rt": RT, "idn": idn, "onesr": onesr, "onesc": onesc,
            "onesb": onesb, "consts": consts,
        })
    return in_maps


_CACHED = {}


def kernel(**inputs):
    if "nc" not in _CACHED:
        _CACHED["nc"] = build_program(debug=False)[0]
    nc = _CACHED["nc"]
    in_maps = _host_inputs(inputs)
    res = run_bass_kernel_spmd(nc, in_maps, core_ids=list(range(NCORES)))
    out = np.empty((B, S, D), dtype=np.float32)
    for c in range(NCORES):
        bi, sc = c // 4, c % 4
        out[bi, sc * TOK:(sc + 1) * TOK, :] = res.results[c]["out"]
    return out


if __name__ == "__main__":
    nc, _, _s = build_program(debug=False)
    print("built ok; instructions:", nc.n_instructions())


# version 24
# speedup vs baseline: 1.0228x; 1.0193x over previous
import numpy as np
import ml_dtypes
import concourse.bass as bass
import concourse.mybir as mybir
from concourse.bass_utils import run_bass_kernel_spmd

F32 = mybir.dt.float32
BF16 = mybir.dt.bfloat16
AF = mybir.ActivationFunctionType
ALU = mybir.AluOpType
AX = mybir.AxisListType

B, S, D = 2, 8192, 2048
NH, HD = 16, 128
DE = 8192
NCORES = 8
TOK = 2048
HALO = 128
TT = TOK + HALO
NST = 4
STOWN = 512
STT = STOWN + HALO
NQT = 4
NTT = 5
HT = 1024
EB = 4
KCH = 16
ECH = DE // 128
NWCH = 128
SM_SCALE = float(1.0 / np.sqrt(HD))
EPS_LN = 1e-5
EPS_Q = 1e-8
THETA_F = 10000.0

SB_BASE = 16512
SB_TOP = SB_BASE + 212_863


def _bytes_pp(shape, dtype):
    n = 1
    for v in shape[1:]:
        n *= v
    return n * mybir.dt.size(dtype)


class SB:
    def __init__(self, nc):
        self.nc = nc
        self.off = SB_BASE
        self.peak = SB_BASE

    def alloc(self, name, shape, dtype, align=64):
        self.off = (self.off + align - 1) // align * align
        t = self.nc.alloc_sbuf_tensor_at(name, list(shape), dtype, offset=self.off)
        self.off += _bytes_pp(shape, dtype)
        self.peak = max(self.peak, self.off)
        assert self.off <= SB_TOP, f"SBUF overflow at {name}: {self.off}"
        return t.ap()

    def mark(self):
        return self.off

    def reset(self, mark):
        self.off = mark


class Sched:
    ENGS = ("sync", "tensor", "vector", "scalar", "gpsimd")
    DMAQ = ("sync", "scalar", "gpsimd")

    def __init__(self, nc):
        self.nc = nc
        self.q = {e: [] for e in self.ENGS}
        self.sem = {e: nc.alloc_semaphore(f"es_{e}") for e in self.ENGS}
        self.cnt = {e: 0 for e in self.ENGS}
        self.dstream = {}
        self.dq_of_sem = {}
        self.hw = {e: {} for e in self.ENGS}
        self.wr = {}
        self.rd = {}
        self.nwaits = 0
        self.log = []

    def op(self, eng, fn, reads=(), writes=(), dma=False, acc=False,
           stream=None):
        waits = {}

        def add(tok):
            sem, val = tok
            if eng == "tensor" and sem is self.sem["tensor"]:
                return
            k = id(sem)
            skey = self.dq_of_sem.get(k)
            if skey is not None:
                val = max(val, self.dstream[skey][1])
            if self.hw[eng].get(k, 0) >= val:
                return
            cur = waits.get(k)
            if cur is None or cur[1] < val:
                waits[k] = (sem, val)

        for r in reads:
            for t in self.wr.get(r, ()):
                add(t)
        for w in writes:
            for t in self.wr.get(w, ()):
                add(t)
            for t in self.rd.get(w, ()):
                add(t)
        wlist = list(waits.values())
        self.nwaits += len(wlist)
        for k, (sem, val) in waits.items():
            self.hw[eng][k] = val
        if dma:
            skey = (eng, stream or eng)
            ent = self.dstream.get(skey)
            if ent is None:
                ent = [self.nc.alloc_semaphore(f"ds_{skey[0]}_{skey[1]}"), 0]
                self.dstream[skey] = ent
                self.dq_of_sem[id(ent[0])] = skey
            ent[1] += 16
            tok = (ent[0], ent[1])
            inc = 16
        else:
            self.cnt[eng] += 1
            tok = (self.sem[eng], self.cnt[eng])
            inc = 1

        def emit(e, wlist=wlist, fn=fn, tok=tok, inc=inc):
            for sem, val in wlist:
                e.wait_ge(sem, val)
            fn(e).then_inc(tok[0], inc)

        self.q[eng].append(emit)
        self.log.append((eng, list(wlist), tok, list(reads), list(writes), dma))
        for w in writes:
            if acc:
                self.wr.setdefault(w, []).append(tok)
            else:
                self.wr[w] = [tok]
                self.rd[w] = []
        for r in reads:
            self.rd.setdefault(r, []).append(tok)
        return tok

    def alias(self, new_regions, old_regions):
        toks = []
        for r in old_regions:
            toks += list(self.wr.get(r, ()))
            toks += list(self.rd.get(r, ()))
        best = {}
        for sem, val in toks:
            k = id(sem)
            if k not in best or best[k][1] < val:
                best[k] = (sem, val)
        toks = list(best.values())
        for w in new_regions:
            self.wr[w] = list(toks)
            self.rd[w] = []

    def finalize(self, block):
        for e in self.ENGS:
            ops = self.q[e]

            def run(eng, ops=ops):
                for f in ops:
                    f(eng)

            getattr(block, e)(run)


def build_program(debug=False):
    nc = bass.Bass("TRN2", target_bir_lowering=False)

    d_x = nc.dram_tensor("x", [TT, D], F32, kind="ExternalInput").ap()
    d_wqkv = nc.dram_tensor("wqkv", [D, 3 * D], F32, kind="ExternalInput").ap()
    d_wout = nc.dram_tensor("wout", [D, D], F32, kind="ExternalInput").ap()
    d_bout = nc.dram_tensor("bout", [128, 16], F32, kind="ExternalInput").ap()
    d_wu = nc.dram_tensor("wu", [DE, D], F32, kind="ExternalInput").ap()
    d_wg = nc.dram_tensor("wg", [DE, D], F32, kind="ExternalInput").ap()
    d_wd = nc.dram_tensor("wd", [D, DE], F32, kind="ExternalInput").ap()
    d_cos = nc.dram_tensor("cost", [128, TT], BF16, kind="ExternalInput").ap()
    d_sin = nc.dram_tensor("sint", [128, TT], BF16, kind="ExternalInput").ap()
    d_mtf = nc.dram_tensor("maskf", [128, 256], BF16, kind="ExternalInput").ap()
    d_mtr = nc.dram_tensor("maskr", [128, 256], BF16, kind="ExternalInput").ap()
    d_rt = nc.dram_tensor("rt", [128, 128], BF16, kind="ExternalInput").ap()
    d_idn = nc.dram_tensor("idn", [128, 128], BF16, kind="ExternalInput").ap()
    d_onesr = nc.dram_tensor("onesr", [1, 128], F32, kind="ExternalInput").ap()
    d_onesc = nc.dram_tensor("onesc", [128, 1], F32, kind="ExternalInput").ap()
    d_onesb = nc.dram_tensor("onesb", [128, 1], BF16, kind="ExternalInput").ap()
    d_consts = nc.dram_tensor("consts", [128, 8], F32, kind="ExternalInput").ap()

    d_out = nc.dram_tensor("out", [TOK, D], F32, kind="ExternalOutput").ap()

    d_xbf = nc.dram_tensor("xbf", [TT, D], BF16, kind="Internal").ap()
    d_wqkvbf = nc.dram_tensor("wqkvbf", [D, 3 * D], BF16, kind="Internal").ap()
    d_woutbf = nc.dram_tensor("woutbf", [D, D], BF16, kind="Internal").ap()
    d_wut = nc.dram_tensor("wut", [DE, D], BF16, kind="Internal").ap()
    d_wgt = nc.dram_tensor("wgt", [DE, D], BF16, kind="Internal").ap()
    d_wdt = nc.dram_tensor("wdt", [D, DE], BF16, kind="Internal").ap()
    d_z1 = nc.dram_tensor("z1s", [D, TOK], BF16, kind="Internal").ap()

    taps = {}
    if debug:
        def tap(name, shape, dtype=BF16):
            taps[name] = nc.dram_tensor(name, list(shape), dtype,
                                        kind="ExternalOutput").ap()
        tap("t_xln1", [128, KCH * STT])
        tap("t_xbf", [640, D])
        tap("t_xln1b", [128, KCH * STT])
        tap("t_kb", [128, STT])
        tap("t_qb", [128, STOWN])
        tap("t_vb", [128, NTT * 256])
        tap("t_q", [128, STOWN])
        tap("t_k", [128, STT])
        tap("t_at", [128, 256])
        tap("t_attn", [128, KCH * TOK])
        tap("t_z1", [D, TOK])
        tap("t_scl", [1, 16], F32)
        tap("t_wut", [256, D])
        tap("t_wdt", [256, DE])
        tap("t_xln2", [128, KCH * HT])
        tap("t_acc", [128, 2 * HT], F32)

    sb = SB(nc)
    k_consts = sb.alloc("k_consts", [128, 8], F32)
    k_cos = sb.alloc("k_cos", [128, TT], BF16)
    k_sin = sb.alloc("k_sin", [128, TT], BF16)
    k_mtf = sb.alloc("k_mtf", [128, 256], BF16)
    k_mtr = sb.alloc("k_mtr", [128, 256], BF16)
    k_rt = sb.alloc("k_rt", [128, 128], BF16)
    k_idn = sb.alloc("k_idn", [128, 128], BF16)
    k_onesr = sb.alloc("k_onesr", [1, 128], F32)
    k_onesc = sb.alloc("k_onesc", [128, 1], F32)
    k_onesb = sb.alloc("k_onesb", [128, 1], BF16)
    k_bout = sb.alloc("k_bout", [128, 16], F32)
    k_scl = sb.alloc("k_scl", [128, 8], F32)
    k_accu = sb.alloc("k_accu", [128, 384], F32)
    k_part = sb.alloc("k_part", [128, 8], F32)
    r_a = sb.alloc("r_a", [1, TT], F32)
    r_b = sb.alloc("r_b", [1, TT], F32)
    r_rr = sb.alloc("r_rr", [1, 512], F32)
    r_s = sb.alloc("r_s", [1, 16], F32)
    pers_end = sb.mark()

    t_wch = [sb.alloc(f"t_wch{i}", [128, 1024], F32) for i in range(3)]
    t_wcht = [sb.alloc(f"t_wcht{i}", [128, 1024], F32) for i in range(3)]

    a_attn = sb.alloc("a_attn", [128, KCH * TOK], BF16)
    thp_mark = sb.mark()
    t_tn = [sb.alloc(f"t_tn{i}", [128, 1024], BF16) for i in range(2)]
    t_s1 = [sb.alloc(f"t_s1{i}", [128, 1024], BF16) for i in range(2)]
    t_abs = sb.alloc("t_abs", [128, 1024], BF16)

    att_mark = sb.mark()
    a_tmp1 = sb.alloc("a_tmp1", [128, STT], F32)
    a_tmp2 = sb.alloc("a_tmp2", [128, STT], F32)
    a_xln1 = sb.alloc("a_xln1", [128, KCH * STT], BF16)
    a_q = sb.alloc("a_q", [128, STOWN], BF16)
    a_qraw = sb.alloc("a_qraw", [128, STOWN], BF16)
    a_k = sb.alloc("a_k", [128, STT], BF16)
    a_kraw = sb.alloc("a_kraw", [128, STT], BF16)
    a_v = [sb.alloc(f"a_v{i}", [128, NTT * 256], BF16) for i in range(2)]
    a_wqk = [sb.alloc(f"a_wqk{i}", [128, KCH * 256], BF16) for i in range(2)]
    a_wv = [sb.alloc(f"a_wv{i}", [128, KCH * 256], BF16) for i in range(2)]
    a_atr = [sb.alloc(f"a_atr{i}", [128, 256], BF16) for i in range(4)]
    a_at = [sb.alloc(f"a_at{i}", [128, 256], BF16) for i in range(4)]
    a_rrb = [sb.alloc(f"a_rrb{i}", [128, 128], F32) for i in range(4)]
    att_end = sb.mark()

    sb.reset(thp_mark)
    c_ybf = sb.alloc("c_ybf", [128, KCH * 512], BF16)
    c_wout = sb.alloc("c_wout", [128, KCH * 2048], BF16)
    c_sq = sb.alloc("c_sq", [128, 512], BF16)
    c_t = sb.alloc("c_t", [128, 512], F32)
    c_z1 = [sb.alloc(f"c_z1{i}", [128, 512], BF16) for i in range(2)]
    sb.reset(att_mark)

    sb.reset(pers_end)
    f_xln2 = sb.alloc("f_xln2", [128, KCH * HT], BF16)
    ffn_w_mark = sb.mark()
    f_acc = sb.alloc("f_acc", [128, KCH * HT], F32)
    ffn_hid_mark = sb.mark()
    f_hid = sb.alloc("f_hid", [128, EB * HT], BF16)
    f_wu = sb.alloc("f_wu", [128, KCH * EB * 128], BF16)
    f_wg = sb.alloc("f_wg", [128, KCH * EB * 128], BF16)
    f_wd = sb.alloc("f_wd", [128, EB * D], BF16)
    f_c1 = [sb.alloc(f"f_c1{i}", [128, HT], BF16) for i in range(2)]
    f_c2 = [sb.alloc(f"f_c2{i}", [128, HT], BF16) for i in range(2)]
    f_t = sb.alloc("f_t", [128, HT], F32)
    ffn_end = sb.mark()

    sb.reset(pers_end)
    e_zsum = sb.alloc("e_zsum", [128, KCH * HT], BF16)
    sb.reset(ffn_hid_mark)
    e_zsT = [sb.alloc(f"e_zsT{i}", [128, D], F32) for i in range(2)]
    e_x = [sb.alloc(f"e_x{i}", [128, D], F32) for i in range(2)]
    e_out = [sb.alloc(f"e_out{i}", [128, D], F32) for i in range(2)]
    sb.reset(ffn_end)

    pp = [nc.alloc_psum_tensor(f"pp{i}", [128, 1024], F32).ap() for i in range(4)]

    def PK(i, half):
        return f"P{i}{half}"

    s = Sched(nc)

    C_EPS1 = k_consts[0:1, 0:1]
    C_SMS = k_consts[:, 2:3]

    def ld(dst, src, key):
        s.op("sync", lambda e, d=dst, sr=src: e.dma_start(out=d, in_=sr),
             writes=[key], dma=True, stream="const")

    ld(k_consts, d_consts, "k_consts")
    ld(k_cos, d_cos, "k_cos")
    ld(k_sin, d_sin, "k_sin")
    ld(k_mtf, d_mtf, "k_mtf")
    ld(k_mtr, d_mtr, "k_mtr")
    ld(k_rt, d_rt, "k_rt")
    ld(k_idn, d_idn, "k_idn")
    ld(k_onesr, d_onesr, "k_onesr")
    ld(k_onesc, d_onesc, "k_onesc")
    ld(k_onesb, d_onesb, "k_onesb")
    ld(k_bout, d_bout, "k_bout")

    xr = [(0, 640), (640, 1152), (1152, 1664), (1664, 2176)]
    for i, (q0, q1) in enumerate(xr):
        s.op("gpsimd", lambda e, q0=q0, q1=q1: e.dma_start(
            out=d_xbf[q0:q1, :], in_=d_x[q0:q1, :]), writes=[f"xbf{i}"],
            dma=True, stream="cast")
    s.op("gpsimd", lambda e: e.dma_start(out=d_wqkvbf, in_=d_wqkv),
         writes=["wqkvbf"], dma=True, stream="cast")
    s.op("gpsimd", lambda e: e.dma_start(out=d_woutbf, in_=d_wout),
         writes=["woutbf"], dma=True, stream="cast")

    def wchunk(dsrc, i):
        if dsrc is d_wd:
            rr, cc = i // 8, i % 8
        else:
            rr, cc = i // 2, i % 2
        return dsrc[rr * 128:(rr + 1) * 128, cc * 1024:(cc + 1) * 1024]

    def wchunk_dst(ddst, i, is_wd):
        if is_wd:
            rr, cc = i // 8, i % 8
        else:
            rr, cc = i // 2, i % 2
        return ddst[rr * 128:(rr + 1) * 128, cc * 1024:(cc + 1) * 1024]

    def bcast_scalar_to_col(val_ap, val_key, dst_col, col_key):
        s.op("tensor", lambda e: e.matmul(out=pp[2][:, 1016:1017], lhsT=k_onesr,
                                          rhs=val_ap, start=True, stop=True),
             reads=[val_key, "k_onesr"], writes=[PK(2, "b")])
        s.op("scalar", lambda e: e.copy(out=dst_col, in_=pp[2][:, 1016:1017]),
             reads=[PK(2, "b")], writes=[col_key])

    def scale_stream(mi, mn, dsrc):
        def dma_in(i, dsrc=dsrc):
            buf, bk = t_wch[i % 3], f"twch{i % 3}"
            s.op("gpsimd", lambda e, i=i, dsrc=dsrc, buf=buf: e.dma_start(
                out=buf, in_=wchunk(dsrc, i)), writes=[bk], dma=True,
                stream=f"scalein{i % 3}")

        for i in range(3):
            dma_in(i)
        for i in range(NWCH):
            buf, bk = t_wch[i % 3], f"twch{i % 3}"
            s.op("scalar", lambda e, i=i, mi=mi, buf=buf: e.activation(
                out=t_abs, in_=buf, func=AF.Abs,
                accum_out=k_accu[:, mi * 128 + i:mi * 128 + i + 1]),
                reads=[bk], writes=["t_abs", "k_accu"], acc=(i > 0))
            if i + 3 < NWCH:
                dma_in(i + 3)
            yield
        s.op("scalar", lambda e, mi=mi: e.copy(
            out=r_s[0:1, 12:13], in_=k_accu[0:1, mi * 128:mi * 128 + 1]),
            reads=["k_accu"], writes=["accfence"])
        s.op("vector", lambda e, mi=mi: e.tensor_reduce(
            out=k_part[:, mi:mi + 1], in_=k_accu[:, mi * 128:(mi + 1) * 128],
            axis=AX.X, op=ALU.add),
            reads=["k_accu", "accfence"], writes=[f"part_{mn}"])
        pcol, prk = k_part[:, mi:mi + 1], f"part_{mn}"
        s.op("tensor", lambda e, pcol=pcol: e.matmul(
            out=pp[2][0:1, 1018:1019], lhsT=pcol, rhs=k_onesc,
            start=True, stop=True),
            reads=[prk, "k_onesc"], writes=[PK(2, "b")])
        s.op("vector", lambda e, mi=mi: e.tensor_scalar(
            out=r_s[0:1, mi:mi + 1], in0=pp[2][0:1, 1018:1019],
            scalar1=1.0 / (DE * D), scalar2=EPS_Q,
            op0=ALU.mult, op1=ALU.add),
            reads=[PK(2, "b")], writes=["r_s"], acc=True)
        s.op("vector", lambda e, mi=mi: e.tensor_scalar(
            out=r_s[0:1, mi + 4:mi + 5], in0=r_s[0:1, mi:mi + 1],
            scalar1=-0.5, scalar2=None, op0=ALU.mult),
            reads=["r_s"], writes=["r_s"], acc=True)
        s.op("vector", lambda e, mi=mi: e.tensor_scalar(
            out=r_s[0:1, mi + 8:mi + 9], in0=r_s[0:1, mi:mi + 1],
            scalar1=0.5, scalar2=None, op0=ALU.mult),
            reads=["r_s"], writes=["r_s"], acc=True)
        s.op("vector", lambda e, mi=mi: e.reciprocal(
            out=r_s[0:1, 13 + mi:14 + mi], in_=r_s[0:1, mi + 8:mi + 9]),
            reads=["r_s"], writes=["r_s"], acc=True)
        bcast_scalar_to_col(r_s[0:1, 13 + mi:14 + mi], "r_s",
                            k_scl[:, mi:mi + 1], "k_scl")
        if mn == "u":
            bcast_scalar_to_col(r_s[0:1, 8:9], "r_s",
                                k_scl[:, 3:4], "sclp_u")
        yield

    def tern_stream(mi, mn, dsrc, ddst):
        def dma_in(i, dsrc=dsrc):
            buf, bk = t_wcht[i % 3], f"twcht{i % 3}"
            s.op("scalar", lambda e, i=i, dsrc=dsrc, buf=buf: e.dma_start(
                out=buf, in_=wchunk(dsrc, i)), writes=[bk], dma=True,
                stream=f"ternin{i % 3}")

        for i in range(3):
            dma_in(i)
        for i in range(NWCH):
            buf, bk = t_wcht[i % 3], f"twcht{i % 3}"
            sp_, spk = t_s1[i % 2], f"ts1{i % 2}"
            tn, tnk = t_tn[i % 2], f"ttn{i % 2}"
            s.op("scalar", lambda e, buf=buf, sp_=sp_, mi=mi: e.activation(
                out=sp_, in_=buf, func=AF.Sign,
                scale=k_scl[:, mi:mi + 1], bias=k_consts[:, 7:8]),
                reads=[bk, "k_scl", "k_consts"], writes=[spk])
            s.op("scalar", lambda e, buf=buf, tn=tn, mi=mi: e.activation(
                out=tn, in_=buf, func=AF.Sign,
                scale=k_scl[:, mi:mi + 1], bias=k_consts[:, 6:7]),
                reads=[bk, "k_scl", "k_consts"], writes=[tnk])
            s.op("gpsimd" if i % 2 == 0 else "vector",
                 lambda e, tn=tn, sp_=sp_: e.tensor_tensor(
                     out=tn, in0=sp_, in1=tn, op=ALU.add),
                 reads=[spk, tnk], writes=[tnk])
            s.op("gpsimd", lambda e, i=i, ddst=ddst, tn=tn, mn=mn: e.dma_start(
                out=wchunk_dst(ddst, i, mn == "d"), in_=tn),
                reads=[tnk], writes=[f"tern_{mn}"], dma=True, acc=True,
                stream="ternout")
            if i + 3 < NWCH:
                dma_in(i + 3)
            yield

    def weight_prep():
        import itertools
        for _ in scale_stream(0, "u", d_wu):
            yield
        for _ in itertools.zip_longest(scale_stream(1, "g", d_wg),
                                       tern_stream(0, "u", d_wu, d_wut)):
            yield
        for _ in itertools.zip_longest(scale_stream(2, "d", d_wd),
                                       tern_stream(1, "g", d_wg, d_wgt)):
            yield
        for _ in tern_stream(2, "d", d_wd, d_wdt):
            yield

    prep = weight_prep()
    prep_done = [False]

    def pump(n):
        if prep_done[0]:
            return
        for _ in range(n):
            try:
                next(prep)
            except StopIteration:
                prep_done[0] = True
                return

    def ln_finish(T, s1_ap, s1_keys, s2_ap, s2_keys, mb_tiles, rb_tiles):
        m_row = r_a[0:1, 0:T]
        v_row = r_b[0:1, 0:T]
        s.op("vector", lambda e: e.tensor_scalar(
            out=m_row, in0=s1_ap, scalar1=1.0 / D, scalar2=None, op0=ALU.mult),
            reads=list(s1_keys), writes=["r_a"])
        s.op("vector", lambda e: e.tensor_tensor(
            out=v_row, in0=m_row, in1=m_row, op=ALU.mult),
            reads=["r_a"], writes=["r_b"])
        s.op("vector", lambda e: e.scalar_tensor_tensor(
            out=v_row, in0=s2_ap, scalar=1.0 / D, in1=v_row,
            op0=ALU.mult, op1=ALU.subtract),
            reads=list(s2_keys) + ["r_b"], writes=["r_b"])
        s.op("scalar", lambda e: e.activation(
            out=v_row, in_=v_row, func=AF.Sqrt, bias=C_EPS1),
            reads=["r_b", "k_consts"], writes=["r_b"])
        s.op("vector", lambda e: e.reciprocal(out=v_row, in_=v_row),
             reads=["r_b"], writes=["r_b"])
        for ap_, bk, j0, j1 in mb_tiles:
            s.op("tensor", lambda e, ap_=ap_, j0=j0, j1=j1: e.matmul(
                out=ap_, lhsT=k_onesr, rhs=m_row[0:1, j0:j1],
                start=True, stop=True),
                reads=["r_a", "k_onesr"], writes=[bk])
        for ap_, bk, j0, j1 in rb_tiles:
            s.op("tensor", lambda e, ap_=ap_, j0=j0, j1=j1: e.matmul(
                out=ap_, lhsT=k_onesr, rhs=v_row[0:1, j0:j1],
                start=True, stop=True),
                reads=["r_b", "k_onesr"], writes=[bk])

    NHG = NH // 2

    for st in range(NST):
        r0 = st * STOWN
        xkeys = [f"xbf{i}" for i, (q0, q1) in enumerate(xr)
                 if q0 < r0 + STT and q1 > r0]
        if debug and st == 0:
            s.op("gpsimd", lambda e: e.dma_start(out=taps["t_xbf"],
                                                 in_=d_xbf[0:640, :]),
                 reads=["xbf0"], writes=["OUT"], dma=True, acc=True)
        for c in range(KCH):
            xk = f"xln1:{c}"
            s.op("sync", lambda e, c=c, r0=r0: e.dma_start(
                out=a_xln1[:, c * STT:(c + 1) * STT],
                in_=d_xbf[r0:r0 + STT, c * 128:(c + 1) * 128],
                transpose=True), reads=xkeys, writes=[xk], dma=True,
                stream=f"xfm{c % 4}")
            s.op("scalar", lambda e, c=c: e.activation(
                out=a_tmp1.bitcast(BF16)[:, 0:STT],
                in_=a_xln1[:, c * STT:(c + 1) * STT], func=AF.Square),
                reads=[xk], writes=["a_tmp1"])
            for j0, j1 in ((0, 512), (512, 640)):
                half = "a" if j0 == 0 else "b"
                s.op("tensor", lambda e, c=c, j0=j0, j1=j1: e.matmul(
                    out=pp[2][0:1, j0:j1], lhsT=k_onesb,
                    rhs=a_xln1[:, c * STT + j0:c * STT + j1],
                    start=(c == 0), stop=(c == KCH - 1)),
                    reads=[f"xln1:{c}", "k_onesb"],
                    writes=[PK(2, half)] if c in (0, KCH - 1) else [],
                    acc=(c == KCH - 1))
                s.op("tensor", lambda e, c=c, j0=j0, j1=j1: e.matmul(
                    out=pp[3][0:1, j0:j1], lhsT=k_onesb,
                    rhs=a_tmp1.bitcast(BF16)[:, j0:j1],
                    start=(c == 0), stop=(c == KCH - 1)),
                    reads=["a_tmp1", "k_onesb"],
                    writes=[PK(3, half)] if c in (0, KCH - 1) else [],
                    acc=(c == KCH - 1))
        pump(8)
        ln_finish(STT, pp[2][0:1, 0:STT], [PK(2, "a"), PK(2, "b")],
                  pp[3][0:1, 0:STT], [PK(3, "a"), PK(3, "b")],
                  [(pp[2][:, 0:512], PK(2, "a"), 0, 512),
                   (pp[2][:, 512:640], PK(2, "b"), 512, 640)],
                  [(pp[3][:, 0:512], PK(3, "a"), 0, 512),
                   (pp[3][:, 512:640], PK(3, "b"), 512, 640)])
        for c in range(KCH):
            s.op("vector", lambda e, c=c: e.tensor_tensor(
                out=a_tmp2, in0=a_xln1[:, c * STT:(c + 1) * STT],
                in1=pp[2][:, 0:STT], op=ALU.subtract),
                reads=[f"xln1:{c}", PK(2, "a"), PK(2, "b")], writes=["a_tmp2"])
            s.op("vector", lambda e, c=c: e.tensor_tensor(
                out=a_xln1[:, c * STT:(c + 1) * STT], in0=a_tmp2,
                in1=pp[3][:, 0:STT], op=ALU.mult),
                reads=["a_tmp2", PK(3, "a"), PK(3, "b")], writes=[f"xln1:{c}"])
        pump(8)
        xlnkeys = [f"xln1:{c}" for c in range(KCH)]

        if debug and st == 0:
            s.op("gpsimd", lambda e: e.dma_start(out=taps["t_xln1"], in_=a_xln1),
                 reads=xlnkeys, writes=["OUT"], dma=True, acc=True)
        if debug and st == 1:
            s.op("gpsimd", lambda e: e.dma_start(out=taps["t_xln1b"], in_=a_xln1),
                 reads=xlnkeys, writes=["OUT"], dma=True, acc=True)

        CHB = [(pp[3], 0, PK(3, "a")), (pp[3], 512, PK(3, "b")),
               (pp[2], 0, PK(2, "a")), (pp[2], 512, PK(2, "b"))]
        qv = d_wqkvbf.rearrange("(c p) f -> p c f", p=128)

        def emit_loads(h):
            if h >= NH:
                return
            wqk, wqkk = a_wqk[h % 2], f"awqk{h % 2}"
            wv2 = wqk.rearrange("p (c f) -> p c f", c=KCH)
            s.op("sync", lambda e, h=h, wv2=wv2: e.dma_start(
                out=wv2, in_=qv[:, :, h * 384:h * 384 + 256]),
                reads=["wqkvbf"], writes=[wqkk], dma=True,
                stream=f"wqk{h % 2}")
            if h % 2 == 0:
                hg = h // 2
                h0 = hg * 2
                vb = a_wv[hg % 2]
                vbk = f"awv{hg % 2}"
                wvv = vb.rearrange("p (c f) -> p c f", c=KCH)
                for hh in range(2):
                    s.op("sync", lambda e, hh=hh, wvv=wvv, h0=h0: e.dma_start(
                        out=wvv[:, :, hh * 128:(hh + 1) * 128],
                        in_=qv[:, :, (h0 + hh) * 384 + 256:(h0 + hh) * 384 + 384]),
                        reads=["wqkvbf"], writes=[vbk], dma=True, acc=(hh == 1),
                        stream=f"wv{hg % 2}")

        def emit_V(hg):
            vbuf = a_v[hg % 2]
            vk = f"av{hg % 2}"
            wv = a_wv[hg % 2]
            wvk = f"awv{hg % 2}"
            for tt in range(NTT):
                par = tt % 2
                pv = pp[0][:, 0:256] if par == 0 else pp[0][:, 512:768]
                pvk = PK(0, "a" if par == 0 else "b")
                for kc in range(KCH):
                    s.op("tensor", lambda e, tt=tt, kc=kc, pv=pv, wv=wv: e.matmul(
                        out=pv,
                        lhsT=a_xln1[:, kc * STT + tt * 128:kc * STT + (tt + 1) * 128],
                        rhs=wv[:, kc * 256:(kc + 1) * 256],
                        start=(kc == 0), stop=(kc == KCH - 1)),
                        reads=xlnkeys + [wvk],
                        writes=[pvk] if kc in (0, KCH - 1) else [],
                        acc=(kc == KCH - 1))
                s.op("scalar", lambda e, tt=tt, pv=pv, vbuf=vbuf: e.copy(
                    out=vbuf[:, tt * 256:(tt + 1) * 256], in_=pv),
                    reads=[pvk], writes=[f"{vk}:{tt}"])
            if debug and st == 1 and hg == 0:
                s.op("gpsimd", lambda e, vbuf=vbuf: e.dma_start(
                    out=taps["t_vb"], in_=vbuf),
                    reads=[f"{vk}:{t}" for t in range(NTT)], writes=["OUT"],
                    dma=True, acc=True)

        def emit_QKrope(h):
            wqk, wqkk = a_wqk[h % 2], f"awqk{h % 2}"
            for kc in range(KCH):
                s.op("tensor", lambda e, kc=kc, wqk=wqk: e.matmul(
                    out=pp[0][:, 0:512],
                    lhsT=wqk[:, kc * 256:kc * 256 + 128],
                    rhs=a_xln1[:, kc * STT + 128:kc * STT + STT],
                    start=(kc == 0), stop=(kc == KCH - 1)),
                    reads=xlnkeys + [wqkk],
                    writes=[PK(0, "a")] if kc in (0, KCH - 1) else [],
                    acc=(kc == KCH - 1))
            s.op("scalar", lambda e: e.copy(out=a_qraw, in_=pp[0][:, 0:512]),
                 reads=[PK(0, "a")], writes=["a_qraw"])
            for kc in range(KCH):
                for j0, j1 in ((0, 512), (512, 640)):
                    half = "a" if j0 == 0 else "b"
                    s.op("tensor", lambda e, kc=kc, j0=j0, j1=j1, wqk=wqk: e.matmul(
                        out=pp[1][:, j0:j1],
                        lhsT=wqk[:, kc * 256 + 128:kc * 256 + 256],
                        rhs=a_xln1[:, kc * STT + j0:kc * STT + j1],
                        start=(kc == 0), stop=(kc == KCH - 1)),
                        reads=xlnkeys + [wqkk],
                        writes=[PK(1, half)] if kc in (0, KCH - 1) else [],
                        acc=(kc == KCH - 1))
            s.op("scalar", lambda e: e.copy(out=a_kraw[:, 0:512],
                                            in_=pp[1][:, 0:512]),
                 reads=[PK(1, "a")], writes=["a_kraw"])
            s.op("scalar", lambda e: e.copy(out=a_kraw[:, 512:640],
                                            in_=pp[1][:, 512:640]),
                 reads=[PK(1, "b")], writes=["a_kraw"], acc=True)
            s.op("tensor", lambda e: e.matmul(
                out=pp[0][:, 512:1024], lhsT=k_rt, rhs=a_qraw,
                start=True, stop=True),
                reads=["a_qraw", "k_rt"], writes=[PK(0, "b")])
            for j0, j1 in ((0, 512), (512, 640)):
                half = "a" if j0 == 0 else "b"
                s.op("tensor", lambda e, j0=j0, j1=j1: e.matmul(
                    out=pp[1][:, j0:j1], lhsT=k_rt, rhs=a_kraw[:, j0:j1],
                    start=True, stop=True),
                    reads=["a_kraw", "k_rt"], writes=[PK(1, half)])
            cq = k_cos[:, 128 + r0:128 + r0 + 512]
            sq_ = k_sin[:, 128 + r0:128 + r0 + 512]
            ck = k_cos[:, r0:r0 + STT]
            sk = k_sin[:, r0:r0 + STT]
            s.op("gpsimd", lambda e, cq=cq: e.tensor_tensor(
                out=a_tmp1[:, 0:512], in0=a_qraw, in1=cq, op=ALU.mult),
                reads=["a_qraw", "k_cos"], writes=["a_tmp1"])
            s.op("vector", lambda e, sq_=sq_: e.tensor_tensor(
                out=a_tmp2[:, 0:512], in0=pp[0][:, 512:1024], in1=sq_,
                op=ALU.mult),
                reads=[PK(0, "b"), "k_sin"], writes=["a_tmp2"])
            s.op("vector", lambda e: e.tensor_tensor(
                out=a_q, in0=a_tmp1[:, 0:512], in1=a_tmp2[:, 0:512],
                op=ALU.add),
                reads=["a_tmp1", "a_tmp2"], writes=["a_q"])
            s.op("gpsimd", lambda e, ck=ck: e.tensor_tensor(
                out=a_tmp1[:, 0:STT], in0=a_kraw, in1=ck, op=ALU.mult),
                reads=["a_kraw", "k_cos"], writes=["a_tmp1"])
            for j0, j1 in ((0, 512), (512, 640)):
                half = "a" if j0 == 0 else "b"
                s.op("vector", lambda e, j0=j0, j1=j1, sk=sk: e.tensor_tensor(
                    out=a_tmp2[:, j0:j1], in0=pp[1][:, j0:j1],
                    in1=sk[:, j0:j1], op=ALU.mult),
                    reads=[PK(1, half), "k_sin"], writes=["a_tmp2"],
                    acc=(j0 != 0))
            s.op("vector", lambda e: e.tensor_tensor(
                out=a_k, in0=a_tmp1[:, 0:STT], in1=a_tmp2[:, 0:STT],
                op=ALU.add),
                reads=["a_tmp1", "a_tmp2"], writes=["a_k"])
            if debug and st == 0 and h == 0:
                s.op("gpsimd", lambda e: e.dma_start(out=taps["t_q"], in_=a_q),
                     reads=["a_q"], writes=["OUT"], dma=True, acc=True)
                s.op("gpsimd", lambda e: e.dma_start(out=taps["t_k"], in_=a_k),
                     reads=["a_k"], writes=["OUT"], dma=True, acc=True)
            if debug and st == 1 and h == 0:
                s.op("gpsimd", lambda e: e.dma_start(out=taps["t_qb"], in_=a_q),
                     reads=["a_q"], writes=["OUT"], dma=True, acc=True)
                s.op("gpsimd", lambda e: e.dma_start(out=taps["t_kb"], in_=a_k),
                     reads=["a_k"], writes=["OUT"], dma=True, acc=True)

        def emit_A(h):
            for qt in range(NQT):
                ppx, off, chk = CHB[qt]
                pst = ppx[:, off:off + 256]
                atr, at = a_atr[qt], a_at[qt]
                for ksub in range(2):
                    ktt = qt + ksub
                    s.op("tensor", lambda e, ksub=ksub, ktt=ktt, pst=pst, qt=qt: e.matmul(
                        out=pst[:, ksub * 128:(ksub + 1) * 128],
                        lhsT=a_k[:, ktt * 128:(ktt + 1) * 128],
                        rhs=a_q[:, qt * 128:(qt + 1) * 128],
                        start=(ksub == 0), stop=(ksub == 1)),
                        reads=["a_k", "a_q"],
                        writes=[chk], acc=(ksub == 1))
                s.op("scalar", lambda e, pst=pst, atr=atr: e.activation(
                    out=atr, in_=pst, func=AF.Exp, scale=C_SMS),
                    reads=[chk, "k_consts"], writes=[f"atr{qt}"])
                mask = k_mtf if (st == 0 and qt == 0) else k_mtr
                mkey = "k_mtf" if (st == 0 and qt == 0) else "k_mtr"
                s.op("gpsimd", lambda e, atr=atr, at=at, mask=mask: e.tensor_tensor(
                    out=at, in0=atr, in1=mask, op=ALU.mult),
                    reads=[f"atr{qt}", mkey], writes=[f"at{qt}"])
                if debug and st == 0 and h == 0 and qt == 0:
                    s.op("gpsimd", lambda e, at=at: e.dma_start(
                        out=taps["t_at"], in_=at),
                        reads=[f"at{qt}"], writes=["OUT"], dma=True, acc=True)

        def emit_B(h):
            hh = h % 2
            vbuf = a_v[(h // 2) % 2]
            vk = f"av{(h // 2) % 2}"
            for qt in range(NQT):
                ppx, off, chk = CHB[qt]
                prs = ppx[0:1, off + 256:off + 384]
                at = a_at[qt]
                for ksub in range(2):
                    s.op("tensor", lambda e, ksub=ksub, at=at, prs=prs: e.matmul(
                        out=prs, lhsT=k_onesb,
                        rhs=at[:, ksub * 128:(ksub + 1) * 128],
                        start=(ksub == 0), stop=(ksub == 1)),
                        reads=[f"at{qt}", "k_onesb"],
                        writes=[chk], acc=True)
            for qt in range(NQT):
                ppx, off, chk = CHB[qt]
                prs = ppx[0:1, off + 256:off + 384]
                rrrow = r_rr[0:1, qt * 128:qt * 128 + 128]
                s.op("vector", lambda e, prs=prs, rrrow=rrrow: e.reciprocal(
                    out=rrrow, in_=prs),
                    reads=[chk], writes=[f"rr{qt}"])
            for qt in range(NQT):
                ppx, off, chk = CHB[qt]
                prrb = ppx[:, off + 384:off + 512]
                rrrow = r_rr[0:1, qt * 128:qt * 128 + 128]
                s.op("tensor", lambda e, rrrow=rrrow, prrb=prrb: e.matmul(
                    out=prrb, lhsT=k_onesr, rhs=rrrow, start=True, stop=True),
                    reads=[f"rr{qt}", "k_onesr"],
                    writes=[chk], acc=True)
            for qt in range(NQT):
                ppx, off, chk = CHB[qt]
                prrb = ppx[:, off + 384:off + 512]
                s.op("scalar", lambda e, prrb=prrb, rrb=a_rrb[qt]: e.copy(
                    out=rrb, in_=prrb),
                    reads=[chk], writes=[f"rrb{qt}"])
            for qt in range(NQT):
                ppx, off, chk = CHB[qt]
                pav = ppx[:, off:off + 128]
                at = a_at[qt]
                for ksub in range(2):
                    ktt = qt + ksub
                    s.op("tensor", lambda e, ksub=ksub, ktt=ktt, at=at, pav=pav, hh=hh, vbuf=vbuf: e.matmul(
                        out=pav,
                        lhsT=vbuf[:, ktt * 256 + hh * 128:ktt * 256 + hh * 128 + 128],
                        rhs=at[:, ksub * 128:(ksub + 1) * 128],
                        start=(ksub == 0), stop=(ksub == 1)),
                        reads=[f"at{qt}", f"{vk}:{ktt}"],
                        writes=[chk], acc=True)
            for qt in range(NQT):
                ppx, off, chk = CHB[qt]
                pav = ppx[:, off:off + 128]
                col = st * 512 + qt * 128
                s.op("vector", lambda e, pav=pav, rrb=a_rrb[qt], h=h, col=col: e.tensor_tensor(
                    out=a_attn[:, h * TOK + col:h * TOK + col + 128],
                    in0=pav, in1=rrb, op=ALU.mult),
                    reads=[chk, f"rrb{qt}"], writes=[f"attn:{h}"], acc=True)

        emit_loads(0)
        emit_loads(1)
        prev = None
        for h in range(NH):
            if prev is not None:
                emit_A(prev)
            if h % 2 == 0:
                emit_V(h // 2)
            emit_QKrope(h)
            emit_loads(h + 2)
            if prev is not None:
                emit_B(prev)
                pump(6)
            prev = h
        emit_A(prev)
        emit_B(prev)
        pump(6)

    attnkeys = [f"attn:{h}" for h in range(NH)]
    if debug:
        s.op("gpsimd", lambda e: e.dma_start(out=taps["t_attn"], in_=a_attn),
             reads=attnkeys, writes=["OUT"], dma=True, acc=True)

    pump(10 ** 9)

    if debug:
        s.op("gpsimd", lambda e: e.dma_start(out=taps["t_scl"],
                                              in_=r_s.broadcast_to([128, 16])
                                              if False else r_s),
             reads=["r_s"], writes=["OUT"], dma=True, acc=True)
        s.op("gpsimd", lambda e: e.dma_start(out=taps["t_wut"], in_=d_wut[0:256, :]),
             reads=["tern_u"], writes=["OUT"], dma=True, acc=True)
        s.op("gpsimd", lambda e: e.dma_start(out=taps["t_wdt"], in_=d_wdt[0:256, :]),
             reads=["tern_d"], writes=["OUT"], dma=True, acc=True)

    att_work_keys = (["a_tmp1", "a_tmp2", "a_q", "a_qraw",
                      "a_k", "a_kraw", "awv0", "awv1", "awqk0", "awqk1"]
                     + [f"atr{q}" for q in range(4)] + [f"at{q}" for q in range(4)]
                     + [f"rrb{q}" for q in range(4)]
                     + [f"av{p}:{t}" for p in range(2) for t in range(NTT)]
                     + xlnkeys
                     + ["ttn0", "ttn1", "ts10", "ts11", "t_abs"])
    s.alias(["c_ybf", "c_wout", "c_sq", "c_t", "cz10", "cz11"],
            att_work_keys)
    s.alias([f"ybf:{fc}" for fc in range(KCH)], att_work_keys)

    for ec in range(KCH):
        s.op("sync", lambda e, ec=ec: e.dma_start(
            out=c_wout[:, ec * 2048:(ec + 1) * 2048],
            in_=d_woutbf[ec * 128:(ec + 1) * 128, :]),
            reads=["woutbf"], writes=["c_wout"], dma=True, acc=(ec > 0),
            stream="wout")
    for tq in range(4):
        tq0 = tq * 512
        for fc in range(KCH):
            par = fc % 2
            py = pp[par][:, 0:512]
            pyk = PK(par, "a")
            for ec in range(KCH):
                s.op("tensor", lambda e, ec=ec, fc=fc, py=py, tq0=tq0: e.matmul(
                    out=py, lhsT=c_wout[:, ec * 2048 + fc * 128:ec * 2048 + (fc + 1) * 128],
                    rhs=a_attn[:, ec * TOK + tq0:ec * TOK + tq0 + 512],
                    start=(ec == 0), stop=(ec == KCH - 1)),
                    reads=attnkeys + ["c_wout"],
                    writes=[pyk] if ec in (0, KCH - 1) else [],
                    acc=(ec == KCH - 1))
            s.op("vector", lambda e, py=py, fc=fc: e.tensor_scalar(
                out=c_ybf[:, fc * 512:(fc + 1) * 512], in0=py,
                scalar1=k_bout[:, fc:fc + 1], scalar2=None, op0=ALU.add),
                reads=[pyk, "k_bout"], writes=[f"ybf:{fc}"])
            pump(3)
        for fc in range(KCH):
            s.op("scalar", lambda e, fc=fc: e.activation(
                out=c_sq, in_=c_ybf[:, fc * 512:(fc + 1) * 512], func=AF.Square),
                reads=[f"ybf:{fc}"], writes=["c_sq"])
            s.op("tensor", lambda e, fc=fc: e.matmul(
                out=pp[2][0:1, 0:512], lhsT=k_onesb,
                rhs=c_ybf[:, fc * 512:(fc + 1) * 512],
                start=(fc == 0), stop=(fc == KCH - 1)),
                reads=[f"ybf:{fc}", "k_onesb"],
                writes=[PK(2, "a")] if fc in (0, KCH - 1) else [],
                acc=(fc == KCH - 1))
            s.op("tensor", lambda e, fc=fc: e.matmul(
                out=pp[2][0:1, 512:1024], lhsT=k_onesb, rhs=c_sq,
                start=(fc == 0), stop=(fc == KCH - 1)),
                reads=["c_sq", "k_onesb"],
                writes=[PK(2, "b")] if fc in (0, KCH - 1) else [],
                acc=(fc == KCH - 1))
        ln_finish(512, pp[2][0:1, 0:512], [PK(2, "a")],
                  pp[2][0:1, 512:1024], [PK(2, "b")],
                  [(pp[3][:, 0:512], PK(3, "a"), 0, 512)],
                  [(pp[3][:, 512:1024], PK(3, "b"), 0, 512)])
        for fc in range(KCH):
            zb, zbk = c_z1[fc % 2], f"cz1{fc % 2}"
            s.op("vector", lambda e, fc=fc: e.tensor_tensor(
                out=c_t, in0=c_ybf[:, fc * 512:(fc + 1) * 512],
                in1=pp[3][:, 0:512], op=ALU.subtract),
                reads=[f"ybf:{fc}", PK(3, "a")], writes=["c_t"])
            s.op("vector", lambda e, zb=zb: e.tensor_tensor(
                out=zb, in0=c_t, in1=pp[3][:, 512:1024], op=ALU.mult),
                reads=["c_t", PK(3, "b")], writes=[zbk])
            s.op("gpsimd", lambda e, fc=fc, tq0=tq0, zb=zb: e.dma_start(
                out=d_z1[fc * 128:(fc + 1) * 128, tq0:tq0 + 512], in_=zb),
                reads=[zbk], writes=["z1s"], dma=True, acc=True, stream="z1w")

    if debug:
        s.op("gpsimd", lambda e: e.dma_start(out=taps["t_z1"], in_=d_z1),
             reads=["z1s"], writes=["OUT"], dma=True, acc=True)

    c_keys = (["c_ybf", "c_wout", "c_sq", "c_t", "cz10", "cz11"]
              + [f"ybf:{fc}" for fc in range(KCH)]
              + ["ts10", "ts11", "t_abs", "ttn0", "ttn1", "twch0", "twch1"] + attnkeys)
    ffn_first = (["fc10", "fc11", "fc20", "fc21", "f_t", "f_wu", "f_wg", "f_wd"]
                 + [f"xn:{c}" for c in range(KCH)]
                 + [f"hid:{ec}" for ec in range(EB)]
                 + [f"facc:{dc}:{tt}" for dc in range(KCH) for tt in range(2)])
    s.alias(ffn_first, c_keys)


    for hf in range(2):
        hr0 = HALO + hf * HT
        hx = [f"xbf{i}" for i, (q0, q1) in enumerate(xr)
              if q0 < hr0 + HT and q1 > hr0]
        if hf == 1:
            s.alias([f"xn:{c}" for c in range(KCH)],
                    [f"zsum:{dc}" for dc in range(KCH)])
            s.alias(["f_wu", "f_wg", "f_wd", "fc10", "fc11", "fc20", "fc21",
                     "f_t"],
                    ["zsT0", "zsT1", "ex0", "ex1", "eout0", "eout1"]
                    + [f"zsum:{dc}" for dc in range(KCH)])
        for c in range(KCH):
            xc, xck = f_c1[c % 2], f"fc1{c % 2}"
            zc, zck = f_c2[c % 2], f"fc2{c % 2}"
            s.op("sync", lambda e, c=c, xc=xc, hr0=hr0: e.dma_start(
                out=xc, in_=d_xbf[hr0:hr0 + HT, c * 128:(c + 1) * 128],
                transpose=True), reads=hx, writes=[xck], dma=True,
                stream=f"xfm{c % 4}")
            s.op("gpsimd", lambda e, c=c, zc=zc, hf=hf: e.dma_start(
                out=zc, in_=d_z1[c * 128:(c + 1) * 128, hf * HT:hf * HT + HT]),
                reads=["z1s"], writes=[zck], dma=True, stream=f"z1r{c % 2}")
            s.op("vector", lambda e, c=c, xc=xc, zc=zc: e.tensor_tensor(
                out=f_xln2[:, c * HT:(c + 1) * HT], in0=xc, in1=zc, op=ALU.add),
                reads=[xck, zck], writes=[f"xn:{c}"])
            s.op("scalar", lambda e, c=c: e.activation(
                out=f_t.bitcast(BF16)[:, 0:HT],
                in_=f_xln2[:, c * HT:(c + 1) * HT], func=AF.Square),
                reads=[f"xn:{c}"], writes=["f_t"])
            for j0, j1 in ((0, 512), (512, 1024)):
                half = "a" if j0 == 0 else "b"
                s.op("tensor", lambda e, c=c, j0=j0, j1=j1: e.matmul(
                    out=pp[2][0:1, j0:j1], lhsT=k_onesb,
                    rhs=f_xln2[:, c * HT + j0:c * HT + j1],
                    start=(c == 0), stop=(c == KCH - 1)),
                    reads=[f"xn:{c}", "k_onesb"],
                    writes=[PK(2, half)] if c in (0, KCH - 1) else [],
                    acc=(c == KCH - 1))
                s.op("tensor", lambda e, c=c, j0=j0, j1=j1: e.matmul(
                    out=pp[3][0:1, j0:j1], lhsT=k_onesb,
                    rhs=f_t.bitcast(BF16)[:, j0:j1],
                    start=(c == 0), stop=(c == KCH - 1)),
                    reads=["f_t", "k_onesb"],
                    writes=[PK(3, half)] if c in (0, KCH - 1) else [],
                    acc=(c == KCH - 1))
        ln_finish(HT, pp[2][0:1, 0:HT], [PK(2, "a"), PK(2, "b")],
                  pp[3][0:1, 0:HT], [PK(3, "a"), PK(3, "b")],
                  [(pp[0][:, 0:512], PK(0, "a"), 0, 512),
                   (pp[0][:, 512:1024], PK(0, "b"), 512, 1024)],
                  [(pp[1][:, 0:512], PK(1, "a"), 0, 512),
                   (pp[1][:, 512:1024], PK(1, "b"), 512, 1024)])
        for c in range(KCH):
            s.op("vector", lambda e, c=c: e.tensor_tensor(
                out=f_t, in0=f_xln2[:, c * HT:(c + 1) * HT],
                in1=pp[0][:, 0:HT], op=ALU.subtract),
                reads=[f"xn:{c}", PK(0, "a"), PK(0, "b")], writes=["f_t"])
            s.op("vector", lambda e, c=c: e.tensor_tensor(
                out=f_xln2[:, c * HT:(c + 1) * HT], in0=f_t,
                in1=pp[1][:, 0:HT], op=ALU.mult),
                reads=["f_t", PK(1, "a"), PK(1, "b")], writes=[f"xln2:{c}"])
        xln2keys = [f"xln2:{c}" for c in range(KCH)]
        if debug and hf == 0:
            s.op("gpsimd", lambda e: e.dma_start(out=taps["t_xln2"], in_=f_xln2),
                 reads=xln2keys, writes=["OUT"], dma=True, acc=True)

        NB = ECH // EB

        def ld_wu(bb):
            if bb >= NB:
                return
            e0 = bb * EB
            for kd in range(KCH):
                s.op("sync", lambda e, kd=kd, e0=e0: e.dma_start(
                    out=f_wu[:, kd * EB * 128:(kd + 1) * EB * 128],
                    in_=d_wut[e0 * 128:e0 * 128 + EB * 128,
                              kd * 128:(kd + 1) * 128], transpose=True),
                    reads=["tern_u"], writes=["f_wu"], dma=True, acc=(kd > 0),
                    stream="fwu")

        def ld_wg(bb):
            if bb >= NB:
                return
            e0 = bb * EB
            for kd in range(KCH):
                s.op("sync", lambda e, kd=kd, e0=e0: e.dma_start(
                    out=f_wg[:, kd * EB * 128:(kd + 1) * EB * 128],
                    in_=d_wgt[e0 * 128:e0 * 128 + EB * 128,
                              kd * 128:(kd + 1) * 128], transpose=True),
                    reads=["tern_g"], writes=["f_wg"], dma=True, acc=(kd > 0),
                    stream="fwg")

        def ld_wd(bb):
            if bb >= NB:
                return
            e0 = bb * EB
            for ec in range(EB):
                s.op("sync", lambda e, ec=ec, e0=e0: e.dma_start(
                    out=f_wd[:, ec * D:(ec + 1) * D],
                    in_=d_wdt[:, (e0 + ec) * 128:(e0 + ec + 1) * 128],
                    transpose=True),
                    reads=["tern_d"], writes=["f_wd"], dma=True, acc=(ec > 0),
                    stream="fwd")

        ld_wu(0)
        ld_wg(0)
        ld_wd(0)
        for bb in range(NB):
            for ec in range(EB):
                for jt in range(2):
                    par = jt
                    pu = pp[0][:, par * 512:(par + 1) * 512]
                    puk = PK(0, "a" if par == 0 else "b")
                    for kd in range(KCH):
                        s.op("tensor", lambda e, ec=ec, jt=jt, kd=kd, pu=pu: e.matmul(
                            out=pu,
                            lhsT=f_wu[:, kd * EB * 128 + ec * 128:kd * EB * 128 + (ec + 1) * 128],
                            rhs=f_xln2[:, kd * HT + jt * 512:kd * HT + (jt + 1) * 512],
                            start=(kd == 0), stop=(kd == KCH - 1)),
                            reads=xln2keys + ["f_wu"],
                            writes=[puk] if kd in (0, KCH - 1) else [],
                            acc=(kd == KCH - 1))
                    s.op("scalar", lambda e, ec=ec, jt=jt, pu=pu: e.activation(
                        out=f_hid[:, ec * HT + jt * 512:ec * HT + (jt + 1) * 512],
                        in_=pu, func=AF.Silu, scale=k_scl[:, 3:4]),
                        reads=[puk, "sclp_u"], writes=[f"hid:{ec}"],
                        acc=(jt == 1))
            ld_wu(bb + 1)
            for ec in range(EB):
                for jt in range(2):
                    par = jt
                    pg = pp[1][:, par * 512:(par + 1) * 512]
                    pgk = PK(1, "a" if par == 0 else "b")
                    for kd in range(KCH):
                        s.op("tensor", lambda e, ec=ec, jt=jt, kd=kd, pg=pg: e.matmul(
                            out=pg,
                            lhsT=f_wg[:, kd * EB * 128 + ec * 128:kd * EB * 128 + (ec + 1) * 128],
                            rhs=f_xln2[:, kd * HT + jt * 512:kd * HT + (jt + 1) * 512],
                            start=(kd == 0), stop=(kd == KCH - 1)),
                            reads=xln2keys + ["f_wg"],
                            writes=[pgk] if kd in (0, KCH - 1) else [],
                            acc=(kd == KCH - 1))
                    s.op("vector", lambda e, ec=ec, jt=jt, pg=pg: e.tensor_tensor(
                        out=f_hid[:, ec * HT + jt * 512:ec * HT + (jt + 1) * 512],
                        in0=pg,
                        in1=f_hid[:, ec * HT + jt * 512:ec * HT + (jt + 1) * 512],
                        op=ALU.mult),
                        reads=[pgk, f"hid:{ec}"], writes=[f"hid:{ec}"],
                        acc=(jt == 1))
            ld_wg(bb + 1)
            hidkeys = [f"hid:{ec}" for ec in range(EB)]
            for dc in range(KCH):
                for tti in range(2):
                    par = (dc * 2 + tti) % 4
                    pd = (pp[2] if par < 2 else pp[3])[
                        :, (par % 2) * 512:(par % 2) * 512 + 512]
                    pdk = PK(2 + par // 2, "a" if par % 2 == 0 else "b")
                    for ec in range(EB):
                        s.op("tensor", lambda e, dc=dc, tti=tti, ec=ec, pd=pd: e.matmul(
                            out=pd,
                            lhsT=f_wd[:, ec * D + dc * 128:ec * D + (dc + 1) * 128],
                            rhs=f_hid[:, ec * HT + tti * 512:ec * HT + (tti + 1) * 512],
                            start=(ec == 0), stop=(ec == EB - 1)),
                            reads=hidkeys + ["f_wd"],
                            writes=[pdk] if ec in (0, EB - 1) else [],
                            acc=(ec == EB - 1))
                    dst = f_acc[:, dc * HT + tti * 512:dc * HT + (tti + 1) * 512]
                    if bb == 0:
                        s.op("vector", lambda e, dst=dst, pd=pd: e.tensor_copy(
                            out=dst, in_=pd),
                            reads=[pdk], writes=[f"facc:{dc}:{tti}"])
                    else:
                        s.op("vector", lambda e, dst=dst, pd=pd: e.tensor_tensor(
                            out=dst, in0=pd, in1=dst, op=ALU.add),
                            reads=[pdk, f"facc:{dc}:{tti}"],
                            writes=[f"facc:{dc}:{tti}"])
            ld_wd(bb + 1)

        facckeys = [f"facc:{dc}:{tti}" for dc in range(KCH) for tti in range(2)]
        if debug and hf == 0:
            s.op("gpsimd", lambda e: e.dma_start(
                out=taps["t_acc"], in_=f_acc[:, 0:2 * HT]),
                reads=facckeys, writes=["OUT"], dma=True, acc=True)

        for dc in range(KCH):
            cb, cbk = f_c1[dc % 2], f"fc1{dc % 2}"
            s.op("gpsimd", lambda e, dc=dc, cb=cb: e.tensor_copy(
                out=cb, in_=f_acc[:, dc * HT:(dc + 1) * HT]),
                reads=[f"facc:{dc}:0", f"facc:{dc}:1"], writes=[cbk])
            s.op("scalar", lambda e, cb=cb: e.activation(
                out=f_t.bitcast(BF16)[:, 0:HT], in_=cb, func=AF.Square),
                reads=[cbk], writes=["f_t"])
            for j0, j1 in ((0, 512), (512, 1024)):
                half = "a" if j0 == 0 else "b"
                s.op("tensor", lambda e, dc=dc, j0=j0, j1=j1, cb=cb: e.matmul(
                    out=pp[2][0:1, j0:j1], lhsT=k_onesb, rhs=cb[:, j0:j1],
                    start=(dc == 0), stop=(dc == KCH - 1)),
                    reads=[cbk, "k_onesb"],
                    writes=[PK(2, half)] if dc in (0, KCH - 1) else [],
                    acc=(dc == KCH - 1))
                s.op("tensor", lambda e, dc=dc, j0=j0, j1=j1: e.matmul(
                    out=pp[3][0:1, j0:j1], lhsT=k_onesb,
                    rhs=f_t.bitcast(BF16)[:, j0:j1],
                    start=(dc == 0), stop=(dc == KCH - 1)),
                    reads=["f_t", "k_onesb"],
                    writes=[PK(3, half)] if dc in (0, KCH - 1) else [],
                    acc=(dc == KCH - 1))
        ln_finish(HT, pp[2][0:1, 0:HT], [PK(2, "a"), PK(2, "b")],
                  pp[3][0:1, 0:HT], [PK(3, "a"), PK(3, "b")],
                  [(pp[0][:, 0:512], PK(0, "a"), 0, 512),
                   (pp[0][:, 512:1024], PK(0, "b"), 512, 1024)],
                  [(pp[1][:, 0:512], PK(1, "a"), 0, 512),
                   (pp[1][:, 512:1024], PK(1, "b"), 512, 1024)])
        s.alias([f"zsum:{dc}" for dc in range(KCH)], xln2keys)
        s.alias(["zsT0", "zsT1", "ex0", "ex1", "eout0", "eout1"],
                ["f_wu", "f_wg", "f_wd"] + [f"hid:{ec}" for ec in range(EB)])
        for dc in range(KCH):
            zc, zck = f_c2[dc % 2], f"fc2{dc % 2}"
            s.op("gpsimd", lambda e, dc=dc, zc=zc, hf=hf: e.dma_start(
                out=zc, in_=d_z1[dc * 128:(dc + 1) * 128, hf * HT:hf * HT + HT]),
                reads=["z1s"], writes=[zck], dma=True, stream=f"z1r{dc % 2}")
            s.op("vector", lambda e, dc=dc: e.tensor_tensor(
                out=f_t, in0=f_acc[:, dc * HT:(dc + 1) * HT],
                in1=pp[0][:, 0:HT], op=ALU.subtract),
                reads=[f"facc:{dc}:0", f"facc:{dc}:1", PK(0, "a"), PK(0, "b")],
                writes=["f_t"])
            s.op("vector", lambda e, dc=dc: e.tensor_tensor(
                out=f_t.bitcast(BF16)[:, 0:HT], in0=f_t, in1=pp[1][:, 0:HT],
                op=ALU.mult),
                reads=["f_t", PK(1, "a"), PK(1, "b")], writes=["f_t"])
            s.op("vector", lambda e, dc=dc, zc=zc: e.tensor_tensor(
                out=e_zsum[:, dc * HT:(dc + 1) * HT],
                in0=f_t.bitcast(BF16)[:, 0:HT], in1=zc, op=ALU.add),
                reads=["f_t", zck], writes=[f"zsum:{dc}"])
        for t8 in range(8):
            par = t8 % 2
            ptr = pp[par].bitcast(BF16)
            for dc in range(KCH):
                bank = dc // 8
                slot = dc % 8
                bkk = PK(par, "a" if bank == 0 else "b")
                s.op("tensor", lambda e, dc=dc, t8=t8, ptr=ptr, bank=bank, slot=slot: e.transpose(
                    out=ptr[:, bank * 1024 + slot * 128:bank * 1024 + (slot + 1) * 128],
                    in_=e_zsum[:, dc * HT + t8 * 128:dc * HT + (t8 + 1) * 128],
                    identity=k_idn),
                    reads=[f"zsum:{dc}", "k_idn"],
                    writes=[bkk], acc=(slot > 0))
            zt = e_zsT[par]
            for bank in range(2):
                bkk = PK(par, "a" if bank == 0 else "b")
                s.op("scalar", lambda e, zt=zt, ptr=ptr, bank=bank: e.copy(
                    out=zt[:, bank * 1024:(bank + 1) * 1024],
                    in_=ptr[:, bank * 1024:bank * 1024 + 1024]),
                    reads=[bkk], writes=[f"zsT{par}"], acc=(bank > 0))
            ex = e_x[par]
            s.op("gpsimd", lambda e, t8=t8, ex=ex, hf=hf: e.dma_start(
                out=ex,
                in_=d_x[HALO + hf * HT + t8 * 128:HALO + hf * HT + (t8 + 1) * 128, :]),
                writes=[f"ex{par}"], dma=True, stream=f"xld{par}")
            eo = e_out[par]
            s.op("vector", lambda e, ex=ex, zt=zt, eo=eo: e.tensor_tensor(
                out=eo, in0=ex, in1=zt, op=ALU.add),
                reads=[f"ex{par}", f"zsT{par}"], writes=[f"eout{par}"])
            s.op("gpsimd", lambda e, t8=t8, eo=eo, hf=hf: e.dma_start(
                out=d_out[hf * HT + t8 * 128:hf * HT + (t8 + 1) * 128, :], in_=eo),
                reads=[f"eout{par}"], writes=["OUT"], dma=True, acc=True,
                stream="outw")

    s.op("sync", lambda e: e.nop(), reads=["OUT"])

    with nc.Block() as block:
        s.finalize(block)

    return nc, taps, s


def _host_inputs(inputs):
    x = np.ascontiguousarray(np.asarray(inputs["L0_attn_x"], dtype=np.float32))
    wqkv = np.asarray(inputs["L0_attn_W_qkv"], dtype=np.float32)
    wout = np.ascontiguousarray(np.asarray(inputs["L0_attn_W_out"], dtype=np.float32))
    bout = np.asarray(inputs["L0_attn_b_out"], dtype=np.float32)
    wu = np.ascontiguousarray(np.asarray(inputs["L0_ffn_W_up"], dtype=np.float32))
    wg = np.ascontiguousarray(np.asarray(inputs["L0_ffn_W_gate"], dtype=np.float32))
    wd = np.ascontiguousarray(np.asarray(inputs["L0_ffn_W_down"], dtype=np.float32))

    wqkv_r = np.ascontiguousarray(
        wqkv.reshape(D, NH, HD, 3).transpose(0, 1, 3, 2).reshape(D, 3 * D))
    bout_fm = np.ascontiguousarray(bout.reshape(16, 128).T)

    inv_freq = 1.0 / (THETA_F ** (np.arange(0, HD, 2, dtype=np.float64) / HD))

    r = np.arange(128)
    m0_rest = (r[:, None] >= r[None, :]).astype(np.float32)
    m1 = (r[:, None] <= r[None, :]).astype(np.float32)
    mask_rest = np.ascontiguousarray(
        np.concatenate([m0_rest, m1], axis=1)).astype(ml_dtypes.bfloat16)
    mask_first = np.ascontiguousarray(
        np.concatenate([np.zeros((128, 128), np.float32), m1],
                       axis=1)).astype(ml_dtypes.bfloat16)

    R = np.zeros((128, 128), np.float32)
    ii = np.arange(0, 128, 2)
    R[ii, ii + 1] = -1.0
    R[ii + 1, ii] = 1.0
    RT = np.ascontiguousarray(R.T).astype(ml_dtypes.bfloat16)
    idn = np.eye(128, dtype=ml_dtypes.bfloat16)
    onesr = np.ones((1, 128), np.float32)
    onesc = np.ones((128, 1), np.float32)
    onesb = np.ones((128, 1), ml_dtypes.bfloat16)
    consts = np.tile(np.array([[EPS_LN, 1.0 / D, SM_SCALE, 0.5,
                                1.0 / (DE * D), EPS_Q, 1.0, -1.0]],
                              np.float32), (128, 1))

    in_maps = []
    for c in range(NCORES):
        bi, sc = c // 4, c % 4
        s0 = sc * TOK
        if sc == 0:
            xs = np.concatenate([np.zeros((HALO, D), np.float32),
                                 x[bi, 0:TOK]], axis=0)
        else:
            xs = x[bi, s0 - HALO:s0 + TOK]
        xs = np.ascontiguousarray(xs)
        pos = np.maximum(np.arange(s0 - HALO, s0 + TOK, dtype=np.float64), 0.0)
        ang = np.repeat(pos[:, None] * inv_freq[None, :], 2, axis=1)
        cosT = np.ascontiguousarray(np.cos(ang).T.astype(ml_dtypes.bfloat16))
        sinT = np.ascontiguousarray(np.sin(ang).T.astype(ml_dtypes.bfloat16))
        in_maps.append({
            "x": xs, "wqkv": wqkv_r, "wout": wout, "bout": bout_fm,
            "wu": wu, "wg": wg, "wd": wd,
            "cost": cosT, "sint": sinT,
            "maskf": mask_first if sc == 0 else mask_rest,
            "maskr": mask_rest,
            "rt": RT, "idn": idn, "onesr": onesr, "onesc": onesc,
            "onesb": onesb, "consts": consts,
        })
    return in_maps


_CACHED = {}


def kernel(**inputs):
    if "nc" not in _CACHED:
        _CACHED["nc"] = build_program(debug=False)[0]
    nc = _CACHED["nc"]
    in_maps = _host_inputs(inputs)
    res = run_bass_kernel_spmd(nc, in_maps, core_ids=list(range(NCORES)))
    out = np.empty((B, S, D), dtype=np.float32)
    for c in range(NCORES):
        bi, sc = c // 4, c % 4
        out[bi, sc * TOK:(sc + 1) * TOK, :] = res.results[c]["out"]
    return out


if __name__ == "__main__":
    nc, _, _s = build_program(debug=False)
    print("built ok; instructions:", nc.n_instructions())
